# version 05
# baseline (speedup 1.0000x reference)
import os
import numpy as np
import concourse.bass as bass
import concourse.mybir as mybir
from concourse.bass_utils import run_bass_kernel_spmd

F32 = mybir.dt.float32
BF16 = mybir.dt.bfloat16
AF = mybir.ActivationFunctionType
ALU = mybir.AluOpType
AX = mybir.AxisListType.X

D = 1024
T = 2048
NKD = 8
NTC = 4
TCW = 512
DH = 4096
NKH = 32
LN_EPS = 1e-5
LN4_EPS = 1e-6
ATTN_SCALE = 0.125
NWIN = 8
KB = 1024

PHASES = int(os.environ.get("MK_PHASES", "4"))
SELFWAIT = bool(int(os.environ.get("MK_SELFWAIT", "0")))


ALL_CHANNELS = set()
CHAN_N = {}


class Stream:

    def __init__(self, name, step):
        self.name = name
        self.step = step
        self.n = {}
        self.ops = []
        self.wm = {}

    def _ch(self, ch):
        key = self.name if ch is None else f"{self.name}.{ch}"
        ALL_CHANNELS.add(key)
        return key

    def wait(self, tok):
        if tok is None:
            return
        nm, val = tok
        if val <= 0:
            return
        if "." in nm:
            val = max(val, CHAN_N.get(nm, val))
        if self.wm.get(nm, 0) >= val:
            return
        self.wm[nm] = val
        self.ops.append(lambda e, sems, nm=nm, val=val: e.wait_ge(sems[nm], val))

    def waits(self, toks):
        best = {}
        for t in toks:
            if t is None:
                continue
            nm, val = t
            if nm not in best or val > best[nm]:
                best[nm] = val
        for nm, val in best.items():
            self.wait((nm, val))

    def emit(self, fn, inc=True, ch=None):
        key = self._ch(ch)
        if SELFWAIT and self.name in ("ve", "se") and ch is None:
            self.wait(self.tok())
        if inc:
            self.n[key] = self.n.get(key, 0) + self.step
            CHAN_N[key] = self.n[key]
            n = self.n[key]
            self.ops.append(lambda e, sems, fn=fn, key=key: fn(e).then_inc(sems[key], self.step))
            return (key, n)
        self.ops.append(lambda e, sems, fn=fn: fn(e))
        return None

    def tok(self, ch=None):
        key = self._ch(ch)
        return (key, self.n.get(key, 0))

    def bar(self):
        return [(k, v) for k, v in self.n.items()]


class Plan:
    def __init__(self):
        CHAN_N.clear()
        self.te = Stream("te", 1)
        self.ve = Stream("ve", 1)
        self.se = Stream("se", 1)
        self.qg = Stream("qg", 16)
        self.qs = Stream("qs", 16)
        self.streams = dict(te=self.te, ve=self.ve, se=self.se, qg=self.qg, qs=self.qs)

    def barrier(self):
        out = []
        for s in self.streams.values():
            out.extend(s.bar())
        return out


def prep_lhsT(w_in_out):
    K, M = w_in_out.shape
    nk, no = K // 128, M // 128
    r = w_in_out.reshape(nk, 128, no, 128).transpose(2, 1, 0, 3)
    return np.ascontiguousarray(r.reshape(no, 128, nk * 128))


def vec_oT(v):
    return np.ascontiguousarray(v.reshape(-1, 128).T)


SM = {}
_sm_off = 0
def _sm(name, w):
    global _sm_off
    SM[name] = (_sm_off, w)
    _sm_off += w
for _nm, _w in [
    ("a_bi", 8), ("a_bf", 8), ("a_bo", 8), ("a_bs", 8), ("a_gamma", 8), ("a_beta", 8),
    ("f_s1", 32), ("f_s2", 32), ("f_h1s", 32), ("f_h2s", 32), ("f_s3", 8),
    ("b_bq", 8), ("b_bk", 8), ("b_bv", 8), ("b_bo", 8), ("b_bc", 8),
    ("b_bg", 3), ("b_bs", 1), ("b_wg", 24), ("b_ws", 8),
]:
    _sm(_nm, _w)
SMALL_W = _sm_off
_sm("a_hst", 8)
_sm("b_bo_half", 8)
_sm("pglob", 16)
_sm("scr", 48)
SMALL_W_ALL = _sm_off


def host_prep(inputs):
    g = lambda k: np.asarray(inputs[k], dtype=np.float32)
    P = {}
    for nm in ["Wi", "Wf", "Wo", "Ws"]:
        P["a_" + nm] = prep_lhsT(g("L0_attn_a_" + nm))
    P["f_w1"] = prep_lhsT(g("L0_ffn_w1_w").T)
    P["f_w2"] = prep_lhsT(g("L0_ffn_w2_w").T)
    P["f_h1"] = prep_lhsT(g("L0_ffn_h1_w").T)
    P["f_h2"] = prep_lhsT(g("L0_ffn_h2_w").T)
    P["f_w3"] = prep_lhsT(g("L0_ffn_w3_w").T)
    for nm in ["Wq", "Wk", "Wv", "Wo", "Wc"]:
        P["b_" + nm] = prep_lhsT(g("L0_attn_b_" + nm))
    sp = np.zeros((128, SMALL_W), np.float32)
    def put(name, arr):
        o, w = SM[name]
        sp[:, o:o + w] = arr
    put("a_bi", vec_oT(g("L0_attn_a_bi")));  put("a_bf", vec_oT(g("L0_attn_a_bf")))
    put("a_bo", vec_oT(g("L0_attn_a_bo")));  put("a_bs", vec_oT(g("L0_attn_a_bs")))
    put("a_gamma", vec_oT(g("L0_attn_a_gamma"))); put("a_beta", vec_oT(g("L0_attn_a_beta")))
    put("f_s1", vec_oT(g("L0_ffn_w1_s")));   put("f_s2", vec_oT(g("L0_ffn_w2_s")))
    put("f_h1s", vec_oT(g("L0_ffn_h1_s")));  put("f_h2s", vec_oT(g("L0_ffn_h2_s")))
    put("f_s3", vec_oT(g("L0_ffn_w3_s")))
    put("b_bq", vec_oT(g("L0_attn_b_bq")));  put("b_bk", vec_oT(g("L0_attn_b_bk")))
    put("b_bv", vec_oT(g("L0_attn_b_bv")));  put("b_bo", vec_oT(g("L0_attn_b_bo")))
    put("b_bc", vec_oT(g("L0_attn_b_bc")))
    put("b_bg", np.broadcast_to(g("L0_attn_b_bg"), (128, 3)))
    put("b_bs", np.broadcast_to(g("L0_attn_b_bs"), (128, 1)))
    put("b_wg", g("L0_attn_b_Wg").reshape(8, 128, 3).transpose(1, 0, 2).reshape(128, 24))
    put("b_ws", g("L0_attn_b_Ws").reshape(8, 128).T)
    P["small"] = np.ascontiguousarray(sp)
    return P


class Prog:
    def __init__(self, nc, dbg=0):
        self.nc = nc
        self.dbg = dbg
        self.p = Plan()
        self.rd56 = [None, None]
        self.rdbk = {}
        self.guard = {}
        self.alloc_dram()
        self.alloc_sbuf()

    def alloc_dram(self):
        nc = self.nc
        self.d_x = nc.dram_tensor("x_in", [D, T], F32, kind="ExternalInput").ap()
        self.d_small = nc.dram_tensor("small", [128, SMALL_W], F32, kind="ExternalInput").ap()
        for nm, sh in [
            ("a_Wi", [NKD, 128, D]), ("a_Wf", [NKD, 128, D]), ("a_Wo", [NKD, 128, D]), ("a_Ws", [NKD, 128, D]),
            ("f_w1", [NKH, 128, D]), ("f_w2", [NKH, 128, D]),
            ("f_h1", [NKH, 128, DH]), ("f_h2", [NKH, 128, DH]),
            ("f_w3", [NKD, 128, DH]),
            ("b_Wq", [NKD, 128, D]), ("b_Wk", [NKD, 128, D]), ("b_Wv", [NKD, 128, D]),
            ("b_Wo", [NKD, 128, 3 * D]), ("b_Wc", [NKD, 128, DH]),
        ]:
            setattr(self, "d_" + nm, nc.dram_tensor(nm, sh, F32, kind="ExternalInput").ap())
        self.d_out = nc.dram_tensor("x_out", [D, T], F32, kind="ExternalOutput").ap()
        self.dbg_on = bool(int(os.environ.get("MK_DBG", "0")))
        if self.dbg_on:
            self.d_dbg = nc.dram_tensor("dbg", [128, 16384], F32, kind="ExternalOutput").ap()
        self.d_hbuf_a = nc.dram_tensor("hbuf_a", [NKH, 128, T], BF16, kind="Internal").ap()
        self.d_hbuf_b = nc.dram_tensor("hbuf_b", [NKH, 128, T], BF16, kind="Internal").ap()
        self.d_xres_a = nc.dram_tensor("xres_a", [NKD, 128, T], F32, kind="Internal").ap()
        self.d_wo_bf = nc.dram_tensor("wo_bf", [NKD, 128, 3 * D], BF16, kind="Internal").ap()
        self.d_qkv_bf = nc.dram_tensor("qkv_bf", [3, NKD, 128, D], BF16, kind="Internal").ap()
        self.d_xres_b = nc.dram_tensor("xres_b", [NKD, 128, T], F32, kind="Internal").ap()

    def alloc_sbuf(self):
        nc = self.nc
        base = (nc.sbuf_base + 31) & ~31
        top = nc.sbuf_top

        def at(name, shape, dtype, off_b):
            nbytes = shape[1] * {F32: 4, BF16: 2}[dtype]
            assert base + off_b + nbytes <= top, f"{name} overflows sbuf: {off_b}+{nbytes} vs {top-base}"
            return nc.alloc_sbuf_tensor_at(name, shape, dtype, offset=base + off_b)

        assert SMALL_W_ALL * 4 <= 1536
        self.wbuf = at("wbuf", [128, 8192], BF16, 0)
        self.stg = at("stg", [128, 4096], BF16, 16 * KB)
        self.small = at("small", [128, SMALL_W_ALL], F32, 24 * KB)
        self.musig = at("musig", [128, 1024], F32, 25 * KB + 512)
        self.tmp = at("tmp", [128, 3584], F32, 29 * KB + 512)
        self.iota_f = at("iota_f", [128, 512], F32, 43 * KB + 512)
        self.eye_bf = at("eye_bf", [128, 128], BF16, 45 * KB + 512)
        self.ones_bf = at("ones_bf", [128, 128], BF16, 45 * KB + 768)
        self.ones_f1 = at("ones_f1", [1, 128], F32, 46 * KB)
        self.tri_f = at("tri_f", [128, 128], F32, 46 * KB + 512)
        self.eye_f = at("eye_f", [128, 128], F32, 47 * KB)
        self.ones_f32_col = at("ones_f32_col", [128, 1], F32, 25 * KB + 352)
        self.ones_row = at("ones_row", [1, 16], F32, 25 * KB + 384)
        assert SMALL_W_ALL * 4 <= 1376
        AR = 47 * KB + 512
        self.x = [at(f"x{i}", [128, T], F32, AR + i * 8 * KB) for i in range(NKD)]
        self.ln_c_bf = at("ln_c_bf", [128, 8 * TCW], BF16, AR + 64 * KB)
        self.ln_c_f = at("ln_c_f", [128, 8 * TCW], F32, AR + 72 * KB)
        self.z_c = at("z_c", [128, 8 * TCW], F32, AR + 88 * KB)
        self.hA = [at(f"hA{k}", [128, T], BF16, AR + k * 4 * KB) for k in range(NKH)]
        self.lnF = [at(f"lnF{k}", [128, T], BF16, AR + 128 * KB + k * 4 * KB) for k in range(NKD)]
        AB = AR
        self.gn = [at(f"gn{b}", [128, T], F32, AB + b * 8 * KB) for b in range(3)]
        self.z4 = at("z4", [128, 8 * TCW], F32, AB + 0 * KB)
        self.comp_g = at("comp_g", [128, 8 * TCW], BF16, AB + 24 * KB)
        self.srow = at("srow", [1, T], F32, AB + 32 * KB)
        self.sel_g = at("sel_g", [128, 8 * TCW], BF16, AB + 32 * KB)
        self.rgsum = at("rgsum", [128, T], F32, AB + 40 * KB)
        self.ln3T = [at(f"ln3T{k}", [128, D], BF16, AB + 40 * KB + k * 2 * KB) for k in range(16)]
        self.win_g = [at(f"win_g{k}", [128, T], BF16, AB + 40 * KB + k * 4 * KB) for k in range(NKD)]
        self.x2c = at("x2c", [128, 8 * TCW], F32, AB + 72 * KB)
        self.cmp = [at(f"cmp{k}", [128, TCW], BF16, AB + 72 * KB + k * KB) for k in range(NKD)]
        self.wq = []
        for pj in range(3):
            self.wq.append([at(f"wq{pj}_{k}", [128, 384], BF16, AB + 72 * KB + (pj * 8 + k) * 768)
                            for k in range(NKD)])
        _wqb_off = [AB + i * 768 for i in range(21)] + [AB + 120 * KB + i * 768 for i in range(3)]
        self.wqB = []
        for pj in range(3):
            self.wqB.append([at(f"wqB{pj}_{k}", [128, 384], BF16, _wqb_off[pj * 8 + k])
                             for k in range(NKD)])
        self.qkv = []
        for pj in range(3):
            self.qkv.append([at(f"qkv{pj}_{k}", [128, TCW], BF16, AB + 80 * KB + pj * 8 * KB + k * KB)
                             for k in range(NKD)])
        self.rkA = at("rkA", [128, T], F32, AB + 48 * KB)
        self.x2cB = at("x2cB", [128, 8 * TCW], F32, AB + 88 * KB)
        self.stgB = at("stgB", [128, 1024], BF16, AB + 104 * KB)
        self.oh = [at(f"oh{k}", [128, TCW], BF16, AB + 80 * KB + k * KB) for k in range(16)]
        self.selT = [at(f"selT{j}", [128, D], BF16, AB + 104 * KB + j * 2 * KB) for j in range(4)]
        self.vT = [at(f"vT{j}", [128, D], BF16, AB + 104 * KB + j * 2 * KB) for j in range(4)]
        self.pP = [at(f"pP{j}", [128, TCW], BF16, AB + 112 * KB + j * KB) for j in range(4)]
        self.pTt = [at(f"pT{j}", [128, TCW], BF16, AB + 116 * KB + j * KB) for j in range(4)]
        self.scb = at("scb", [128, T], F32, AB + 120 * KB)
        self.wocast = at("wocast", [128, 3 * D], BF16, AB + 120 * KB)
        self.psf = nc.alloc_psum_tensor("psf", [128, 3584], F32)
        self.psb = nc.alloc_psum_tensor("psb", [128, 1024], BF16)

    def bankap(self, i, w=TCW, off=0):
        return self.psf[:, i * TCW + off:i * TCW + off + w]

    def dump(self, ap, deps, col, ncols):
        if not self.dbg_on:
            return None
        self.p.qs.waits(deps)
        return self.p.qs.emit(lambda e, ap=ap, col=col, ncols=ncols: e.dma_start(
            out=self.d_dbg[:, col:col + ncols], in_=ap), ch="dbg")

    def dump_bf(self, ap, deps, col, ncols):
        if not self.dbg_on:
            return None
        self.p.qg.waits(deps)
        return self.p.qg.emit(lambda e, ap=ap, col=col, ncols=ncols: e.dma_start(
            out=self.d_dbg[:, col:col + ncols], in_=ap), ch="dbg")

    def sm(self, name, i=0, w=1):
        o, wd = SM[name]
        assert i + w <= wd
        return self.small[:, o + i:o + i + w]

    def setup(self):
        p = self.p
        for k in range(NKD):
            p.qs.emit(lambda e, k=k: e.dma_start(
                out=self.x[k][:], in_=self.d_x[k * 128:(k + 1) * 128, :]), ch="ld")
        p.qs.emit(lambda e: e.dma_start(out=self.small[:, 0:SMALL_W], in_=self.d_small[:, :]), ch="ld")
        ld_all = p.qs.tok("ld")
        self.tok_x = [ld_all] * NKD
        self.tok_small = ld_all
        t_io = p.qg.emit(lambda e: e.iota(self.iota_f[:], pattern=[[1, 512]], base=0,
                                          channel_multiplier=0, allow_small_or_imprecise_dtypes=True), ch="misc")
        self.tok_pglob = p.qg.emit(lambda e: e.iota(self.sm("pglob", 0, 16), pattern=[[128, 16]], base=0,
                                                    channel_multiplier=1, allow_small_or_imprecise_dtypes=True), ch="misc")
        t_pi = p.qg.emit(lambda e: e.iota(self.sm("scr", 0, 1), pattern=[[0, 1]], base=0,
                                          channel_multiplier=1, allow_small_or_imprecise_dtypes=True), ch="misc")
        self.tok_iota = t_io
        p.ve.waits([t_io, t_pi])
        t_eye_f = p.ve.emit(lambda e: e.tensor_scalar(self.tmp[:, 0:128], self.iota_f[:, 0:128],
                                                      self.sm("scr", 0, 1), None, op0=ALU.is_equal))
        p.ve.wait(t_eye_f)
        self.tok_eye = p.ve.emit(lambda e: e.tensor_copy(self.eye_bf[:], self.tmp[:, 0:128]))
        self.tok_eye_f = p.ve.emit(lambda e: e.tensor_copy(self.eye_f[:], self.tmp[:, 0:128]))
        self.tok_tri = p.ve.emit(lambda e: e.tensor_scalar(self.tri_f[:], self.iota_f[:, 0:128],
                                                           self.sm("scr", 0, 1), None, op0=ALU.is_gt))
        self.tok_ones = p.ve.emit(lambda e: e.memset(self.ones_bf[:], 1.0))
        self.tok_ones1 = p.ve.emit(lambda e: e.memset(self.ones_f1[:], 1.0))
        self.tok_ones_col = p.ve.emit(lambda e: e.memset(self.ones_f32_col[:], 1.0))
        self.tok_ones_row = p.ve.emit(lambda e: e.memset(self.ones_row[:], 1.0))
        p.ve.wait(self.tok_small)
        self.tok_bo_half = p.ve.emit(lambda e: e.tensor_scalar(self.sm("b_bo_half", 0, 8),
                                                               self.sm("b_bo", 0, 8), 0.5, None, op0=ALU.mult))
        self.tok_eps5 = p.ve.emit(lambda e: e.memset(self.sm("scr", 1, 1), LN_EPS))
        self.tok_eps6 = p.ve.emit(lambda e: e.memset(self.sm("scr", 2, 1), LN4_EPS))

    def ln_stats_chunk(self, tiles, eps, bA=5, bB=6):
        p = self.p
        tok_mmA = tok_mmB = None
        prev_B = None
        for e_i, (ap, tk) in enumerate(tiles):
            sl = self.stgB[:, (e_i % 2) * TCW:(e_i % 2 + 1) * TCW]
            sq = self.tmp[:, 6 * TCW:7 * TCW]
            p.se.waits([tk, prev_B, getattr(self, "stgB_guard", None)])
            t_c = p.se.emit(lambda e, ap=ap, sl=sl: e.activation(sl, ap, AF.Copy))
            p.te.waits([t_c, self.tok_ones])
            if e_i == 0:
                p.te.waits(self.rdbk.get(bA, []) + self.rdbk.get(bB, []))
            tok_mmA = p.te.emit(lambda e, sl=sl, e_i=e_i, bA=bA: e.matmul(
                self.bankap(bA), self.ones_bf[:], sl, start=(e_i == 0), stop=(e_i == NKD - 1)))
            p.se.waits([tk, getattr(self, "tmp6_guard", None)])
            t_q = p.se.emit(lambda e, ap=ap, sq=sq: e.activation(sq, ap, AF.Square))
            p.se.wait(tok_mmA)
            t_c2 = p.se.emit(lambda e, sl=sl, sq=sq: e.activation(sl, sq, AF.Copy))
            p.te.wait(t_c2)
            tok_mmB = p.te.emit(lambda e, sl=sl, e_i=e_i, bB=bB: e.matmul(
                self.bankap(bB), self.ones_bf[:], sl, start=(e_i == 0), stop=(e_i == NKD - 1)))
            prev_B = tok_mmB
        self.stgB_guard = tok_mmB
        return self.ln_finalize(tok_mmA, tok_mmB, eps, bA, bB)

    def ln_finalize(self, tok_mmA, tok_mmB, eps, bA=5, bB=6):
        p = self.p
        mu = self.musig[:, 0:TCW]
        r = self.musig[:, TCW:2 * TCW]
        t6 = self.tmp[:, 6 * TCW:7 * TCW]
        p.ve.wait(tok_mmA)
        tok_mu = p.ve.emit(lambda e, bA=bA: e.tensor_scalar(mu, self.bankap(bA), 1.0 / D, None, op0=ALU.mult))
        p.ve.wait(tok_mmB)
        t_v1 = p.ve.emit(lambda e, bB=bB: e.tensor_scalar(t6, self.bankap(bB), 1.0 / D, None, op0=ALU.mult))
        p.ve.wait(tok_mu)
        t_v2 = p.ve.emit(lambda e: e.tensor_tensor(r, mu, mu, op=ALU.mult))
        p.ve.waits([t_v1, t_v2])
        t_v3 = p.ve.emit(lambda e: e.tensor_tensor(t6, t6, r, op=ALU.subtract))
        eps_ap = self.sm("scr", 1 if eps == LN_EPS else 2, 1)
        p.se.waits([t_v3, self.tok_eps5, self.tok_eps6])
        t_l = p.se.emit(lambda e: e.activation(r, t6, AF.Ln, bias=eps_ap))
        p.se.wait(t_l)
        tok_r = p.se.emit(lambda e: e.activation(r, r, AF.Exp, scale=-0.5))
        self.rdbk[bA] = [tok_mu]
        self.rdbk[bB] = [t_v1]
        self.rd56 = [tok_mu, t_v1]
        return tok_mu, tok_r

    def attn_a(self):
        p = self.p
        W = [self.d_a_Wi, self.d_a_Wf, self.d_a_Wo, self.d_a_Ws]
        BIAS = ["a_bi", "a_bf", "a_bo", "a_bs"]
        FUNC = [AF.Sigmoid, AF.Sigmoid, AF.Sigmoid, AF.Tanh]
        wtok = {}
        wbuf_user = {}
        h_upd = [None] * NKD
        self.tok_x1 = [[None] * NTC for _ in range(NKD)]
        self.tok_x1_hbm = [[None] * NTC for _ in range(NKD)]
        last_proj_mm = None
        last_zread = None
        guard = self.guard

        def issue_w(gi):
            tc, e_i, buf = gi // NKD, gi % NKD, gi % 2
            toks = []
            for pj in range(4):
                p.qg.wait(wbuf_user.get((buf, pj)))
                toks.append(p.qg.emit(lambda e, pj=pj, e_i=e_i, buf=buf: e.dma_start(
                    out=self.wbuf[:, buf * 4096 + pj * D:buf * 4096 + (pj + 1) * D],
                    in_=W[pj][e_i, :, :]), ch="wA" if buf == 0 else "wB"))
            wtok[gi] = toks

        issue_w(0)
        issue_w(1)

        for tc in range(NTC):
            c0 = tc * TCW
            tiles = [(self.x[k][:, c0:c0 + TCW], self.tok_x[k]) for k in range(NKD)]
            tok_mu, tok_r = self.ln_stats_chunk(tiles, LN_EPS)
            if tc == 0:
                self.dump(self.x[0][:, 0:TCW], [self.tok_x[0]], 13312, TCW)
                self.dump(self.musig[:], [tok_mu, tok_r], 14336, 1024)
            ln_toks = []
            for k in range(NKD):
                lf = self.ln_c_f[:, k * TCW:(k + 1) * TCW]
                lb = self.ln_c_bf[:, k * TCW:(k + 1) * TCW]
                p.ve.waits([tok_mu, self.tok_x[k]])
                x_sl = self.x[k][:, c0:c0 + TCW]
                t1 = p.ve.emit(lambda e, lf=lf, x_sl=x_sl: e.tensor_tensor(
                    lf, x_sl, self.musig[:, 0:TCW], op=ALU.subtract))
                p.ve.waits([t1, tok_r])
                t2 = p.ve.emit(lambda e, lf=lf: e.tensor_tensor(lf, lf, self.musig[:, TCW:2 * TCW], op=ALU.mult))
                p.se.waits([t2, last_proj_mm])
                t3 = p.se.emit(lambda e, lb=lb, lf=lf: e.activation(lb, lf, AF.Copy))
                ln_toks.append((t2, t3))
                if tc == 0:
                    self.dump(lf, [t2], k * TCW, TCW)

            zb_toks = []
            for e_i in range(NKD):
                gi = tc * NKD + e_i
                buf = gi % 2
                mm_tok = [None] * 4
                for pj in range(4):
                    bank = (gi * 4 + pj) % 5
                    p.te.waits(wtok[gi])
                    p.te.wait(guard.get(("bank", bank)))
                    for k in range(NKD):
                        p.te.wait(ln_toks[k][1])
                        tk = p.te.emit(lambda e, pj=pj, k=k, bank=bank, buf=buf: e.matmul(
                            self.bankap(bank),
                            self.wbuf[:, buf * 4096 + pj * D + k * 128:buf * 4096 + pj * D + (k + 1) * 128],
                            self.ln_c_bf[:, k * TCW:(k + 1) * TCW],
                            start=(k == 0), stop=(k == NKD - 1)), inc=(k == NKD - 1))
                        if tk is not None:
                            mm_tok[pj] = tk
                    wbuf_user[(buf, pj)] = mm_tok[pj]
                last_proj_mm = mm_tok[3]
                if gi + 2 < NTC * NKD:
                    issue_w(gi + 2)
                slots = {}
                ev = {}
                for pj, slot in [(0, 0), (3, 1), (1, 2), (2, 3)]:
                    bank = (gi * 4 + pj) % 5
                    ap = self.tmp[:, slot * TCW:(slot + 1) * TCW]
                    p.se.waits([mm_tok[pj], guard.get(("tmp", slot))])
                    ev[pj] = p.se.emit(lambda e, ap=ap, pj=pj, bank=bank, e_i=e_i: e.activation(
                        ap, self.bankap(bank), FUNC[pj], bias=self.sm(BIAS[pj], e_i, 1)))
                    guard[("bank", bank)] = ev[pj]
                    slots[pj] = ap
                a_ap = self.tmp[:, 4 * TCW:5 * TCW]
                p.ve.waits([ev[0], ev[3], guard.get(("tmp", 4))])
                t_a = p.ve.emit(lambda e, a_ap=a_ap, s0=slots[0], s3=slots[3]: e.tensor_tensor(
                    a_ap, s0, s3, op=ALU.mult))
                guard[("tmp", 0)] = t_a
                guard[("tmp", 1)] = t_a
                h_ap = self.tmp[:, 5 * TCW:6 * TCW]
                p.ve.waits([ev[1], t_a, h_upd[e_i], guard.get(("tmp", 5))])
                if tc == 0:
                    t_h = p.ve.emit(lambda e, h_ap=h_ap, s1=slots[1], a_ap=a_ap: e.tensor_tensor_scan(
                        h_ap, s1, a_ap, 0.0, op0=ALU.mult, op1=ALU.add))
                else:
                    t_h = p.ve.emit(lambda e, h_ap=h_ap, s1=slots[1], a_ap=a_ap, e_i=e_i: e.tensor_tensor_scan(
                        h_ap, s1, a_ap, self.sm("a_hst", e_i, 1), op0=ALU.mult, op1=ALU.add))
                guard[("tmp", 2)] = t_h
                guard[("tmp", 4)] = t_h
                p.ve.wait(t_h)
                h_upd[e_i] = p.ve.emit(lambda e, h_ap=h_ap, e_i=e_i: e.tensor_copy(
                    self.sm("a_hst", e_i, 1), h_ap[:, TCW - 1:TCW]))
                p.ve.waits([ev[2], h_upd[e_i]])
                t_y = p.ve.emit(lambda e, h_ap=h_ap, s2=slots[2]: e.tensor_tensor(h_ap, s2, h_ap, op=ALU.mult))
                guard[("tmp", 3)] = t_y
                z_ap = self.z_c[:, e_i * TCW:(e_i + 1) * TCW]
                p.ve.waits([t_y, ln_toks[e_i][0], last_zread])
                t_z = p.ve.emit(lambda e, z_ap=z_ap, h_ap=h_ap, e_i=e_i: e.tensor_tensor(
                    z_ap, self.ln_c_f[:, e_i * TCW:(e_i + 1) * TCW], h_ap, op=ALU.add))
                guard[("tmp", 5)] = t_z
                if tc == 0 and e_i == 0:
                    d1 = self.dump(slots[0], [ev[0]], 4096, TCW)
                    d2 = self.dump(slots[3], [ev[3]], 4608, TCW)
                    d3 = self.dump(slots[1], [ev[1]], 5120, TCW)
                    d4 = self.dump(slots[2], [ev[2]], 5632, TCW)
                    d5 = self.dump(a_ap, [t_a], 6144, TCW)
                    d6 = self.dump(h_ap, [t_y], 6656, TCW)
                    for dd, sl in [(d1,0),(d2,1),(d3,2),(d4,3),(d5,4),(d6,5)]:
                        if dd is not None:
                            guard[("tmp", sl)] = dd
                zb = self.stg[:, e_i * TCW:(e_i + 1) * TCW]
                p.se.waits([t_z, guard.get(("zbslot", e_i))])
                t_zb = p.se.emit(lambda e, zb=zb, z_ap=z_ap: e.activation(zb, z_ap, AF.Copy))
                zb_toks.append((zb, z_ap, t_zb))
                guard[("stg67w", e_i)] = t_zb

            tA = tB = None
            for e_i, (zb, z_ap, t_zb) in enumerate(zb_toks):
                p.te.waits([t_zb, self.tok_ones])
                if e_i == 0:
                    p.te.waits(self.rd56)
                tA = p.te.emit(lambda e, zb=zb, e_i=e_i: e.matmul(
                    self.bankap(5), self.ones_bf[:], zb, start=(e_i == 0), stop=(e_i == NKD - 1)))
                sq = self.tmp[:, 6 * TCW:7 * TCW]
                p.se.waits([guard.get(("sq",)), t_zb])
                t_q = p.se.emit(lambda e, sq=sq, z_ap=z_ap: e.activation(sq, z_ap, AF.Square))
                p.se.wait(tA)
                t_c2 = p.se.emit(lambda e, zb=zb, sq=sq: e.activation(zb, sq, AF.Copy))
                guard[("sq",)] = t_c2
                p.te.wait(t_c2)
                tB = p.te.emit(lambda e, zb=zb, e_i=e_i: e.matmul(
                    self.bankap(6), self.ones_bf[:], zb, start=(e_i == 0), stop=(e_i == NKD - 1)))
                guard[("zbslot", e_i)] = tB
            self.stg_guard = tB
            if tc == 0:
                self.dump(self.z_c[:], [t for (_, _, t) in zb_toks], 8192, 4096)
            tok_mu2, tok_r2 = self.ln_finalize(tA, tB, LN_EPS)
            if tc == 0:
                self.dump(self.musig[:], [tok_mu2, tok_r2], 12288, 1024)

            o_ap = self.tmp[:, 6 * TCW:7 * TCW]
            for e_i in range(NKD):
                z_ap = self.z_c[:, e_i * TCW:(e_i + 1) * TCW]
                p.ve.waits([tok_mu2, tok_r2])
                tz1 = p.ve.emit(lambda e, z_ap=z_ap, o_ap=o_ap: e.tensor_tensor(
                    o_ap, z_ap, self.musig[:, 0:TCW], op=ALU.subtract))
                p.ve.wait(tz1)
                tz2 = p.ve.emit(lambda e, o_ap=o_ap: e.tensor_tensor(
                    o_ap, o_ap, self.musig[:, TCW:2 * TCW], op=ALU.mult))
                p.ve.wait(tz2)
                tz3 = p.ve.emit(lambda e, o_ap=o_ap, e_i=e_i: e.tensor_scalar(
                    o_ap, o_ap, self.sm("a_gamma", e_i, 1), self.sm("a_beta", e_i, 1),
                    op0=ALU.mult, op1=ALU.add))
                p.ve.wait(tz3)
                x_sl = self.x[e_i][:, c0:c0 + TCW]
                tx = p.ve.emit(lambda e, o_ap=o_ap, x_sl=x_sl: e.tensor_tensor(
                    x_sl, x_sl, o_ap, op=ALU.add))
                self.tok_x1[e_i][tc] = tx
                p.qs.wait(tx)
                xr_sl = self.d_xres_a[e_i, :, c0:c0 + TCW]
                self.tok_x1_hbm[e_i][tc] = p.qs.emit(lambda e, xr_sl=xr_sl, x_sl=x_sl: e.dma_start(
                    out=xr_sl, in_=x_sl), ch="x")
            guard[("sq",)] = tx
            self.tmp6_guard = tx
            last_zread = tok_r2
            last_zread = guard.get(("sq",))


    def ffn(self, x_toks, d_xres_in, d_xres_out, is_final=False, x_from_hbm=False):
        p = self.p
        bar = p.barrier()
        g = self.guard

        for tc in range(NTC):
            c0 = tc * TCW
            if x_from_hbm:
                for k in range(NKD):
                    xs = d_xres_in[k, :, c0:c0 + TCW]
                    dst = self.x2c[:, k * TCW:(k + 1) * TCW]
                    p.qg.waits(bar + [g.get(("fx2c", tc))])
                    p.qg.emit(lambda e, xs=xs, dst=dst: e.dma_start(out=dst, in_=xs), ch="x2")
                ld_all = p.qg.tok("x2")
                xtile = lambda k: self.x2c[:, k * TCW:(k + 1) * TCW]
                xtok = lambda k: ld_all
            else:
                xtile = lambda k: self.x[k][:, c0:c0 + TCW]
                xtok = lambda k: x_toks[k]
            tiles = [(xtile(k), xtok(k)) for k in range(NKD)]
            tok_mu, tok_r = self.ln_stats_chunk(tiles, LN_EPS)
            t2 = t3 = None
            for k in range(NKD):
                sl = self.tmp[:, (k % 2) * TCW:(k % 2 + 1) * TCW]
                lb = self.lnF[k][:, c0:c0 + TCW]
                x_sl = xtile(k)
                p.ve.waits([tok_mu, xtok(k), g.get(("lntmp", k % 2))] + bar)
                t1 = p.ve.emit(lambda e, sl=sl, x_sl=x_sl: e.tensor_tensor(
                    sl, x_sl, self.musig[:, 0:TCW], op=ALU.subtract))
                p.ve.waits([t1, tok_r])
                t2 = p.ve.emit(lambda e, sl=sl: e.tensor_tensor(sl, sl, self.musig[:, TCW:2 * TCW], op=ALU.mult))
                p.se.waits([t2] + bar)
                t3 = p.se.emit(lambda e, lb=lb, sl=sl: e.activation(lb, sl, AF.Copy))
                g[("lntmp", k % 2)] = t3
                g[("lnF", k, tc)] = t3
            g[("fx2c", tc + 1)] = t2
        ln_bar = [p.ve.tok(), p.se.tok()]
        self.dump_bf(self.lnF[0][:], [g[("lnF", 0, tc)] for tc in range(NTC)], 0, T)

        g[("bank", 5)] = self.rd56[0]
        rot = [0]

        def stage_mm_tc(o, nk, wslice, rhs_fn, rhs_toks, wtoks, tc):
            bank = rot[0] % 6
            rot[0] += 1
            p.te.waits(wtoks + [g.get(("bank", bank))])
            last = None
            for k in range(nk):
                p.te.wait(rhs_toks(k, tc))
                last = p.te.emit(lambda e, k=k, tc=tc, bank=bank, wslice=wslice, rhs_fn=rhs_fn: e.matmul(
                    self.bankap(bank), wslice(k), rhs_fn(k, tc),
                    start=(k == 0), stop=(k == nk - 1)), inc=(k == nk - 1))
            return [(tc, bank, last)]

        def stage_mm(o, nk, wslice, rhs_fn, rhs_toks, wtoks, extra_mm_wait=None, korder=None):
            out = []
            ks = korder if korder is not None else list(range(nk))
            for tc in range(NTC):
                bank = rot[0] % 6
                rot[0] += 1
                p.te.waits(wtoks + [g.get(("bank", bank))])
                if extra_mm_wait:
                    p.te.waits(extra_mm_wait)
                last = None
                for i_k, k in enumerate(ks):
                    p.te.wait(rhs_toks(k, tc))
                    st_f, sp_f = (i_k == 0), (i_k == len(ks) - 1)
                    last = p.te.emit(lambda e, k=k, tc=tc, bank=bank, wslice=wslice, rhs_fn=rhs_fn,
                                     st_f=st_f, sp_f=sp_f: e.matmul(
                        self.bankap(bank), wslice(k), rhs_fn(k, tc),
                        start=st_f, stop=sp_f), inc=sp_f)
                out.append((tc, bank, last))
            return out

        wuser = {}
        s1_wtok = {}

        def s1_issue(o):
            buf = o % 2
            p.qg.waits([wuser.get((buf, 0)), wuser.get((buf, 1))] + bar)
            chn = "wA" if buf == 0 else "wB"
            tw1 = p.qg.emit(lambda e, o=o, buf=buf: e.dma_start(
                out=self.wbuf[:, buf * 4096:buf * 4096 + D], in_=self.d_f_w1[o, :, :]), ch=chn)
            tw2 = p.qg.emit(lambda e, o=o, buf=buf: e.dma_start(
                out=self.wbuf[:, buf * 4096 + D:buf * 4096 + 2 * D], in_=self.d_f_w2[o, :, :]), ch=chn)
            s1_wtok[o] = [tw1, tw2]

        s1_issue(0)
        s1_issue(1)
        hA_tok = [[None] * NTC for _ in range(NKH)]
        for o in range(NKH):
            buf = o % 2
            w1s = lambda k, buf=buf: self.wbuf[:, buf * 4096 + k * 128:buf * 4096 + (k + 1) * 128]
            w2s = lambda k, buf=buf: self.wbuf[:, buf * 4096 + D + k * 128:buf * 4096 + D + (k + 1) * 128]
            rhs = lambda k, tc: self.lnF[k][:, tc * TCW:(tc + 1) * TCW]
            rtok = lambda k, tc: g[("lnF", k, tc)]
            last1 = last2 = None
            for tc in range(NTC):
                ((_, b1, t1k),) = stage_mm_tc(o, NKD, w1s, rhs, rtok, s1_wtok[o], tc)
                ((_, b2, t2k),) = stage_mm_tc(o, NKD, w2s, rhs, rtok, s1_wtok[o], tc)
                last1, last2 = t1k, t2k
                t1s = self.tmp[:, (2 + tc % 2) * TCW:(3 + tc % 2) * TCW]
                p.se.waits([t1k, g.get(("s1tmp", tc % 2))] + bar)
                ts = p.se.emit(lambda e, t1s=t1s, b1=b1, o=o: e.activation(
                    t1s, self.bankap(b1), AF.Silu, scale=self.sm("f_s1", o, 1)))
                g[("bank", b1)] = ts
                dst = self.hA[o][:, tc * TCW:(tc + 1) * TCW]
                p.ve.waits([t2k, ts] + bar + ln_bar)
                th = p.ve.emit(lambda e, dst=dst, b2=b2, o=o, t1s=t1s: e.scalar_tensor_tensor(
                    dst, self.bankap(b2), self.sm("f_s2", o, 1), t1s, op0=ALU.mult, op1=ALU.mult))
                g[("bank", b2)] = th
                g[("s1tmp", tc % 2)] = th
                hA_tok[o][tc] = th
            wuser[(buf, 0)] = last1
            wuser[(buf, 1)] = last2
            if o + 2 < NKH:
                s1_issue(o + 2)

        self.dump_bf(self.hA[0][:], hA_tok[0], 2048, T)
        def big_stage(d_w, s_name, d_spill, in_toks, final_stage=False, d_xin=None, d_xout=None,
                      in_res=None, resident_out=False, korder=None):
            wtok = {}

            def issue(o):
                buf = o % 2
                p.qg.waits([wuser.get((buf, "big")), wuser.get((buf, 0)), wuser.get((buf, 1))] + bar)
                wtok[o] = [p.qg.emit(lambda e, o=o, buf=buf: e.dma_start(
                    out=self.wbuf[:, buf * 4096:(buf + 1) * 4096], in_=d_w[o, :, :]),
                    ch="wA" if buf == 0 else "wB")]

            issue(0)
            issue(1)
            no = NKD if final_stage else NKH
            spill_tok = [[None] * NTC for _ in range(no)]
            out_toks = [[None] * NTC for _ in range(no)]
            res_out = {}
            scale_nm = s_name
            for o in range(no):
                buf = o % 2
                ws = lambda k, buf=buf: self.wbuf[:, buf * 4096 + k * 128:buf * 4096 + (k + 1) * 128]
                if in_res:
                    rhs = lambda k, tc: (in_res[k][0](tc) if k in in_res
                                         else self.hA[k][:, tc * TCW:(tc + 1) * TCW])
                    rtok = lambda k, tc: (in_res[k][1] if k in in_res else in_toks[k][tc])
                else:
                    rhs = lambda k, tc: self.hA[k][:, tc * TCW:(tc + 1) * TCW]
                    rtok = lambda k, tc: in_toks[k][tc]
                mm = stage_mm(o, NKH, ws, rhs, rtok, wtok[o], korder=korder)
                wuser[(buf, "big")] = mm[-1][2]
                if o + 2 < no:
                    issue(o + 2)
                for tc, bank, mtk in mm:
                    if not final_stage:
                        if resident_out and o >= 24:
                            rdst = self.lnF[o - 24][:, tc * TCW:(tc + 1) * TCW]
                            p.se.waits([mtk, g.get(("resout", o, tc))] + bar)
                            tev = p.se.emit(lambda e, rdst=rdst, bank=bank, o=o: e.activation(
                                rdst, self.bankap(bank), AF.Gelu, scale=self.sm(scale_nm, o, 1)))
                            g[("bank", bank)] = tev
                            res_out[(o, tc)] = tev
                            spill_tok[o][tc] = tev
                            out_toks[o][tc] = tev
                            continue
                        slot = (o * NTC + tc) % 4
                        sdst = self.stg[:, slot * TCW:(slot + 1) * TCW]
                        p.se.waits([mtk, g.get(("stgslot", slot))] + bar)
                        tev = p.se.emit(lambda e, sdst=sdst, bank=bank, o=o: e.activation(
                            sdst, self.bankap(bank), AF.Gelu, scale=self.sm(scale_nm, o, 1)))
                        g[("bank", bank)] = tev
                        dsp = d_spill[o, :, tc * TCW:(tc + 1) * TCW]
                        p.qs.wait(tev)
                        tsp = p.qs.emit(lambda e, dsp=dsp, sdst=sdst: e.dma_start(out=dsp, in_=sdst), ch="sp")
                        g[("stgslot", slot)] = tsp
                        spill_tok[o][tc] = tsp
                    else:
                        slot = (o * NTC + tc) % 2
                        xo = self.tmp[:, slot * TCW:(slot + 1) * TCW]
                        xsrc = d_xin[o, :, tc * TCW:(tc + 1) * TCW]
                        p.qg.waits([g.get(("xold", slot))] + bar)
                        txo = p.qg.emit(lambda e, xo=xo, xsrc=xsrc: e.dma_start(out=xo, in_=xsrc),
                                        ch="xo0" if slot == 0 else "xo1")
                        dst = self.tmp[:, (2 + slot) * TCW:(3 + slot) * TCW]
                        p.ve.waits([mtk, txo, g.get(("xnew", slot))] + bar)
                        tx = p.ve.emit(lambda e, dst=dst, bank=bank, o=o, xo=xo: e.scalar_tensor_tensor(
                            dst, self.bankap(bank), self.sm("f_s3", o, 1), xo, op0=ALU.mult, op1=ALU.add))
                        g[("bank", bank)] = tx
                        g[("xold", slot)] = tx
                        p.qs.wait(tx)
                        if is_final:
                            dd = self.d_out[o * 128:(o + 1) * 128, tc * TCW:(tc + 1) * TCW]
                        else:
                            dd = d_xout[o, :, tc * TCW:(tc + 1) * TCW]
                        tw = p.qs.emit(lambda e, dd=dd, dst=dst: e.dma_start(out=dd, in_=dst), ch="out")
                        g[("xnew", slot)] = tw
                        out_toks[o][tc] = (tx, tw)
            return spill_tok, out_toks, mm[-1][2]

        def chased_reload(d_src, last_mm, sp_toks, n_tiles, phase_tag):
            toks = [None] * n_tiles
            grp = (n_tiles + 3) // 4
            for gi4 in range(4):
                ks = range(gi4 * grp, min((gi4 + 1) * grp, n_tiles))
                q = p.qs if gi4 % 2 == 0 else p.qg
                chn = f"rl{phase_tag}{gi4}"
                for k in ks:
                    q.waits([last_mm, p.qs.tok("sp")])
                    q.emit(lambda e, k=k: e.dma_start(out=self.hA[k][:], in_=d_src[k, :, :]), ch=chn)
                t = q.tok(chn)
                for k in ks:
                    toks[k] = t
            return toks

        KORD = list(range(24, NKH)) + list(range(24))
        sp_a, s2out, s2_last = big_stage(self.d_f_h1, "f_h1s", self.d_hbuf_a, hA_tok,
                                         resident_out=True)
        rel = chased_reload(self.d_hbuf_a, s2_last, None, 24, "a" + str(is_final))
        rel_tok = [[rel[k]] * NTC if k < 24 else None for k in range(24)]
        in_res2 = {k: ((lambda tc, kk=k: self.lnF[kk - 24][:, tc * TCW:(tc + 1) * TCW]), s2out[k][NTC - 1])
                   for k in range(24, NKH)}
        rel_full = [[rel[k]] * NTC for k in range(24)] + [[None] * NTC for _ in range(8)]
        sp_b, _, s3_last = big_stage(self.d_f_h2, "f_h2s", self.d_hbuf_b, rel_full,
                                     in_res=in_res2, korder=KORD)
        rel2 = chased_reload(self.d_hbuf_b, s3_last, None, NKH, "b" + str(is_final))
        rel2_full = [[rel2[k]] * NTC for k in range(NKH)]
        _, out_toks, _ = big_stage(self.d_f_w3, "f_s3", None, rel2_full, final_stage=True,
                                   d_xin=d_xres_in, d_xout=d_xres_out)
        new_hbm_toks = [[out_toks[k][tc][1] for tc in range(NTC)] for k in range(NKD)]
        return None, new_hbm_toks


    def attn_b(self, x2_hbm_tok, is_final=False):
        p = self.p
        g = self.guard
        bar = p.barrier() + [x2_hbm_tok]
        SC = ATTN_SCALE

        def scr(i, w=1):
            return self.sm("scr", i, w)

        sc_ev = None
        for tc in range(NTC):
            c0 = tc * TCW
            ld_toks = []
            for k in range(NKD):
                xs = self.d_xres_b[k, :, c0:c0 + TCW]
                dst = self.x2c[:, k * TCW:(k + 1) * TCW]
                p.qg.waits(bar + [g.get(("x2c", tc))])
                ld_toks.append(p.qg.emit(lambda e, xs=xs, dst=dst: e.dma_start(out=dst, in_=xs), ch="x2"))
            ld_all = p.qg.tok("x2")
            tiles = [(self.x2c[:, k * TCW:(k + 1) * TCW], ld_all) for k in range(NKD)]
            tok_mu, tok_r = self.ln_stats_chunk(tiles, LN_EPS)
            if tc == 0:
                p.te.waits([self.tok_small, self.tok_ones_col, g.get(("bank", 4))])
                for k in range(NKD):
                    t_sw = p.te.emit(lambda e, k=k: e.matmul(
                        self.psf[0:1, 4 * TCW:4 * TCW + 1], self.sm("b_ws", k, 1),
                        self.ones_f32_col[:, 0:1], start=(k == 0), stop=(k == NKD - 1)),
                        inc=(k == NKD - 1))
                p.se.wait(t_sw)
                self.tok_sws = p.se.emit(lambda e: e.activation(
                    self.small[0:1, SM["scr"][0] + 16:SM["scr"][0] + 17],
                    self.psf[0:1, 4 * TCW:4 * TCW + 1], AF.Copy))
                g[("bank", 4)] = self.tok_sws
            for k in range(NKD):
                sl = self.tmp[:, (k % 2) * TCW:(k % 2 + 1) * TCW]
                lb = self.lnF[k][:, c0:c0 + TCW]
                x_sl = self.x2c[:, k * TCW:(k + 1) * TCW]
                p.ve.waits([tok_mu, ld_all, g.get(("lntmp", k % 2))] + bar)
                t1 = p.ve.emit(lambda e, sl=sl, x_sl=x_sl: e.tensor_tensor(
                    sl, x_sl, self.musig[:, 0:TCW], op=ALU.subtract))
                p.ve.waits([t1, tok_r])
                t2 = p.ve.emit(lambda e, sl=sl: e.tensor_tensor(sl, sl, self.musig[:, TCW:2 * TCW], op=ALU.mult))
                p.se.waits([t2] + bar)
                t3 = p.se.emit(lambda e, lb=lb, sl=sl: e.activation(lb, sl, AF.Copy))
                g[("lnF3", k, tc)] = t3
                g[("lntmp", k % 2)] = t3
                p.te.waits([ld_all, self.tok_small, g.get(("bank", 4)) if k == 0 else None])
                tmm = p.te.emit(lambda e, x_sl=x_sl, k=k: e.matmul(
                    self.psf[0:1, 4 * TCW:5 * TCW], self.sm("b_ws", k, 1), x_sl,
                    start=(k == 0), stop=(k == NKD - 1)))
            srow_sl = self.srow[0:1, c0:c0 + TCW]
            rowt = self.tmp[0:1, 6 * TCW:7 * TCW]
            mu_row = self.musig[0:1, 0:TCW]
            r_row = self.musig[0:1, TCW:2 * TCW]
            p.ve.waits([tmm, tok_mu, tok_r, self.tok_sws, g.get(("srow", tc))])
            tv1 = p.ve.emit(lambda e, srow_sl=srow_sl, r_row=r_row: e.tensor_tensor(
                srow_sl, self.psf[0:1, 4 * TCW:5 * TCW], r_row, op=ALU.mult))
            tv2 = p.ve.emit(lambda e, rowt=rowt, mu_row=mu_row, r_row=r_row: e.tensor_tensor(
                rowt, mu_row, r_row, op=ALU.mult))
            p.ve.wait(tv2)
            tv3 = p.ve.emit(lambda e, rowt=rowt: e.tensor_scalar(
                rowt, rowt, self.small[0:1, SM["scr"][0] + 16:SM["scr"][0] + 17], None, op0=ALU.mult))
            p.ve.waits([tv1, tv3])
            tv4 = p.ve.emit(lambda e, srow_sl=srow_sl, rowt=rowt: e.tensor_tensor(
                srow_sl, srow_sl, rowt, op=ALU.subtract))
            p.ve.wait(tv4)
            sc_ev = p.ve.emit(lambda e, srow_sl=srow_sl: e.tensor_scalar(
                srow_sl, srow_sl, self.small[0:1, SM["b_bs"][0]:SM["b_bs"][0] + 1], None, op0=ALU.add))
            g[("bank", 4)] = sc_ev
            g[("x2c", tc + 1)] = sc_ev
        self.dump_bf(self.lnF[0][:, 0:512], [g[("lnF3", 0, 0)]], 1536, 512)
        self.dump(self.x2c[:, 0:512], [p.te.tok()], 15456, 512)
        if self.dbg_on:
            self.p.qs.waits([sc_ev])
            self.p.qs.emit(lambda e: e.dma_start(out=self.d_dbg[0:1, 15968:16384], in_=self.srow[0:1, 0:416]), ch="dbg")
        ln3_done = [p.ve.tok(), p.se.tok(), p.te.tok()]

        wt_toks = []
        p.ve.waits([self.tok_ones, self.tok_small] + bar + [g.get(("bank", 0))])
        for b3 in range(3):
            for k in range(NKD):
                col = (b3 * NKD + k) * 128
                wcol = self.sm("b_wg", k * 3 + b3, 1)
                p.ve.waits([g.get(("wtg",))])
                wt_toks.append(p.ve.emit(lambda e, col=col, wcol=wcol: e.tensor_scalar(
                    self.wbuf[:, col:col + 128], self.ones_bf[:], wcol, None, op0=ALU.mult)))
        for b3 in range(3):
            for tc in range(NTC):
                bank = (b3 * NTC + tc) % 5
                p.te.waits(wt_toks + [g.get(("bank", bank))])
                for k in range(NKD):
                    p.te.wait(g[("lnF3", k, tc)])
                    tmm = p.te.emit(lambda e, b3=b3, k=k, tc=tc, bank=bank: e.matmul(
                        self.bankap(bank),
                        self.wbuf[:, (b3 * NKD + k) * 128:(b3 * NKD + k + 1) * 128],
                        self.lnF[k][:, tc * TCW:(tc + 1) * TCW],
                        start=(k == 0), stop=(k == NKD - 1)), inc=(k == NKD - 1))
                g[("wtg",)] = tmm
                dst = self.gn[b3][:, tc * TCW:(tc + 1) * TCW]
                p.se.waits([tmm] + bar)
                tev = p.se.emit(lambda e, dst=dst, bank=bank, b3=b3: e.activation(
                    dst, self.bankap(bank), AF.Sigmoid, bias=self.sm("b_bg", b3, 1)))
                g[("bank", bank)] = tev
                g[("gn", b3, tc)] = tev
        gn_toks = [g[("gn", b3, tc)] for b3 in range(3) for tc in range(NTC)]
        p.ve.waits(gn_toks + bar)
        ts1 = p.ve.emit(lambda e: e.tensor_tensor(self.rgsum[:], self.gn[0][:], self.gn[1][:], op=ALU.add))
        p.ve.wait(ts1)
        ts2 = p.ve.emit(lambda e: e.tensor_tensor(self.rgsum[:], self.rgsum[:], self.gn[2][:], op=ALU.add))
        p.ve.wait(ts2)
        ts3 = p.ve.emit(lambda e: e.tensor_scalar(self.rgsum[:], self.rgsum[:], 1e-6, None, op0=ALU.add))
        p.ve.wait(ts3)
        ts4 = p.ve.emit(lambda e: e.reciprocal(self.rgsum[:], self.rgsum[:]))
        gtok = []
        for b3 in range(3):
            p.ve.wait(ts4)
            gtok.append(p.ve.emit(lambda e, b3=b3: e.tensor_tensor(
                self.gn[b3][:], self.gn[b3][:], self.rgsum[:], op=ALU.mult)))
        self.gn_tok = gtok

        self.dump(self.gn[0][:], gtok, 13408, 2048)
        qkv_wuser = {}
        self.qkv_cached = getattr(self, "qkv_cached", False)

        def load_qkv_w(od, buf, extra=()):
            chn = "wA" if buf == 0 else "wB"
            toks = []
            p.qg.waits([qkv_wuser.get(buf)] + list(extra) + bar)
            if self.qkv_cached:
                p.qs.waits([qkv_wuser.get(buf)] + list(extra) + bar)
                for pj in range(3):
                    toks.append(p.qs.emit(lambda e, pj=pj, od=od, buf=buf: e.dma_start(
                        out=self.wbuf[:, buf * 4096 + pj * D:buf * 4096 + (pj + 1) * D],
                        in_=self.d_qkv_bf[pj, od, :, :]), ch="wC0" if buf == 0 else "wC1"))
                return toks
            for pj, dw in enumerate([self.d_b_Wq, self.d_b_Wk, self.d_b_Wv]):
                p.qg.emit(lambda e, pj=pj, dw=dw, od=od, buf=buf: e.dma_start(
                    out=self.wbuf[:, buf * 4096 + pj * D:buf * 4096 + (pj + 1) * D],
                    in_=dw[od, :, :]), ch=chn)
            t_all = p.qg.tok(chn)
            p.qs.wait(t_all)
            for pj in range(3):
                ts_ = p.qs.emit(lambda e, pj=pj, od=od, buf=buf: e.dma_start(
                    out=self.d_qkv_bf[pj, od, :, :],
                    in_=self.wbuf[:, buf * 4096 + pj * D:buf * 4096 + (pj + 1) * D]), ch="wS0" if buf == 0 else "wS1")
                toks.append(ts_)
            return toks

        def qkv_project(src_tiles, src_toks, width, dst, bias_names=("b_bq", "b_bk", "b_bv"),
                        wtg_guard=None):
            wt = {0: load_qkv_w(0, 0, extra=(wtg_guard,)), 1: load_qkv_w(1, 1, extra=(wtg_guard,))}
            ev_toks = [[None] * NKD for _ in range(3)]
            for od in range(NKD):
                buf = od % 2
                lastmm = None
                for pj in range(3):
                    bank = (od * 3 + pj) % 5
                    p.te.waits(wt[od] + [g.get(("bank", bank))])
                    for k in range(NKD):
                        st_ = src_toks[k]
                        p.te.waits(st_ if isinstance(st_, list) else [st_])
                        lastmm = p.te.emit(lambda e, pj=pj, k=k, bank=bank, buf=buf: e.matmul(
                            self.bankap(bank, w=width),
                            self.wbuf[:, buf * 4096 + pj * D + k * 128:buf * 4096 + pj * D + (k + 1) * 128],
                            src_tiles[k], start=(k == 0), stop=(k == NKD - 1)), inc=(k == NKD - 1))
                    dstap = dst[pj][od][:]
                    p.se.waits([lastmm] + bar)
                    tev = p.se.emit(lambda e, dstap=dstap, bank=bank, pj=pj, od=od, bias_names=bias_names:
                                    e.activation(dstap, self.bankap(bank, w=width), AF.Identity,
                                                 bias=self.sm(bias_names[pj], od, 1)))
                    g[("bank", bank)] = tev
                    ev_toks[pj][od] = tev
                qkv_wuser[buf] = lastmm
                if od + 2 < NKD:
                    wt[od + 2] = load_qkv_w(od + 2, (od + 2) % 2)
            return ev_toks

        def attend(qt_n, kw, q_tiles, k_tiles, v_tiles, qkv_toks, out_fn, gslice, gtok):
            p_toks = []
            for qt in range(qt_n):
                bank = qt % 5
                p.te.waits([g.get(("bank", bank))])
                last = None
                for dk in range(NKD):
                    p.te.waits([qkv_toks[0][dk], qkv_toks[1][dk]])
                    last = p.te.emit(lambda e, dk=dk, qt=qt, bank=bank: e.matmul(
                        self.bankap(bank, w=kw),
                        q_tiles[dk][:, qt * 128:(qt + 1) * 128],
                        k_tiles[dk][:, 0:kw], start=(dk == 0), stop=(dk == NKD - 1)),
                        inc=(dk == NKD - 1))
                rmax = scr(4 + qt, 1)
                p.ve.waits([last, g.get(("smx", qt))])
                t_m = p.ve.emit(lambda e, rmax=rmax, bank=bank, kw=kw: e.reduce_max(
                    rmax, self.bankap(bank, w=kw), axis=AX))
                p.ve.wait(t_m)
                t_b = p.ve.emit(lambda e, rmax=rmax, qt=qt: e.tensor_scalar(
                    scr(8 + qt, 1), rmax, -SC, None, op0=ALU.mult))
                pdst = self.pP[qt][:, 0:kw]
                p.se.waits([t_b, g.get(("pP", qt))] + bar)
                t_e = p.se.emit(lambda e, pdst=pdst, bank=bank, qt=qt, kw=kw: e.activation(
                    pdst, self.bankap(bank, w=kw), AF.Exp, bias=scr(8 + qt, 1), scale=SC,
                    accum_out=scr(12 + qt, 1)))
                g[("bank", bank)] = t_e
                p.ve.wait(t_e)
                t_r = p.ve.emit(lambda e, qt=qt: e.reciprocal(scr(8 + qt, 1), scr(12 + qt, 1)))
                p.ve.wait(t_r)
                t_n = p.ve.emit(lambda e, pdst=pdst, qt=qt: e.tensor_scalar(
                    pdst, pdst, scr(8 + qt, 1), None, op0=ALU.mult))
                g[("smx", qt)] = t_n
                p_toks.append(t_n)
            jn = kw // 128
            tr_toks = []
            ev_ps = g.get(("psb",))
            ntr = 0
            pend = []
            pT_tok = [None] * jn
            for qt in range(qt_n):
                for j in range(jn):
                    p.te.waits([p_toks[qt], self.tok_eye, ev_ps])
                    ttr = p.te.emit(lambda e, qt=qt, j=j, ntr=ntr: e.transpose(
                        self.psb[:, (ntr % 8) * 128:(ntr % 8 + 1) * 128],
                        self.pP[qt][:, j * 128:(j + 1) * 128], self.eye_bf[:]))
                    pend.append((ttr, qt, j, ntr % 8))
                    ntr += 1
                    if ntr % 8 == 0 or (qt == qt_n - 1 and j == jn - 1):
                        batch_last = pend[-1][0]
                        for (tt, q2, j2, s2) in pend:
                            dstp = self.pTt[j2][:, q2 * 128:(q2 + 1) * 128]
                            p.se.waits([batch_last, g.get(("pT", j2))] + bar)
                            tev = p.se.emit(lambda e, dstp=dstp, s2=s2: e.activation(
                                dstp, self.psb[:, s2 * 128:(s2 + 1) * 128], AF.Copy))
                            pT_tok[j2] = tev
                        ev_ps = tev
                        pend = []
            for qt in range(qt_n):
                g[("pP", qt)] = ev_ps
            g[("psb",)] = ev_ps
            vT_tok = [None] * jn
            pend = []
            ntr = 0
            for dv in range(NKD):
                for j in range(jn):
                    p.te.waits([qkv_toks[2][dv], self.tok_eye, ev_ps])
                    ttr = p.te.emit(lambda e, dv=dv, j=j, ntr=ntr: e.transpose(
                        self.psb[:, (ntr % 8) * 128:(ntr % 8 + 1) * 128],
                        v_tiles[dv][:, j * 128:(j + 1) * 128], self.eye_bf[:]))
                    pend.append((ttr, dv, j, ntr % 8))
                    ntr += 1
                    if ntr % 8 == 0 or (dv == NKD - 1 and j == jn - 1):
                        batch_last = pend[-1][0]
                        for (tt, d2, j2, s2) in pend:
                            dstv = self.vT[j2][:, d2 * 128:(d2 + 1) * 128]
                            p.se.waits([batch_last, g.get(("vT", j2))] + bar)
                            tev = p.se.emit(lambda e, dstv=dstv, s2=s2: e.activation(
                                dstv, self.psb[:, s2 * 128:(s2 + 1) * 128], AF.Copy))
                            vT_tok[j2] = tev
                        ev_ps = tev
                        pend = []
            g[("psb",)] = ev_ps
            out_toks = []
            qw = qt_n * 128
            for dv in range(NKD):
                bank = dv % 5
                p.te.waits([g.get(("bank", bank))])
                last = None
                for j in range(jn):
                    p.te.waits([vT_tok[j], pT_tok[j]])
                    last = p.te.emit(lambda e, dv=dv, j=j, bank=bank, qw=qw: e.matmul(
                        self.bankap(bank, w=qw),
                        self.vT[j][:, dv * 128:(dv + 1) * 128],
                        self.pTt[j][:, 0:qw], start=(j == 0), stop=(j == jn - 1)),
                        inc=(j == jn - 1))
                dst, dguard = out_fn(dv)
                p.ve.waits([last, gtok, dguard] + bar)
                tev = p.ve.emit(lambda e, dst=dst, bank=bank, gslice=gslice, qw=qw: e.tensor_tensor(
                    dst, self.bankap(bank, w=qw), gslice, op=ALU.mult))
                g[("bank", bank)] = tev
                g[("pT", None)] = tev
                out_toks.append(tev)
            for j in range(jn):
                g[("pT", j)] = out_toks[-1]
                g[("vT", j)] = out_toks[-1]
            return out_toks

        for tc in range(NTC):
            bank = tc % 5
            p.te.waits([sc_ev, self.tok_ones1, g.get(("bank", bank))])
            tb = p.te.emit(lambda e, tc=tc, bank=bank: e.matmul(
                self.bankap(bank), self.ones_f1[0:1, :],
                self.srow[0:1, tc * TCW:(tc + 1) * TCW], start=True, stop=True))
            dst = self.scb[:, tc * TCW:(tc + 1) * TCW]
            p.ve.waits([tb] + bar)
            tev = p.ve.emit(lambda e, dst=dst, bank=bank: e.tensor_copy(dst, self.bankap(bank)))
            g[("bank", bank)] = tev
            g[("scb", tc)] = tev
        p.te.waits([sc_ev, self.tok_eye_f, g.get(("bank", 5)), g.get(("bank", 6))] + self.rd56)
        ttrs = []
        for k in range(16):
            ttrs.append(p.te.emit(lambda e, k=k: e.transpose(
                self.psf[:, 5 * TCW + k:5 * TCW + k + 1],
                self.srow[0:1, k * 128:(k + 1) * 128], self.ones_f1[0:1, 0:1])))
        p.se.waits(ttrs + bar)
        t_sT = p.se.emit(lambda e: e.activation(self.musig[:, 0:16], self.psf[:, 5 * TCW:5 * TCW + 16], AF.Copy))
        self.rd56 = [t_sT, self.rd56[1]]
        rkA = self.rkA[:]
        rk_dep = [t_sT] + [g[("scb", tc)] for tc in range(NTC)] + bar
        for k in range(16):
            p.ve.waits(rk_dep + [g.get(("rk",))])
            t1 = p.ve.emit(lambda e, k=k: e.tensor_scalar(
                rkA, self.scb[:], self.musig[:, k:k + 1], 0.0, op0=ALU.is_gt, op1=ALU.add,
                accum_out=self.musig[:, 64 + k:65 + k]))
            g[("rk",)] = t1
        p.ve.wait(g[("rk",)])
        tm = p.ve.emit(lambda e: e.tensor_scalar(self.musig[:, 16:32], self.musig[:, 64:80],
                                                 512.0, None, op0=ALU.is_lt))
        wc_user = None
        wc_tok = {}

        def wc_issue(od):
            buf = od % 2
            chn = "wA" if buf == 0 else "wB"
            p.qg.waits([wc_user, g.get(("wtg",))] + bar)
            wc_tok[od] = [p.qg.emit(lambda e, od=od, buf=buf: e.dma_start(
                out=self.wbuf[:, buf * 4096:(buf + 1) * 4096], in_=self.d_b_Wc[od, :, :]), ch=chn)]

        wc_issue(0)
        wc_issue(1)
        cmp_toks = [None] * NKD
        for od in range(NKD):
            buf = od % 2
            bank = od % 5
            p.te.waits(wc_tok[od] + [g.get(("bank", bank)), g.get(("wtg",))])
            last = None
            for kk in range(NKH):
                u, c = kk // NKD, kk % NKD
                p.te.wait(g[("lnF3", c, 0)])
                p.te.wait(g[("lnF3", c, 1)])
                p.te.wait(g[("lnF3", c, 2)])
                p.te.wait(g[("lnF3", c, 3)])
                rhs = self.lnF[c][:, u::4]
                last = p.te.emit(lambda e, kk=kk, bank=bank, buf=buf, rhs=rhs: e.matmul(
                    self.bankap(bank), self.wbuf[:, buf * 4096 + kk * 128:buf * 4096 + (kk + 1) * 128],
                    rhs, start=(kk == 0), stop=(kk == NKH - 1)), inc=(kk == NKH - 1))
            wc_user = last
            if od + 2 < NKD:
                wc_issue(od + 2)
            dst = self.cmp[od][:]
            p.se.waits([last] + bar)
            tev = p.se.emit(lambda e, dst=dst, bank=bank, od=od: e.activation(
                dst, self.bankap(bank), AF.Identity, bias=self.sm("b_bc", od, 1)))
            g[("bank", bank)] = tev
            cmp_toks[od] = tev
        self.dump_bf(self.cmp[0][:], cmp_toks, 2048, TCW)
        qkv_toks = qkv_project([self.cmp[k][:] for k in range(NKD)], cmp_toks, TCW,
                               self.qkv, wtg_guard=wc_user)
        self.qkv_cached = True
        self.dump_bf(self.qkv[0][0][:], [qkv_toks[0][0]], 2560, TCW)
        q_t = [self.qkv[0][k] for k in range(NKD)]
        k_t = [self.qkv[1][k] for k in range(NKD)]
        v_t = [self.qkv[2][k] for k in range(NKD)]

        def comp_out(dv):
            return self.comp_g[:, dv * TCW:(dv + 1) * TCW], None
        comp_done = attend(4, TCW, q_t, k_t, v_t, qkv_toks, comp_out,
                           self.gn[0][:, 0:TCW], gtok[0])

        self.dump_bf(self.comp_g[:], comp_done, 3072, 8 * TCW)
        p.ve.wait(tm)
        p.te.waits([tm, self.tok_tri, g.get(("bank", 5)), self.rd56[0]])
        t_pr = p.te.emit(lambda e: e.matmul(self.bankap(5, w=16), self.tri_f[:],
                                            self.musig[:, 16:32], start=True, stop=True))
        p.te.waits([tm, self.tok_ones1])
        t_cs = p.te.emit(lambda e: e.matmul(self.psf[0:1, 6 * TCW:6 * TCW + 16], self.ones_f1[0:1, :].transpose([0, 1]) if False else self.ones_f32_col[:, 0:1],
                                            self.musig[:, 16:32], start=True, stop=True))
        p.se.waits([t_pr] + bar)
        t_pre = p.se.emit(lambda e: e.activation(self.musig[:, 48:64], self.bankap(5, w=16), AF.Copy))
        self.rd56 = [t_pre, self.rd56[1]]
        p.se.wait(t_cs)
        t_cse = p.se.emit(lambda e: e.activation(self.musig[0:1, 96:112], self.psf[0:1, 6 * TCW:6 * TCW + 16], AF.Copy))
        p.ve.waits([t_cse, self.tok_ones_row])
        t_sc1 = p.ve.emit(lambda e: e.tensor_tensor_scan(
            self.musig[0:1, 112:128], self.ones_row[0:1, 0:16], self.musig[0:1, 96:112],
            0.0, op0=ALU.mult, op1=ALU.add))
        p.ve.wait(t_sc1)
        t_sc2 = p.ve.emit(lambda e: e.tensor_tensor(self.musig[0:1, 112:128], self.musig[0:1, 112:128],
                                                    self.musig[0:1, 96:112], op=ALU.subtract))
        p.te.waits([t_sc2, self.tok_ones1, g.get(("bank", 6)), self.rd56[1]])
        t_bc = p.te.emit(lambda e: e.matmul(self.bankap(6, w=16), self.ones_f1[0:1, :],
                                            self.musig[0:1, 112:128], start=True, stop=True))
        p.ve.waits([t_bc, t_pre])
        t_slot = p.ve.emit(lambda e: e.tensor_tensor(self.musig[:, 32:48], self.musig[:, 48:64],
                                                     self.bankap(6, w=16), op=ALU.add))
        self.rd56 = [t_pre, t_slot]
        self.dump(self.musig[:, 0:96], [t_slot, tm], 7168, 96)
        oh_toks = []
        for k in range(16):
            p.ve.waits([t_slot, self.tok_iota, comp_done[-1], g.get(("oh", k))])
            oh_toks.append(p.ve.emit(lambda e, k=k: e.tensor_scalar(
                self.oh[k][:], self.iota_f[:], self.musig[:, 32 + k:33 + k],
                self.musig[:, 16 + k:17 + k], op0=ALU.is_equal, op1=ALU.mult)))
        ln3T_tok = [None] * 16
        ev_ps = g.get(("psb",))
        ln3T_extra = list(gtok) + [g.get(("rk",)), tm]
        for tt in range(16):
            trs = []
            for c in range(NKD):
                p.te.waits([g[("lnF3", c, tt // 4)], self.tok_eye, ev_ps])
                trs.append(p.te.emit(lambda e, tt=tt, c=c: e.transpose(
                    self.psb[:, c * 128:(c + 1) * 128],
                    self.lnF[c][:, tt * 128:(tt + 1) * 128], self.eye_bf[:])))
            p.se.waits(trs + [g.get(("ln3T", tt))] + ln3T_extra + bar)
            ln3T_tok[tt] = p.se.emit(lambda e, tt=tt: e.activation(
                self.ln3T[tt][:], self.psb[:], AF.Copy))
            ev_ps = ln3T_tok[tt]
        g[("psb",)] = ev_ps
        selT_tok = [None] * 4
        for rt in range(4):
            for half in range(2):
                bank = (rt * 2 + half) % 5
                p.te.waits([g.get(("bank", bank))])
                last = None
                for k in range(16):
                    p.te.waits([oh_toks[k], ln3T_tok[k]])
                    last = p.te.emit(lambda e, rt=rt, half=half, k=k, bank=bank: e.matmul(
                        self.bankap(bank),
                        self.oh[k][:, rt * 128:(rt + 1) * 128],
                        self.ln3T[k][:, half * TCW:(half + 1) * TCW],
                        start=(k == 0), stop=(k == 15)), inc=(k == 15))
                dst = self.selT[rt][:, half * TCW:(half + 1) * TCW]
                p.se.waits([last] + bar)
                tev = p.se.emit(lambda e, dst=dst, bank=bank: e.activation(
                    dst, self.bankap(bank), AF.Copy))
                g[("bank", bank)] = tev
                selT_tok[rt] = tev
        sel_tok = [None] * NKD
        ev_ps = g.get(("psb",))
        for ct in range(NKD):
            trs = []
            for rt in range(4):
                p.te.waits([selT_tok[rt], self.tok_eye, ev_ps, g.get(("rk2",))])
                trs.append(p.te.emit(lambda e, ct=ct, rt=rt: e.transpose(
                    self.psb[:, rt * 128:(rt + 1) * 128],
                    self.selT[rt][:, ct * 128:(ct + 1) * 128], self.eye_bf[:])))
            p.se.waits(trs + [cmp_toks[ct], g.get(("selc", ct))] + bar)
            sel_tok[ct] = p.se.emit(lambda e, ct=ct: e.activation(
                self.cmp[ct][:], self.psb[:, 0:TCW], AF.Copy))
            ev_ps = sel_tok[ct]
        g[("psb",)] = ev_ps
        qkv_toks2 = qkv_project([self.cmp[k][:] for k in range(NKD)], sel_tok, TCW,
                                self.qkv, wtg_guard=selT_tok[3])

        def sel_out(dv):
            return self.sel_g[:, dv * TCW:(dv + 1) * TCW], None
        sel_done = attend(4, TCW, [self.qkv[0][k] for k in range(NKD)],
                          [self.qkv[1][k] for k in range(NKD)],
                          [self.qkv[2][k] for k in range(NKD)], qkv_toks2, sel_out,
                          self.gn[1][:, 0:TCW], gtok[1])

        self.dump_bf(self.sel_g[:], sel_done, 7264, 8 * TCW)
        self.dump(self.scb[:, 0:1536], [g[("scb", tc)] for tc in range(NTC)] + [t_sT], 0, 1536)
        woc = self.scb[:, 0:1536]
        woc_bf = self.nc.alloc_sbuf_tensor_at("woc_bf", [128, 3 * D], BF16,
                                              offset=self.nc.lookup_mloc(self.scb).addr - 0 + 0)             if False else None
        prev_st = None
        rank_done = g.get(("rk",))
        for ot in range(NKD):
            p.qg.waits([prev_st, rank_done, t_slot] + bar)
            tl = p.qg.emit(lambda e, ot=ot: e.dma_start(
                out=self.wocast[:], in_=self.d_b_Wo[ot, :, :]), ch="woL")
            p.qs.wait(tl)
            prev_st = p.qs.emit(lambda e, ot=ot: e.dma_start(
                out=self.d_wo_bf[ot, :, :], in_=self.wocast[:]), ch="woS")
        self.tok_wo_cache = prev_st
        for Q in range(4):
            wq_tiles = self.wq if Q % 2 == 0 else self.wqB
            start = 256 * Q
            src_tiles = [self.lnF[k][:, start:start + 384] for k in range(NKD)]
            tc_a, tc_b = start // TCW, (start + 383) // TCW
            src_toks = [[g[("lnF3", k, tc_a)], g[("lnF3", k, tc_b)]] for k in range(NKD)]
            wg_guard = g.get(("wqg", Q % 2))
            if Q == 0:
                wg_guard = sel_done[-1]
            elif Q == 1:
                wg_guard = self.tok_wo_cache
            qkv_toksw = qkv_project(src_tiles, src_toks, 384, wq_tiles, wtg_guard=wg_guard)
            for wloc in range(2):
                wg = 2 * Q + wloc
                off = 128 * wloc
                qw = [wq_tiles[0][k][:, off:off + 256] for k in range(NKD)]
                kw_ = [wq_tiles[1][k][:, off:off + 256] for k in range(NKD)]
                vw = [wq_tiles[2][k][:, off:off + 256] for k in range(NKD)]

                def win_out(dv, wg=wg):
                    return self.win_g[dv][:, wg * 256:(wg + 1) * 256], selT_tok[3]
                wdone = attend(2, 256, qw, kw_, vw, qkv_toksw, win_out,
                               self.gn[2][:, wg * 256:(wg + 1) * 256], gtok[2])
                g[("wqg", Q % 2)] = wdone[-1]
                for dv in range(NKD):
                    g[("win", dv, wg)] = wdone[dv]

        self.dump_bf(self.win_g[0][:], [g[("win", 0, wg)] for wg in range(NWIN)], 11360, T)
        wo_user = None
        wo_tok = {}

        def wo_issue(idx):
            buf = idx % 2
            p.qs.waits([wo_user, g.get(("wqg", 0)), g.get(("wqg", 1)), self.tok_wo_cache] + bar)
            wo_tok[idx] = [p.qs.emit(lambda e, idx=idx, buf=buf: e.dma_start(
                out=self.wbuf[:, buf * 4096:buf * 4096 + 3 * D], in_=self.d_wo_bf[idx % NKD, :, :]),
                ch="woB0" if buf == 0 else "woB1")]

        self.tok_x3 = [None] * NKD
        self.tok_x3_hbm = [[None] * NTC for _ in range(NKD)]
        comp_all = comp_done[-1]
        sel_all = sel_done[-1]
        win_all = [g[("win", dv, wg)] for dv in range(NKD) for wg in range(NWIN)]
        wo_issue(0)
        wo_issue(1)
        widx = 0
        for tc in range(NTC):
            c0 = tc * TCW
            xbuf = self.x2c if tc % 2 == 0 else self.x2cB
            xch = "x2" if tc % 2 == 0 else "x2b"
            ld_toks = []
            for k in range(NKD):
                xs = self.d_xres_b[k, :, c0:c0 + TCW]
                dst = xbuf[:, k * TCW:(k + 1) * TCW]
                extra0 = [g.get(("rk",)), oh_toks[-1], sel_done[-1], win_all[-1]] + win_all if tc <= 1 else []
                p.qg.waits(bar + [g.get(("x2cc", tc))] + extra0)
                ld_toks.append(p.qg.emit(lambda e, xs=xs, dst=dst: e.dma_start(out=dst, in_=xs), ch=xch))
            ld_all = p.qg.tok(xch)
            z4_toks = []
            for ot in range(NKD):
                buf = widx % 2
                bank = widx % 3
                p.te.waits(wo_tok[widx] + [g.get(("bank", bank))])
                last = None
                klist = list(range(24)) if tc == 0 else list(range(16, 24))
                for i2, kk in enumerate(klist):
                    if kk < 8:
                        rhs = self.comp_g[:, kk * TCW:(kk + 1) * TCW]
                        p.te.wait(comp_all)
                    elif kk < 16:
                        rhs = self.sel_g[:, (kk - 8) * TCW:(kk - 7) * TCW]
                        p.te.wait(sel_all)
                    else:
                        rhs = self.win_g[kk - 16][:, c0:c0 + TCW]
                        p.te.waits(win_all)
                    st_f = (i2 == 0)
                    sp_f = (i2 == len(klist) - 1)
                    last = p.te.emit(lambda e, kk=kk, bank=bank, buf=buf, rhs=rhs, st_f=st_f, sp_f=sp_f: e.matmul(
                        self.bankap(bank), self.wbuf[:, buf * 4096 + kk * 128:buf * 4096 + (kk + 1) * 128],
                        rhs, start=st_f, stop=sp_f), inc=sp_f)
                wo_user = last
                widx += 1
                if widx + 2 <= NTC * NKD:
                    wo_issue(widx + 1)
                m1 = self.tmp[:, (2 + ot % 2) * TCW:(3 + ot % 2) * TCW]
                p.se.waits([last, self.tok_small, g.get(("m1", ot % 2))] + bar)
                t_m1 = p.se.emit(lambda e, m1=m1, bank=bank, ot=ot: e.activation(
                    m1, self.bankap(bank), AF.Identity, bias=self.sm("b_bo", ot, 1)))
                g[("bank", bank)] = t_m1
                z4d = self.z4[:, ot * TCW:(ot + 1) * TCW]
                lnsl = self.lnF[ot][:, c0:c0 + TCW]
                p.ve.waits([t_m1, g.get(("z4r", ot))] + bar)
                t_z4 = p.ve.emit(lambda e, z4d=z4d, m1=m1, lnsl=lnsl: e.tensor_tensor(
                    z4d, m1, lnsl, op=ALU.add))
                g[("m1", ot % 2)] = t_z4
                z4_toks.append((z4d, t_z4))
            tA = tB = None
            for e_i, (z4d, t_z4) in enumerate(z4_toks):
                zb = self.stg[:, e_i * TCW:(e_i + 1) * TCW]
                p.se.waits([t_z4, g.get(("zbslot", e_i)), getattr(self, "stg_guard", None)])
                t_zb = p.se.emit(lambda e, zb=zb, z4d=z4d: e.activation(zb, z4d, AF.Copy))
                p.te.waits([t_zb, self.tok_ones])
                if e_i == 0:
                    p.te.waits(self.rdbk.get(3, []) + self.rdbk.get(4, []) + [g.get(("bank", 3)), g.get(("bank", 4))])
                tA = p.te.emit(lambda e, zb=zb, e_i=e_i: e.matmul(
                    self.bankap(3), self.ones_bf[:], zb, start=(e_i == 0), stop=(e_i == NKD - 1)))
                sq = self.tmp[:, 6 * TCW:7 * TCW]
                p.se.waits([g.get(("sq",)), t_zb, getattr(self, "tmp6_guard", None)])
                t_q = p.se.emit(lambda e, sq=sq, z4d=z4d: e.activation(sq, z4d, AF.Square))
                p.se.wait(tA)
                t_c2 = p.se.emit(lambda e, zb=zb, sq=sq: e.activation(zb, sq, AF.Copy))
                g[("sq",)] = t_c2
                p.te.wait(t_c2)
                tB = p.te.emit(lambda e, zb=zb, e_i=e_i: e.matmul(
                    self.bankap(4), self.ones_bf[:], zb, start=(e_i == 0), stop=(e_i == NKD - 1)))
                g[("zbslot", e_i)] = tB
            self.stg_guard = tB
            tok_mu4, tok_r4 = self.ln_finalize(tA, tB, LN4_EPS, 3, 4)
            o_ap = self.tmp[:, 6 * TCW:7 * TCW]
            for ot in range(NKD):
                z4d = self.z4[:, ot * TCW:(ot + 1) * TCW]
                x_sl = xbuf[:, ot * TCW:(ot + 1) * TCW]
                p.ve.waits([tok_mu4, tok_r4, ld_all])
                tz1 = p.ve.emit(lambda e, z4d=z4d, o_ap=o_ap: e.tensor_tensor(
                    o_ap, z4d, self.musig[:, 0:TCW], op=ALU.subtract))
                p.ve.wait(tz1)
                tz2 = p.ve.emit(lambda e, o_ap=o_ap: e.tensor_tensor(
                    o_ap, o_ap, self.musig[:, TCW:2 * TCW], op=ALU.mult))
                p.ve.wait(tz2)
                tx = p.ve.emit(lambda e, x_sl=x_sl, o_ap=o_ap: e.tensor_tensor(
                    x_sl, x_sl, o_ap, op=ALU.add))
                g[("z4r", ot)] = tx
                self.tok_x3[ot] = tx
                if is_final:
                    xr = self.d_out[ot * 128:(ot + 1) * 128, c0:c0 + TCW]
                    chn = "out"
                else:
                    xr = self.d_xres_a[ot, :, c0:c0 + TCW]
                    chn = "x"
                p.qs.wait(tx)
                self.tok_x3_hbm[ot][tc] = p.qs.emit(lambda e, xr=xr, x_sl=x_sl: e.dma_start(out=xr, in_=x_sl), ch=chn)
            g[("sq",)] = tx
            self.tmp6_guard = tx
            g[("x2cc", tc + 2)] = self.tok_x3_hbm[NKD - 1][tc]
        return self.tok_x3


    def finish_hbm_out(self, hbm_toks):
        p = self.p
        for k in range(NKD):
            p.qs.waits(hbm_toks[k])
        for key, v in list(p.qs.n.items()) + list(p.qg.n.items()):
            if key.endswith(".out") or key.endswith(".dbg"):
                p.qs.ops.append(lambda e, sems, key=key, v=v: e.wait_ge(sems[key], v))

    def finish_output(self, src_tiles, src_toks):
        p = self.p
        for k in range(NKD):
            p.qs.wait(src_toks[k])
            p.qs.emit(lambda e, k=k: e.dma_start(
                out=self.d_out[k * 128:(k + 1) * 128, :], in_=src_tiles[k][:]), ch="out")
        for key, v in list(p.qs.n.items()) + list(p.qg.n.items()):
            if key.endswith(".out") or key.endswith(".dbg"):
                p.qs.ops.append(lambda e, sems, key=key, v=v: e.wait_ge(sems[key], v))

    def run_block(self):
        nc = self.nc
        p = self.p
        from contextlib import ExitStack
        chans = set()
        for s in p.streams.values():
            chans.update(s.n.keys())
            chans.add(s._ch(None))
        with ExitStack() as es:
            sems = {nm: es.enter_context(nc.semaphore(nm.replace(".", "_"))) for nm in sorted(chans)}
            block = es.enter_context(nc.Block())

            def runner(stream):
                def f(e):
                    for op in stream.ops:
                        op(e, sems)
                return f
            block.tensor(runner(p.te))
            block.vector(runner(p.ve))
            block.scalar(runner(p.se))
            block.gpsimd(runner(p.qg))
            block.sync(runner(p.qs))


def build():
    nc = bass.Bass("TRN2", target_bir_lowering=False)
    pr = Prog(nc)
    pr.setup()
    pr.attn_a()
    cur_toks = [pr.tok_x1[k][NTC - 1] for k in range(NKD)]
    if PHASES >= 2:
        is_final = (PHASES == 2)
        cur_toks, hbm_toks = pr.ffn(cur_toks, pr.d_xres_a, pr.d_xres_b, is_final=is_final)
        if is_final:
            pr.finish_hbm_out(hbm_toks)
            pr.run_block()
            return nc
    if PHASES >= 3:
        x2_tok = pr.p.qs.tok("out")
        cur_toks = pr.attn_b(x2_tok, is_final=(PHASES == 3))
        if PHASES == 3:
            pr.finish_hbm_out(pr.tok_x3_hbm)
            pr.run_block()
            return nc
    if PHASES >= 4:
        cur_toks, hbm_toks = pr.ffn(None, pr.d_xres_a, pr.d_xres_b, is_final=True, x_from_hbm=True)
        pr.finish_hbm_out(hbm_toks)
    pr.run_block()
    return nc


_NC_CACHE = None

def kernel(**inputs):
    global _NC_CACHE
    P = host_prep(inputs)
    x_full = np.asarray(inputs["L0_attn_a_x"], dtype=np.float32)
    if _NC_CACHE is None:
        _NC_CACHE = build()
    nc = _NC_CACHE
    in_maps = []
    for b in range(8):
        m = {"x_in": np.ascontiguousarray(x_full[b].T), "small": P["small"]}
        for nm in ["a_Wi", "a_Wf", "a_Wo", "a_Ws", "f_w1", "f_w2", "f_h1", "f_h2", "f_w3",
                   "b_Wq", "b_Wk", "b_Wv", "b_Wo", "b_Wc"]:
            m[nm] = P[nm]
        in_maps.append(m)
    res = run_bass_kernel_spmd(nc, in_maps, core_ids=list(range(8)))
    return np.stack([res.results[b]["x_out"].T for b in range(8)])


# version 06
# speedup vs baseline: 1.1271x; 1.1271x over previous
import os
import numpy as np
import concourse.bass as bass
import concourse.mybir as mybir
from concourse.bass_utils import run_bass_kernel_spmd

F32 = mybir.dt.float32
BF16 = mybir.dt.bfloat16
AF = mybir.ActivationFunctionType
ALU = mybir.AluOpType
AX = mybir.AxisListType.X

D = 1024
T = 2048
NKD = 8
NTC = 4
TCW = 512
DH = 4096
NKH = 32
LN_EPS = 1e-5
LN4_EPS = 1e-6
ATTN_SCALE = 0.125
NWIN = 8
KB = 1024

PHASES = int(os.environ.get("MK_PHASES", "4"))
SELFWAIT = bool(int(os.environ.get("MK_SELFWAIT", "0")))


ALL_CHANNELS = set()
CHAN_N = {}


class Stream:

    def __init__(self, name, step):
        self.name = name
        self.step = step
        self.n = {}
        self.ops = []
        self.wm = {}

    def _ch(self, ch):
        key = self.name if ch is None else f"{self.name}.{ch}"
        ALL_CHANNELS.add(key)
        return key

    def wait(self, tok):
        if tok is None:
            return
        nm, val = tok
        if val <= 0:
            return
        if "." in nm:
            val = max(val, CHAN_N.get(nm, val))
        if self.wm.get(nm, 0) >= val:
            return
        self.wm[nm] = val
        self.ops.append(lambda e, sems, nm=nm, val=val: e.wait_ge(sems[nm], val))

    def waits(self, toks):
        best = {}
        for t in toks:
            if t is None:
                continue
            nm, val = t
            if nm not in best or val > best[nm]:
                best[nm] = val
        for nm, val in best.items():
            self.wait((nm, val))

    def emit(self, fn, inc=True, ch=None):
        key = self._ch(ch)
        if SELFWAIT and self.name in ("ve", "se") and ch is None:
            self.wait(self.tok())
        if inc:
            self.n[key] = self.n.get(key, 0) + self.step
            CHAN_N[key] = self.n[key]
            n = self.n[key]
            self.ops.append(lambda e, sems, fn=fn, key=key: fn(e).then_inc(sems[key], self.step))
            return (key, n)
        self.ops.append(lambda e, sems, fn=fn: fn(e))
        return None

    def tok(self, ch=None):
        key = self._ch(ch)
        return (key, self.n.get(key, 0))

    def bar(self):
        return [(k, v) for k, v in self.n.items()]


class Plan:
    def __init__(self):
        CHAN_N.clear()
        self.te = Stream("te", 1)
        self.ve = Stream("ve", 1)
        self.se = Stream("se", 1)
        self.qg = Stream("qg", 16)
        self.qs = Stream("qs", 16)
        self.streams = dict(te=self.te, ve=self.ve, se=self.se, qg=self.qg, qs=self.qs)

    def barrier(self):
        out = []
        for s in self.streams.values():
            out.extend(s.bar())
        return out


def prep_lhsT(w_in_out):
    K, M = w_in_out.shape
    nk, no = K // 128, M // 128
    r = w_in_out.reshape(nk, 128, no, 128).transpose(2, 1, 0, 3)
    return np.ascontiguousarray(r.reshape(no, 128, nk * 128))


def vec_oT(v):
    return np.ascontiguousarray(v.reshape(-1, 128).T)


SM = {}
_sm_off = 0
def _sm(name, w):
    global _sm_off
    SM[name] = (_sm_off, w)
    _sm_off += w
for _nm, _w in [
    ("a_bi", 8), ("a_bf", 8), ("a_bo", 8), ("a_bs", 8), ("a_gamma", 8), ("a_beta", 8),
    ("f_s1", 32), ("f_s2", 32), ("f_h1s", 32), ("f_h2s", 32), ("f_s3", 8),
    ("b_bq", 8), ("b_bk", 8), ("b_bv", 8), ("b_bo", 8), ("b_bc", 8),
    ("b_bg", 3), ("b_bs", 1), ("b_wg", 24), ("b_ws", 8),
]:
    _sm(_nm, _w)
SMALL_W = _sm_off
_sm("a_hst", 8)
_sm("b_bo_half", 8)
_sm("pglob", 16)
_sm("scr", 48)
SMALL_W_ALL = _sm_off


def host_prep(inputs):
    g = lambda k: np.asarray(inputs[k], dtype=np.float32)
    P = {}
    for nm in ["Wi", "Wf", "Wo", "Ws"]:
        P["a_" + nm] = prep_lhsT(g("L0_attn_a_" + nm))
    P["f_w1"] = prep_lhsT(g("L0_ffn_w1_w").T)
    P["f_w2"] = prep_lhsT(g("L0_ffn_w2_w").T)
    P["f_h1"] = prep_lhsT(g("L0_ffn_h1_w").T)
    P["f_h2"] = prep_lhsT(g("L0_ffn_h2_w").T)
    P["f_w3"] = prep_lhsT(g("L0_ffn_w3_w").T)
    for nm in ["Wq", "Wk", "Wv", "Wo", "Wc"]:
        P["b_" + nm] = prep_lhsT(g("L0_attn_b_" + nm))
    sp = np.zeros((128, SMALL_W), np.float32)
    def put(name, arr):
        o, w = SM[name]
        sp[:, o:o + w] = arr
    put("a_bi", vec_oT(g("L0_attn_a_bi")));  put("a_bf", vec_oT(g("L0_attn_a_bf")))
    put("a_bo", vec_oT(g("L0_attn_a_bo")));  put("a_bs", vec_oT(g("L0_attn_a_bs")))
    put("a_gamma", vec_oT(g("L0_attn_a_gamma"))); put("a_beta", vec_oT(g("L0_attn_a_beta")))
    put("f_s1", vec_oT(g("L0_ffn_w1_s")));   put("f_s2", vec_oT(g("L0_ffn_w2_s")))
    put("f_h1s", vec_oT(g("L0_ffn_h1_s")));  put("f_h2s", vec_oT(g("L0_ffn_h2_s")))
    put("f_s3", vec_oT(g("L0_ffn_w3_s")))
    put("b_bq", vec_oT(g("L0_attn_b_bq")));  put("b_bk", vec_oT(g("L0_attn_b_bk")))
    put("b_bv", vec_oT(g("L0_attn_b_bv")));  put("b_bo", vec_oT(g("L0_attn_b_bo")))
    put("b_bc", vec_oT(g("L0_attn_b_bc")))
    put("b_bg", np.broadcast_to(g("L0_attn_b_bg"), (128, 3)))
    put("b_bs", np.broadcast_to(g("L0_attn_b_bs"), (128, 1)))
    put("b_wg", g("L0_attn_b_Wg").reshape(8, 128, 3).transpose(1, 0, 2).reshape(128, 24))
    put("b_ws", g("L0_attn_b_Ws").reshape(8, 128).T)
    P["small"] = np.ascontiguousarray(sp)
    return P


class Prog:
    def __init__(self, nc, dbg=0):
        self.nc = nc
        self.dbg = dbg
        self.p = Plan()
        self.rd56 = [None, None]
        self.rdbk = {}
        self.guard = {}
        self.alloc_dram()
        self.alloc_sbuf()

    def alloc_dram(self):
        nc = self.nc
        self.d_x = nc.dram_tensor("x_in", [D, T], F32, kind="ExternalInput").ap()
        self.d_small = nc.dram_tensor("small", [128, SMALL_W], F32, kind="ExternalInput").ap()
        for nm, sh in [
            ("a_Wi", [NKD, 128, D]), ("a_Wf", [NKD, 128, D]), ("a_Wo", [NKD, 128, D]), ("a_Ws", [NKD, 128, D]),
            ("f_w1", [NKH, 128, D]), ("f_w2", [NKH, 128, D]),
            ("f_h1", [NKH, 128, DH]), ("f_h2", [NKH, 128, DH]),
            ("f_w3", [NKD, 128, DH]),
            ("b_Wq", [NKD, 128, D]), ("b_Wk", [NKD, 128, D]), ("b_Wv", [NKD, 128, D]),
            ("b_Wo", [NKD, 128, 3 * D]), ("b_Wc", [NKD, 128, DH]),
        ]:
            setattr(self, "d_" + nm, nc.dram_tensor(nm, sh, F32, kind="ExternalInput").ap())
        self.d_out = nc.dram_tensor("x_out", [D, T], F32, kind="ExternalOutput").ap()
        self.dbg_on = bool(int(os.environ.get("MK_DBG", "0")))
        if self.dbg_on:
            self.d_dbg = nc.dram_tensor("dbg", [128, 16384], F32, kind="ExternalOutput").ap()
        self.d_hbuf_a = nc.dram_tensor("hbuf_a", [NKH, 128, T], BF16, kind="Internal").ap()
        self.d_hbuf_b = nc.dram_tensor("hbuf_b", [NKH, 128, T], BF16, kind="Internal").ap()
        self.d_xres_a = nc.dram_tensor("xres_a", [NKD, 128, T], F32, kind="Internal").ap()
        self.d_wo_bf = nc.dram_tensor("wo_bf", [NKD, 128, 3 * D], BF16, kind="Internal").ap()
        self.d_qkv_bf = nc.dram_tensor("qkv_bf", [3, NKD, 128, D], BF16, kind="Internal").ap()
        self.d_xres_b = nc.dram_tensor("xres_b", [NKD, 128, T], F32, kind="Internal").ap()

    def alloc_sbuf(self):
        nc = self.nc
        base = (nc.sbuf_base + 31) & ~31
        top = nc.sbuf_top

        def at(name, shape, dtype, off_b):
            nbytes = shape[1] * {F32: 4, BF16: 2}[dtype]
            assert base + off_b + nbytes <= top, f"{name} overflows sbuf: {off_b}+{nbytes} vs {top-base}"
            return nc.alloc_sbuf_tensor_at(name, shape, dtype, offset=base + off_b)

        assert SMALL_W_ALL * 4 <= 1536
        self.wbuf = at("wbuf", [128, 8192], BF16, 0)
        self.stg = at("stg", [128, 4096], BF16, 16 * KB)
        self.small = at("small", [128, SMALL_W_ALL], F32, 24 * KB)
        self.musig = at("musig", [128, 1024], F32, 25 * KB + 512)
        self.tmp = at("tmp", [128, 3584], F32, 29 * KB + 512)
        self.iota_f = at("iota_f", [128, 512], F32, 43 * KB + 512)
        self.eye_bf = at("eye_bf", [128, 128], BF16, 45 * KB + 512)
        self.ones_bf = at("ones_bf", [128, 128], BF16, 45 * KB + 768)
        self.ones_f1 = at("ones_f1", [1, 128], F32, 46 * KB)
        self.tri_f = at("tri_f", [128, 128], F32, 46 * KB + 512)
        self.eye_f = at("eye_f", [128, 128], F32, 47 * KB)
        self.ones_f32_col = at("ones_f32_col", [128, 1], F32, 25 * KB + 352)
        self.ones_row = at("ones_row", [1, 16], F32, 25 * KB + 384)
        assert SMALL_W_ALL * 4 <= 1376
        AR = 47 * KB + 512
        self.x = [at(f"x{i}", [128, T], F32, AR + i * 8 * KB) for i in range(NKD)]
        self.ln_c_bf = at("ln_c_bf", [128, 8 * TCW], BF16, AR + 64 * KB)
        self.ln_c_f = at("ln_c_f", [128, 8 * TCW], F32, AR + 72 * KB)
        self.z_c = at("z_c", [128, 8 * TCW], F32, AR + 88 * KB)
        self.hA = [at(f"hA{k}", [128, T], BF16, AR + k * 4 * KB) for k in range(NKH)]
        self.lnF = [at(f"lnF{k}", [128, T], BF16, AR + 128 * KB + k * 4 * KB) for k in range(NKD)]
        AB = AR
        self.gn = [at(f"gn{b}", [128, T], F32, AB + b * 8 * KB) for b in range(3)]
        self.z4 = at("z4", [128, 8 * TCW], F32, AB + 0 * KB)
        self.comp_g = at("comp_g", [128, 8 * TCW], BF16, AB + 24 * KB)
        self.srow = at("srow", [1, T], F32, AB + 32 * KB)
        self.sel_g = at("sel_g", [128, 8 * TCW], BF16, AB + 32 * KB)
        self.rgsum = at("rgsum", [128, T], F32, AB + 40 * KB)
        self.ln3T = [at(f"ln3T{k}", [128, D], BF16, AB + 40 * KB + k * 2 * KB) for k in range(16)]
        self.win_g = [at(f"win_g{k}", [128, T], BF16, AB + 40 * KB + k * 4 * KB) for k in range(NKD)]
        self.x2c = at("x2c", [128, 8 * TCW], F32, AB + 72 * KB)
        self.cmp = [at(f"cmp{k}", [128, TCW], BF16, AB + 72 * KB + k * KB) for k in range(NKD)]
        self.wq = []
        for pj in range(3):
            self.wq.append([at(f"wq{pj}_{k}", [128, 384], BF16, AB + 72 * KB + (pj * 8 + k) * 768)
                            for k in range(NKD)])
        _wqb_off = [AB + i * 768 for i in range(21)] + [AB + 120 * KB + i * 768 for i in range(3)]
        self.wqB = []
        for pj in range(3):
            self.wqB.append([at(f"wqB{pj}_{k}", [128, 384], BF16, _wqb_off[pj * 8 + k])
                             for k in range(NKD)])
        self.qkv = []
        for pj in range(3):
            self.qkv.append([at(f"qkv{pj}_{k}", [128, TCW], BF16, AB + 80 * KB + pj * 8 * KB + k * KB)
                             for k in range(NKD)])
        self.rkA = at("rkA", [128, T], F32, AB + 48 * KB)
        self.x2cB = at("x2cB", [128, 8 * TCW], F32, AB + 88 * KB)
        self.stgB = at("stgB", [128, 1024], BF16, AB + 104 * KB)
        self.oh = [at(f"oh{k}", [128, TCW], BF16, AB + 80 * KB + k * KB) for k in range(16)]
        self.selT = [at(f"selT{j}", [128, D], BF16, AB + 104 * KB + j * 2 * KB) for j in range(4)]
        self.vT = [at(f"vT{j}", [128, D], BF16, AB + 104 * KB + j * 2 * KB) for j in range(4)]
        self.pP = [at(f"pP{j}", [128, TCW], BF16, AB + 112 * KB + j * KB) for j in range(4)]
        self.pTt = [at(f"pT{j}", [128, TCW], BF16, AB + 116 * KB + j * KB) for j in range(4)]
        self.scb = at("scb", [128, T], F32, AB + 120 * KB)
        self.wocast = at("wocast", [128, 3 * D], BF16, AB + 120 * KB)
        self.psf = nc.alloc_psum_tensor("psf", [128, 3584], F32)
        self.psb = nc.alloc_psum_tensor("psb", [128, 1024], BF16)

    def bankap(self, i, w=TCW, off=0):
        return self.psf[:, i * TCW + off:i * TCW + off + w]

    def dump(self, ap, deps, col, ncols):
        if not self.dbg_on:
            return None
        self.p.qs.waits(deps)
        return self.p.qs.emit(lambda e, ap=ap, col=col, ncols=ncols: e.dma_start(
            out=self.d_dbg[:, col:col + ncols], in_=ap), ch="dbg")

    def dump_bf(self, ap, deps, col, ncols):
        if not self.dbg_on:
            return None
        self.p.qg.waits(deps)
        return self.p.qg.emit(lambda e, ap=ap, col=col, ncols=ncols: e.dma_start(
            out=self.d_dbg[:, col:col + ncols], in_=ap), ch="dbg")

    def sm(self, name, i=0, w=1):
        o, wd = SM[name]
        assert i + w <= wd
        return self.small[:, o + i:o + i + w]

    def setup(self):
        p = self.p
        for k in range(NKD):
            p.qs.emit(lambda e, k=k: e.dma_start(
                out=self.x[k][:], in_=self.d_x[k * 128:(k + 1) * 128, :]), ch="ld")
        p.qs.emit(lambda e: e.dma_start(out=self.small[:, 0:SMALL_W], in_=self.d_small[:, :]), ch="ld")
        ld_all = p.qs.tok("ld")
        self.tok_x = [ld_all] * NKD
        self.tok_small = ld_all
        t_io = p.qg.emit(lambda e: e.iota(self.iota_f[:], pattern=[[1, 512]], base=0,
                                          channel_multiplier=0, allow_small_or_imprecise_dtypes=True), ch="misc")
        self.tok_pglob = p.qg.emit(lambda e: e.iota(self.sm("pglob", 0, 16), pattern=[[128, 16]], base=0,
                                                    channel_multiplier=1, allow_small_or_imprecise_dtypes=True), ch="misc")
        t_pi = p.qg.emit(lambda e: e.iota(self.sm("scr", 0, 1), pattern=[[0, 1]], base=0,
                                          channel_multiplier=1, allow_small_or_imprecise_dtypes=True), ch="misc")
        self.tok_iota = t_io
        p.ve.waits([t_io, t_pi])
        t_eye_f = p.ve.emit(lambda e: e.tensor_scalar(self.tmp[:, 0:128], self.iota_f[:, 0:128],
                                                      self.sm("scr", 0, 1), None, op0=ALU.is_equal))
        p.ve.wait(t_eye_f)
        self.tok_eye = p.ve.emit(lambda e: e.tensor_copy(self.eye_bf[:], self.tmp[:, 0:128]))
        self.tok_eye_f = p.ve.emit(lambda e: e.tensor_copy(self.eye_f[:], self.tmp[:, 0:128]))
        self.tok_tri = p.ve.emit(lambda e: e.tensor_scalar(self.tri_f[:], self.iota_f[:, 0:128],
                                                           self.sm("scr", 0, 1), None, op0=ALU.is_gt))
        self.tok_ones = p.ve.emit(lambda e: e.memset(self.ones_bf[:], 1.0))
        self.tok_ones1 = p.ve.emit(lambda e: e.memset(self.ones_f1[:], 1.0))
        self.tok_ones_col = p.ve.emit(lambda e: e.memset(self.ones_f32_col[:], 1.0))
        self.tok_ones_row = p.ve.emit(lambda e: e.memset(self.ones_row[:], 1.0))
        p.ve.wait(self.tok_small)
        self.tok_bo_half = p.ve.emit(lambda e: e.tensor_scalar(self.sm("b_bo_half", 0, 8),
                                                               self.sm("b_bo", 0, 8), 0.5, None, op0=ALU.mult))
        self.tok_eps5 = p.ve.emit(lambda e: e.memset(self.sm("scr", 1, 1), LN_EPS))
        self.tok_eps6 = p.ve.emit(lambda e: e.memset(self.sm("scr", 2, 1), LN4_EPS))

    def ln_stats_chunk(self, tiles, eps, bA=5, bB=6):
        p = self.p
        tok_mmA = tok_mmB = None
        prev_B = None
        for e_i, (ap, tk) in enumerate(tiles):
            sl = self.stgB[:, (e_i % 2) * TCW:(e_i % 2 + 1) * TCW]
            sq = self.tmp[:, 6 * TCW:7 * TCW]
            p.se.waits([tk, prev_B, getattr(self, "stgB_guard", None)])
            t_c = p.se.emit(lambda e, ap=ap, sl=sl: e.activation(sl, ap, AF.Copy))
            p.te.waits([t_c, self.tok_ones])
            if e_i == 0:
                p.te.waits(self.rdbk.get(bA, []) + self.rdbk.get(bB, []))
            tok_mmA = p.te.emit(lambda e, sl=sl, e_i=e_i, bA=bA: e.matmul(
                self.bankap(bA), self.ones_bf[:], sl, start=(e_i == 0), stop=(e_i == NKD - 1)))
            p.se.waits([tk, getattr(self, "tmp6_guard", None)])
            t_q = p.se.emit(lambda e, ap=ap, sq=sq: e.activation(sq, ap, AF.Square))
            p.se.wait(tok_mmA)
            t_c2 = p.se.emit(lambda e, sl=sl, sq=sq: e.activation(sl, sq, AF.Copy))
            p.te.wait(t_c2)
            tok_mmB = p.te.emit(lambda e, sl=sl, e_i=e_i, bB=bB: e.matmul(
                self.bankap(bB), self.ones_bf[:], sl, start=(e_i == 0), stop=(e_i == NKD - 1)))
            prev_B = tok_mmB
        self.stgB_guard = tok_mmB
        return self.ln_finalize(tok_mmA, tok_mmB, eps, bA, bB)

    def ln_finalize(self, tok_mmA, tok_mmB, eps, bA=5, bB=6):
        p = self.p
        mu = self.musig[:, 0:TCW]
        r = self.musig[:, TCW:2 * TCW]
        t6 = self.tmp[:, 6 * TCW:7 * TCW]
        p.ve.wait(tok_mmA)
        tok_mu = p.ve.emit(lambda e, bA=bA: e.tensor_scalar(mu, self.bankap(bA), 1.0 / D, None, op0=ALU.mult))
        p.ve.wait(tok_mmB)
        t_v1 = p.ve.emit(lambda e, bB=bB: e.tensor_scalar(t6, self.bankap(bB), 1.0 / D, None, op0=ALU.mult))
        p.ve.wait(tok_mu)
        t_v2 = p.ve.emit(lambda e: e.tensor_tensor(r, mu, mu, op=ALU.mult))
        p.ve.waits([t_v1, t_v2])
        t_v3 = p.ve.emit(lambda e: e.tensor_tensor(t6, t6, r, op=ALU.subtract))
        eps_ap = self.sm("scr", 1 if eps == LN_EPS else 2, 1)
        p.se.waits([t_v3, self.tok_eps5, self.tok_eps6])
        t_l = p.se.emit(lambda e: e.activation(r, t6, AF.Ln, bias=eps_ap))
        p.se.wait(t_l)
        tok_r = p.se.emit(lambda e: e.activation(r, r, AF.Exp, scale=-0.5))
        self.rdbk[bA] = [tok_mu]
        self.rdbk[bB] = [t_v1]
        self.rd56 = [tok_mu, t_v1]
        return tok_mu, tok_r

    def attn_a(self):
        p = self.p
        W = [self.d_a_Wi, self.d_a_Wf, self.d_a_Wo, self.d_a_Ws]
        BIAS = ["a_bi", "a_bf", "a_bo", "a_bs"]
        FUNC = [AF.Sigmoid, AF.Sigmoid, AF.Sigmoid, AF.Tanh]
        wtok = {}
        wbuf_user = {}
        h_upd = [None] * NKD
        self.tok_x1 = [[None] * NTC for _ in range(NKD)]
        self.tok_x1_hbm = [[None] * NTC for _ in range(NKD)]
        last_proj_mm = None
        last_zread = None
        guard = self.guard

        def issue_w(gi):
            tc, e_i, buf = gi // NKD, gi % NKD, gi % 2
            toks = []
            for pj in range(4):
                p.qg.wait(wbuf_user.get((buf, pj)))
                toks.append(p.qg.emit(lambda e, pj=pj, e_i=e_i, buf=buf: e.dma_start(
                    out=self.wbuf[:, buf * 4096 + pj * D:buf * 4096 + (pj + 1) * D],
                    in_=W[pj][e_i, :, :]), ch="wA" if buf == 0 else "wB"))
            wtok[gi] = toks

        issue_w(0)
        issue_w(1)

        for tc in range(NTC):
            c0 = tc * TCW
            tiles = [(self.x[k][:, c0:c0 + TCW], self.tok_x[k]) for k in range(NKD)]
            tok_mu, tok_r = self.ln_stats_chunk(tiles, LN_EPS)
            if tc == 0:
                self.dump(self.x[0][:, 0:TCW], [self.tok_x[0]], 13312, TCW)
                self.dump(self.musig[:], [tok_mu, tok_r], 14336, 1024)
            ln_toks = []
            for k in range(NKD):
                lf = self.ln_c_f[:, k * TCW:(k + 1) * TCW]
                lb = self.ln_c_bf[:, k * TCW:(k + 1) * TCW]
                p.ve.waits([tok_mu, self.tok_x[k]])
                x_sl = self.x[k][:, c0:c0 + TCW]
                t1 = p.ve.emit(lambda e, lf=lf, x_sl=x_sl: e.tensor_tensor(
                    lf, x_sl, self.musig[:, 0:TCW], op=ALU.subtract))
                p.ve.waits([t1, tok_r])
                t2 = p.ve.emit(lambda e, lf=lf: e.tensor_tensor(lf, lf, self.musig[:, TCW:2 * TCW], op=ALU.mult))
                p.se.waits([t2, last_proj_mm])
                t3 = p.se.emit(lambda e, lb=lb, lf=lf: e.activation(lb, lf, AF.Copy))
                ln_toks.append((t2, t3))
                if tc == 0:
                    self.dump(lf, [t2], k * TCW, TCW)

            zb_toks = []
            for e_i in range(NKD):
                gi = tc * NKD + e_i
                buf = gi % 2
                mm_tok = [None] * 4
                for pj in range(4):
                    bank = (gi * 4 + pj) % 5
                    p.te.waits(wtok[gi])
                    p.te.wait(guard.get(("bank", bank)))
                    for k in range(NKD):
                        p.te.wait(ln_toks[k][1])
                        tk = p.te.emit(lambda e, pj=pj, k=k, bank=bank, buf=buf: e.matmul(
                            self.bankap(bank),
                            self.wbuf[:, buf * 4096 + pj * D + k * 128:buf * 4096 + pj * D + (k + 1) * 128],
                            self.ln_c_bf[:, k * TCW:(k + 1) * TCW],
                            start=(k == 0), stop=(k == NKD - 1)), inc=(k == NKD - 1))
                        if tk is not None:
                            mm_tok[pj] = tk
                    wbuf_user[(buf, pj)] = mm_tok[pj]
                last_proj_mm = mm_tok[3]
                if gi + 2 < NTC * NKD:
                    issue_w(gi + 2)
                slots = {}
                ev = {}
                for pj, slot in [(0, 0), (3, 1), (1, 2), (2, 3)]:
                    bank = (gi * 4 + pj) % 5
                    ap = self.tmp[:, slot * TCW:(slot + 1) * TCW]
                    p.se.waits([mm_tok[pj], guard.get(("tmp", slot))])
                    ev[pj] = p.se.emit(lambda e, ap=ap, pj=pj, bank=bank, e_i=e_i: e.activation(
                        ap, self.bankap(bank), FUNC[pj], bias=self.sm(BIAS[pj], e_i, 1)))
                    guard[("bank", bank)] = ev[pj]
                    slots[pj] = ap
                a_ap = self.tmp[:, 4 * TCW:5 * TCW]
                p.ve.waits([ev[0], ev[3], guard.get(("tmp", 4))])
                t_a = p.ve.emit(lambda e, a_ap=a_ap, s0=slots[0], s3=slots[3]: e.tensor_tensor(
                    a_ap, s0, s3, op=ALU.mult))
                guard[("tmp", 0)] = t_a
                guard[("tmp", 1)] = t_a
                h_ap = self.tmp[:, 5 * TCW:6 * TCW]
                p.ve.waits([ev[1], t_a, h_upd[e_i], guard.get(("tmp", 5))])
                if tc == 0:
                    t_h = p.ve.emit(lambda e, h_ap=h_ap, s1=slots[1], a_ap=a_ap: e.tensor_tensor_scan(
                        h_ap, s1, a_ap, 0.0, op0=ALU.mult, op1=ALU.add))
                else:
                    t_h = p.ve.emit(lambda e, h_ap=h_ap, s1=slots[1], a_ap=a_ap, e_i=e_i: e.tensor_tensor_scan(
                        h_ap, s1, a_ap, self.sm("a_hst", e_i, 1), op0=ALU.mult, op1=ALU.add))
                guard[("tmp", 2)] = t_h
                guard[("tmp", 4)] = t_h
                p.ve.wait(t_h)
                h_upd[e_i] = p.ve.emit(lambda e, h_ap=h_ap, e_i=e_i: e.tensor_copy(
                    self.sm("a_hst", e_i, 1), h_ap[:, TCW - 1:TCW]))
                p.ve.waits([ev[2], h_upd[e_i]])
                t_y = p.ve.emit(lambda e, h_ap=h_ap, s2=slots[2]: e.tensor_tensor(h_ap, s2, h_ap, op=ALU.mult))
                guard[("tmp", 3)] = t_y
                z_ap = self.z_c[:, e_i * TCW:(e_i + 1) * TCW]
                p.ve.waits([t_y, ln_toks[e_i][0], last_zread])
                t_z = p.ve.emit(lambda e, z_ap=z_ap, h_ap=h_ap, e_i=e_i: e.tensor_tensor(
                    z_ap, self.ln_c_f[:, e_i * TCW:(e_i + 1) * TCW], h_ap, op=ALU.add))
                guard[("tmp", 5)] = t_z
                if tc == 0 and e_i == 0:
                    d1 = self.dump(slots[0], [ev[0]], 4096, TCW)
                    d2 = self.dump(slots[3], [ev[3]], 4608, TCW)
                    d3 = self.dump(slots[1], [ev[1]], 5120, TCW)
                    d4 = self.dump(slots[2], [ev[2]], 5632, TCW)
                    d5 = self.dump(a_ap, [t_a], 6144, TCW)
                    d6 = self.dump(h_ap, [t_y], 6656, TCW)
                    for dd, sl in [(d1,0),(d2,1),(d3,2),(d4,3),(d5,4),(d6,5)]:
                        if dd is not None:
                            guard[("tmp", sl)] = dd
                zb = self.stg[:, e_i * TCW:(e_i + 1) * TCW]
                p.se.waits([t_z, guard.get(("zbslot", e_i))])
                t_zb = p.se.emit(lambda e, zb=zb, z_ap=z_ap: e.activation(zb, z_ap, AF.Copy))
                zb_toks.append((zb, z_ap, t_zb))
                guard[("stg67w", e_i)] = t_zb

            tA = tB = None
            for e_i, (zb, z_ap, t_zb) in enumerate(zb_toks):
                p.te.waits([t_zb, self.tok_ones])
                if e_i == 0:
                    p.te.waits(self.rd56)
                tA = p.te.emit(lambda e, zb=zb, e_i=e_i: e.matmul(
                    self.bankap(5), self.ones_bf[:], zb, start=(e_i == 0), stop=(e_i == NKD - 1)))
                sq = self.tmp[:, 6 * TCW:7 * TCW]
                p.se.waits([guard.get(("sq",)), t_zb])
                t_q = p.se.emit(lambda e, sq=sq, z_ap=z_ap: e.activation(sq, z_ap, AF.Square))
                p.se.wait(tA)
                t_c2 = p.se.emit(lambda e, zb=zb, sq=sq: e.activation(zb, sq, AF.Copy))
                guard[("sq",)] = t_c2
                p.te.wait(t_c2)
                tB = p.te.emit(lambda e, zb=zb, e_i=e_i: e.matmul(
                    self.bankap(6), self.ones_bf[:], zb, start=(e_i == 0), stop=(e_i == NKD - 1)))
                guard[("zbslot", e_i)] = tB
            self.stg_guard = tB
            if tc == 0:
                self.dump(self.z_c[:], [t for (_, _, t) in zb_toks], 8192, 4096)
            tok_mu2, tok_r2 = self.ln_finalize(tA, tB, LN_EPS)
            if tc == 0:
                self.dump(self.musig[:], [tok_mu2, tok_r2], 12288, 1024)

            o_ap = self.tmp[:, 6 * TCW:7 * TCW]
            for e_i in range(NKD):
                z_ap = self.z_c[:, e_i * TCW:(e_i + 1) * TCW]
                p.ve.waits([tok_mu2, tok_r2])
                tz1 = p.ve.emit(lambda e, z_ap=z_ap, o_ap=o_ap: e.tensor_tensor(
                    o_ap, z_ap, self.musig[:, 0:TCW], op=ALU.subtract))
                p.ve.wait(tz1)
                tz2 = p.ve.emit(lambda e, o_ap=o_ap: e.tensor_tensor(
                    o_ap, o_ap, self.musig[:, TCW:2 * TCW], op=ALU.mult))
                p.ve.wait(tz2)
                tz3 = p.ve.emit(lambda e, o_ap=o_ap, e_i=e_i: e.tensor_scalar(
                    o_ap, o_ap, self.sm("a_gamma", e_i, 1), self.sm("a_beta", e_i, 1),
                    op0=ALU.mult, op1=ALU.add))
                p.ve.wait(tz3)
                x_sl = self.x[e_i][:, c0:c0 + TCW]
                tx = p.ve.emit(lambda e, o_ap=o_ap, x_sl=x_sl: e.tensor_tensor(
                    x_sl, x_sl, o_ap, op=ALU.add))
                self.tok_x1[e_i][tc] = tx
                p.qs.wait(tx)
                xr_sl = self.d_xres_a[e_i, :, c0:c0 + TCW]
                self.tok_x1_hbm[e_i][tc] = p.qs.emit(lambda e, xr_sl=xr_sl, x_sl=x_sl: e.dma_start(
                    out=xr_sl, in_=x_sl), ch="x")
            guard[("sq",)] = tx
            self.tmp6_guard = tx
            last_zread = tok_r2
            last_zread = guard.get(("sq",))


    def ffn(self, x_toks, d_xres_in, d_xres_out, is_final=False, x_from_hbm=False):
        p = self.p
        bar = p.barrier()
        g = self.guard

        for tc in range(NTC):
            c0 = tc * TCW
            if x_from_hbm:
                for k in range(NKD):
                    xs = d_xres_in[k, :, c0:c0 + TCW]
                    dst = self.x2c[:, k * TCW:(k + 1) * TCW]
                    p.qg.waits(bar + [g.get(("fx2c", tc))])
                    p.qg.emit(lambda e, xs=xs, dst=dst: e.dma_start(out=dst, in_=xs), ch="x2")
                ld_all = p.qg.tok("x2")
                xtile = lambda k: self.x2c[:, k * TCW:(k + 1) * TCW]
                xtok = lambda k: ld_all
            else:
                xtile = lambda k: self.x[k][:, c0:c0 + TCW]
                xtok = lambda k: x_toks[k]
            tiles = [(xtile(k), xtok(k)) for k in range(NKD)]
            tok_mu, tok_r = self.ln_stats_chunk(tiles, LN_EPS)
            t2 = t3 = None
            for k in range(NKD):
                sl = self.tmp[:, (k % 2) * TCW:(k % 2 + 1) * TCW]
                lb = self.lnF[k][:, c0:c0 + TCW]
                x_sl = xtile(k)
                p.ve.waits([tok_mu, xtok(k), g.get(("lntmp", k % 2))] + bar)
                t1 = p.ve.emit(lambda e, sl=sl, x_sl=x_sl: e.tensor_tensor(
                    sl, x_sl, self.musig[:, 0:TCW], op=ALU.subtract))
                p.ve.waits([t1, tok_r])
                t2 = p.ve.emit(lambda e, sl=sl: e.tensor_tensor(sl, sl, self.musig[:, TCW:2 * TCW], op=ALU.mult))
                p.se.waits([t2] + bar)
                t3 = p.se.emit(lambda e, lb=lb, sl=sl: e.activation(lb, sl, AF.Copy))
                g[("lntmp", k % 2)] = t3
                g[("lnF", k, tc)] = t3
            g[("fx2c", tc + 1)] = t2
        ln_bar = [p.ve.tok(), p.se.tok()]
        self.dump_bf(self.lnF[0][:], [g[("lnF", 0, tc)] for tc in range(NTC)], 0, T)

        g[("bank", 5)] = self.rd56[0]
        rot = [0]

        def stage_mm_tc(o, nk, wslice, rhs_fn, rhs_toks, wtoks, tc):
            bank = rot[0] % 6
            rot[0] += 1
            p.te.waits(wtoks + [g.get(("bank", bank))])
            last = None
            for k in range(nk):
                p.te.wait(rhs_toks(k, tc))
                last = p.te.emit(lambda e, k=k, tc=tc, bank=bank, wslice=wslice, rhs_fn=rhs_fn: e.matmul(
                    self.bankap(bank), wslice(k), rhs_fn(k, tc),
                    start=(k == 0), stop=(k == nk - 1)), inc=(k == nk - 1))
            return [(tc, bank, last)]

        def stage_mm(o, nk, wslice, rhs_fn, rhs_toks, wtoks, extra_mm_wait=None, korder=None):
            out = []
            ks = korder if korder is not None else list(range(nk))
            for tc in range(NTC):
                bank = rot[0] % 6
                rot[0] += 1
                p.te.waits(wtoks + [g.get(("bank", bank))])
                if extra_mm_wait:
                    p.te.waits(extra_mm_wait)
                last = None
                for i_k, k in enumerate(ks):
                    p.te.wait(rhs_toks(k, tc))
                    st_f, sp_f = (i_k == 0), (i_k == len(ks) - 1)
                    last = p.te.emit(lambda e, k=k, tc=tc, bank=bank, wslice=wslice, rhs_fn=rhs_fn,
                                     st_f=st_f, sp_f=sp_f: e.matmul(
                        self.bankap(bank), wslice(k), rhs_fn(k, tc),
                        start=st_f, stop=sp_f), inc=sp_f)
                out.append((tc, bank, last))
            return out

        wuser = {}
        s1_wtok = {}

        def s1_issue(o):
            buf = o % 2
            p.qg.waits([wuser.get((buf, 0)), wuser.get((buf, 1))] + bar)
            chn = "wA" if buf == 0 else "wB"
            tw1 = p.qg.emit(lambda e, o=o, buf=buf: e.dma_start(
                out=self.wbuf[:, buf * 4096:buf * 4096 + D], in_=self.d_f_w1[o, :, :]), ch=chn)
            tw2 = p.qg.emit(lambda e, o=o, buf=buf: e.dma_start(
                out=self.wbuf[:, buf * 4096 + D:buf * 4096 + 2 * D], in_=self.d_f_w2[o, :, :]), ch=chn)
            s1_wtok[o] = [tw1, tw2]

        s1_issue(0)
        s1_issue(1)
        hA_tok = [[None] * NTC for _ in range(NKH)]
        for o in range(NKH):
            buf = o % 2
            w1s = lambda k, buf=buf: self.wbuf[:, buf * 4096 + k * 128:buf * 4096 + (k + 1) * 128]
            w2s = lambda k, buf=buf: self.wbuf[:, buf * 4096 + D + k * 128:buf * 4096 + D + (k + 1) * 128]
            rhs = lambda k, tc: self.lnF[k][:, tc * TCW:(tc + 1) * TCW]
            rtok = lambda k, tc: g[("lnF", k, tc)]
            last1 = last2 = None
            for tc in range(NTC):
                ((_, b1, t1k),) = stage_mm_tc(o, NKD, w1s, rhs, rtok, s1_wtok[o], tc)
                ((_, b2, t2k),) = stage_mm_tc(o, NKD, w2s, rhs, rtok, s1_wtok[o], tc)
                last1, last2 = t1k, t2k
                t1s = self.tmp[:, (2 + tc % 2) * TCW:(3 + tc % 2) * TCW]
                p.se.waits([t1k, g.get(("s1tmp", tc % 2))] + bar)
                ts = p.se.emit(lambda e, t1s=t1s, b1=b1, o=o: e.activation(
                    t1s, self.bankap(b1), AF.Silu, scale=self.sm("f_s1", o, 1)))
                g[("bank", b1)] = ts
                dst = self.hA[o][:, tc * TCW:(tc + 1) * TCW]
                p.ve.waits([t2k, ts] + bar + ln_bar)
                th = p.ve.emit(lambda e, dst=dst, b2=b2, o=o, t1s=t1s: e.scalar_tensor_tensor(
                    dst, self.bankap(b2), self.sm("f_s2", o, 1), t1s, op0=ALU.mult, op1=ALU.mult))
                g[("bank", b2)] = th
                g[("s1tmp", tc % 2)] = th
                hA_tok[o][tc] = th
            wuser[(buf, 0)] = last1
            wuser[(buf, 1)] = last2
            if o + 2 < NKH:
                s1_issue(o + 2)

        self.dump_bf(self.hA[0][:], hA_tok[0], 2048, T)
        def big_stage(d_w, s_name, d_spill, in_toks, final_stage=False, d_xin=None, d_xout=None,
                      in_res=None, resident_out=False, korder=None):
            wtok = {}

            def issue(o):
                buf = o % 2
                p.qg.waits([wuser.get((buf, "big")), wuser.get((buf, 0)), wuser.get((buf, 1))] + bar)
                wtok[o] = [p.qg.emit(lambda e, o=o, buf=buf: e.dma_start(
                    out=self.wbuf[:, buf * 4096:(buf + 1) * 4096], in_=d_w[o, :, :]),
                    ch="wA" if buf == 0 else "wB")]

            issue(0)
            issue(1)
            no = NKD if final_stage else NKH
            spill_tok = [[None] * NTC for _ in range(no)]
            out_toks = [[None] * NTC for _ in range(no)]
            res_out = {}
            scale_nm = s_name
            for o in range(no):
                buf = o % 2
                ws = lambda k, buf=buf: self.wbuf[:, buf * 4096 + k * 128:buf * 4096 + (k + 1) * 128]
                if in_res:
                    rhs = lambda k, tc: (in_res[k][0](tc) if k in in_res
                                         else self.hA[k][:, tc * TCW:(tc + 1) * TCW])
                    rtok = lambda k, tc: (in_res[k][1] if k in in_res else in_toks[k][tc])
                else:
                    rhs = lambda k, tc: self.hA[k][:, tc * TCW:(tc + 1) * TCW]
                    rtok = lambda k, tc: in_toks[k][tc]
                mm = stage_mm(o, NKH, ws, rhs, rtok, wtok[o], korder=korder)
                wuser[(buf, "big")] = mm[-1][2]
                if o + 2 < no:
                    issue(o + 2)
                for tc, bank, mtk in mm:
                    if not final_stage:
                        if resident_out and o >= 24:
                            rdst = self.lnF[o - 24][:, tc * TCW:(tc + 1) * TCW]
                            p.se.waits([mtk, g.get(("resout", o, tc))] + bar)
                            tev = p.se.emit(lambda e, rdst=rdst, bank=bank, o=o: e.activation(
                                rdst, self.bankap(bank), AF.Gelu, scale=self.sm(scale_nm, o, 1)))
                            g[("bank", bank)] = tev
                            res_out[(o, tc)] = tev
                            spill_tok[o][tc] = tev
                            out_toks[o][tc] = tev
                            continue
                        slot = (o * NTC + tc) % 4
                        sdst = self.stg[:, slot * TCW:(slot + 1) * TCW]
                        p.se.waits([mtk, g.get(("stgslot", slot))] + bar)
                        tev = p.se.emit(lambda e, sdst=sdst, bank=bank, o=o: e.activation(
                            sdst, self.bankap(bank), AF.Gelu, scale=self.sm(scale_nm, o, 1)))
                        g[("bank", bank)] = tev
                        dsp = d_spill[o, :, tc * TCW:(tc + 1) * TCW]
                        p.qs.wait(tev)
                        tsp = p.qs.emit(lambda e, dsp=dsp, sdst=sdst: e.dma_start(out=dsp, in_=sdst), ch="sp")
                        g[("stgslot", slot)] = tsp
                        spill_tok[o][tc] = tsp
                    else:
                        slot = (o * NTC + tc) % 2
                        xo = self.tmp[:, slot * TCW:(slot + 1) * TCW]
                        xsrc = d_xin[o, :, tc * TCW:(tc + 1) * TCW]
                        p.qg.waits([g.get(("xold", slot))] + bar)
                        txo = p.qg.emit(lambda e, xo=xo, xsrc=xsrc: e.dma_start(out=xo, in_=xsrc),
                                        ch="xo0" if slot == 0 else "xo1")
                        dst = self.tmp[:, (2 + slot) * TCW:(3 + slot) * TCW]
                        p.ve.waits([mtk, txo, g.get(("xnew", slot))] + bar)
                        tx = p.ve.emit(lambda e, dst=dst, bank=bank, o=o, xo=xo: e.scalar_tensor_tensor(
                            dst, self.bankap(bank), self.sm("f_s3", o, 1), xo, op0=ALU.mult, op1=ALU.add))
                        g[("bank", bank)] = tx
                        g[("xold", slot)] = tx
                        p.qs.wait(tx)
                        if is_final:
                            dd = self.d_out[o * 128:(o + 1) * 128, tc * TCW:(tc + 1) * TCW]
                        else:
                            dd = d_xout[o, :, tc * TCW:(tc + 1) * TCW]
                        tw = p.qs.emit(lambda e, dd=dd, dst=dst: e.dma_start(out=dd, in_=dst), ch="out")
                        g[("xnew", slot)] = tw
                        out_toks[o][tc] = (tx, tw)
            return spill_tok, out_toks, mm[-1][2]

        def chased_reload(d_src, last_mm, sp_toks, n_tiles, phase_tag):
            toks = [None] * n_tiles
            grp = (n_tiles + 3) // 4
            for gi4 in range(4):
                ks = range(gi4 * grp, min((gi4 + 1) * grp, n_tiles))
                q = p.qs if gi4 % 2 == 0 else p.qg
                chn = f"rl{phase_tag}{gi4}"
                for k in ks:
                    q.waits([last_mm, p.qs.tok("sp")])
                    q.emit(lambda e, k=k: e.dma_start(out=self.hA[k][:], in_=d_src[k, :, :]), ch=chn)
                t = q.tok(chn)
                for k in ks:
                    toks[k] = t
            return toks

        KORD = list(range(24, NKH)) + list(range(24))
        sp_a, s2out, s2_last = big_stage(self.d_f_h1, "f_h1s", self.d_hbuf_a, hA_tok,
                                         resident_out=True)
        rel = chased_reload(self.d_hbuf_a, s2_last, None, 24, "a" + str(is_final))
        rel_tok = [[rel[k]] * NTC if k < 24 else None for k in range(24)]
        in_res2 = {k: ((lambda tc, kk=k: self.lnF[kk - 24][:, tc * TCW:(tc + 1) * TCW]), s2out[k][NTC - 1])
                   for k in range(24, NKH)}
        rel_full = [[rel[k]] * NTC for k in range(24)] + [[None] * NTC for _ in range(8)]
        sp_b, _, s3_last = big_stage(self.d_f_h2, "f_h2s", self.d_hbuf_b, rel_full,
                                     in_res=in_res2, korder=KORD)
        rel2 = chased_reload(self.d_hbuf_b, s3_last, None, NKH, "b" + str(is_final))
        rel2_full = [[rel2[k]] * NTC for k in range(NKH)]
        _, out_toks, _ = big_stage(self.d_f_w3, "f_s3", None, rel2_full, final_stage=True,
                                   d_xin=d_xres_in, d_xout=d_xres_out)
        new_hbm_toks = [[out_toks[k][tc][1] for tc in range(NTC)] for k in range(NKD)]
        return None, new_hbm_toks


    def attn_b(self, x2_hbm_tok, is_final=False):
        p = self.p
        g = self.guard
        bar = p.barrier() + [x2_hbm_tok]
        SC = ATTN_SCALE

        def scr(i, w=1):
            return self.sm("scr", i, w)

        sc_ev = None
        for tc in range(NTC):
            c0 = tc * TCW
            ld_toks = []
            for k in range(NKD):
                xs = self.d_xres_b[k, :, c0:c0 + TCW]
                dst = self.x2c[:, k * TCW:(k + 1) * TCW]
                p.qg.waits(bar + [g.get(("x2c", tc))])
                ld_toks.append(p.qg.emit(lambda e, xs=xs, dst=dst: e.dma_start(out=dst, in_=xs), ch="x2"))
            ld_all = p.qg.tok("x2")
            tiles = [(self.x2c[:, k * TCW:(k + 1) * TCW], ld_all) for k in range(NKD)]
            tok_mu, tok_r = self.ln_stats_chunk(tiles, LN_EPS)
            if tc == 0:
                p.te.waits([self.tok_small, self.tok_ones_col, g.get(("bank", 4))])
                for k in range(NKD):
                    t_sw = p.te.emit(lambda e, k=k: e.matmul(
                        self.psf[0:1, 4 * TCW:4 * TCW + 1], self.sm("b_ws", k, 1),
                        self.ones_f32_col[:, 0:1], start=(k == 0), stop=(k == NKD - 1)),
                        inc=(k == NKD - 1))
                p.se.wait(t_sw)
                self.tok_sws = p.se.emit(lambda e: e.activation(
                    self.small[0:1, SM["scr"][0] + 16:SM["scr"][0] + 17],
                    self.psf[0:1, 4 * TCW:4 * TCW + 1], AF.Copy))
                g[("bank", 4)] = self.tok_sws
            for k in range(NKD):
                sl = self.tmp[:, (k % 2) * TCW:(k % 2 + 1) * TCW]
                lb = self.lnF[k][:, c0:c0 + TCW]
                x_sl = self.x2c[:, k * TCW:(k + 1) * TCW]
                p.ve.waits([tok_mu, ld_all, g.get(("lntmp", k % 2))] + bar)
                t1 = p.ve.emit(lambda e, sl=sl, x_sl=x_sl: e.tensor_tensor(
                    sl, x_sl, self.musig[:, 0:TCW], op=ALU.subtract))
                p.ve.waits([t1, tok_r])
                t2 = p.ve.emit(lambda e, sl=sl: e.tensor_tensor(sl, sl, self.musig[:, TCW:2 * TCW], op=ALU.mult))
                p.se.waits([t2] + bar)
                t3 = p.se.emit(lambda e, lb=lb, sl=sl: e.activation(lb, sl, AF.Copy))
                g[("lnF3", k, tc)] = t3
                g[("lntmp", k % 2)] = t3
                p.te.waits([ld_all, self.tok_small, g.get(("bank", 4)) if k == 0 else None])
                tmm = p.te.emit(lambda e, x_sl=x_sl, k=k: e.matmul(
                    self.psf[0:1, 4 * TCW:5 * TCW], self.sm("b_ws", k, 1), x_sl,
                    start=(k == 0), stop=(k == NKD - 1)))
            srow_sl = self.srow[0:1, c0:c0 + TCW]
            rowt = self.tmp[0:1, 6 * TCW:7 * TCW]
            mu_row = self.musig[0:1, 0:TCW]
            r_row = self.musig[0:1, TCW:2 * TCW]
            p.ve.waits([tmm, tok_mu, tok_r, self.tok_sws, g.get(("srow", tc))])
            tv1 = p.ve.emit(lambda e, srow_sl=srow_sl, r_row=r_row: e.tensor_tensor(
                srow_sl, self.psf[0:1, 4 * TCW:5 * TCW], r_row, op=ALU.mult))
            tv2 = p.ve.emit(lambda e, rowt=rowt, mu_row=mu_row, r_row=r_row: e.tensor_tensor(
                rowt, mu_row, r_row, op=ALU.mult))
            p.ve.wait(tv2)
            tv3 = p.ve.emit(lambda e, rowt=rowt: e.tensor_scalar(
                rowt, rowt, self.small[0:1, SM["scr"][0] + 16:SM["scr"][0] + 17], None, op0=ALU.mult))
            p.ve.waits([tv1, tv3])
            tv4 = p.ve.emit(lambda e, srow_sl=srow_sl, rowt=rowt: e.tensor_tensor(
                srow_sl, srow_sl, rowt, op=ALU.subtract))
            p.ve.wait(tv4)
            sc_ev = p.ve.emit(lambda e, srow_sl=srow_sl: e.tensor_scalar(
                srow_sl, srow_sl, self.small[0:1, SM["b_bs"][0]:SM["b_bs"][0] + 1], None, op0=ALU.add))
            g[("bank", 4)] = sc_ev
            g[("x2c", tc + 1)] = sc_ev
        self.dump_bf(self.lnF[0][:, 0:512], [g[("lnF3", 0, 0)]], 1536, 512)
        self.dump(self.x2c[:, 0:512], [p.te.tok()], 15456, 512)
        if self.dbg_on:
            self.p.qs.waits([sc_ev])
            self.p.qs.emit(lambda e: e.dma_start(out=self.d_dbg[0:1, 15968:16384], in_=self.srow[0:1, 0:416]), ch="dbg")
        ln3_done = [p.ve.tok(), p.se.tok(), p.te.tok()]

        wt_toks = []
        p.ve.waits([self.tok_ones, self.tok_small] + bar + [g.get(("bank", 0))])
        for b3 in range(3):
            for k in range(NKD):
                col = (b3 * NKD + k) * 128
                wcol = self.sm("b_wg", k * 3 + b3, 1)
                p.ve.waits([g.get(("wtg",))])
                wt_toks.append(p.ve.emit(lambda e, col=col, wcol=wcol: e.tensor_scalar(
                    self.wbuf[:, col:col + 128], self.ones_bf[:], wcol, None, op0=ALU.mult)))
        for b3 in range(3):
            for tc in range(NTC):
                bank = (b3 * NTC + tc) % 5
                p.te.waits(wt_toks + [g.get(("bank", bank))])
                for k in range(NKD):
                    p.te.wait(g[("lnF3", k, tc)])
                    tmm = p.te.emit(lambda e, b3=b3, k=k, tc=tc, bank=bank: e.matmul(
                        self.bankap(bank),
                        self.wbuf[:, (b3 * NKD + k) * 128:(b3 * NKD + k + 1) * 128],
                        self.lnF[k][:, tc * TCW:(tc + 1) * TCW],
                        start=(k == 0), stop=(k == NKD - 1)), inc=(k == NKD - 1))
                g[("wtg",)] = tmm
                dst = self.gn[b3][:, tc * TCW:(tc + 1) * TCW]
                p.se.waits([tmm] + bar)
                tev = p.se.emit(lambda e, dst=dst, bank=bank, b3=b3: e.activation(
                    dst, self.bankap(bank), AF.Sigmoid, bias=self.sm("b_bg", b3, 1)))
                g[("bank", bank)] = tev
                g[("gn", b3, tc)] = tev
        gn_toks = [g[("gn", b3, tc)] for b3 in range(3) for tc in range(NTC)]
        p.ve.waits(gn_toks + bar)
        ts1 = p.ve.emit(lambda e: e.tensor_tensor(self.rgsum[:], self.gn[0][:], self.gn[1][:], op=ALU.add))
        p.ve.wait(ts1)
        ts2 = p.ve.emit(lambda e: e.tensor_tensor(self.rgsum[:], self.rgsum[:], self.gn[2][:], op=ALU.add))
        p.ve.wait(ts2)
        ts3 = p.ve.emit(lambda e: e.tensor_scalar(self.rgsum[:], self.rgsum[:], 1e-6, None, op0=ALU.add))
        p.ve.wait(ts3)
        ts4 = p.ve.emit(lambda e: e.reciprocal(self.rgsum[:], self.rgsum[:]))
        gtok = []
        for b3 in range(3):
            p.ve.wait(ts4)
            gtok.append(p.ve.emit(lambda e, b3=b3: e.tensor_tensor(
                self.gn[b3][:], self.gn[b3][:], self.rgsum[:], op=ALU.mult)))
        self.gn_tok = gtok

        self.dump(self.gn[0][:], gtok, 13408, 2048)
        qkv_wuser = {}
        self.qkv_cached = getattr(self, "qkv_cached", False)

        def load_qkv_w(od, buf, extra=()):
            chn = "wA" if buf == 0 else "wB"
            toks = []
            p.qg.waits([qkv_wuser.get(buf)] + list(extra) + bar)
            if self.qkv_cached:
                p.qs.waits([qkv_wuser.get(buf)] + list(extra) + bar)
                for pj in range(3):
                    toks.append(p.qs.emit(lambda e, pj=pj, od=od, buf=buf: e.dma_start(
                        out=self.wbuf[:, buf * 4096 + pj * D:buf * 4096 + (pj + 1) * D],
                        in_=self.d_qkv_bf[pj, od, :, :]), ch="wC0" if buf == 0 else "wC1"))
                return toks
            for pj, dw in enumerate([self.d_b_Wq, self.d_b_Wk, self.d_b_Wv]):
                p.qg.emit(lambda e, pj=pj, dw=dw, od=od, buf=buf: e.dma_start(
                    out=self.wbuf[:, buf * 4096 + pj * D:buf * 4096 + (pj + 1) * D],
                    in_=dw[od, :, :]), ch=chn)
            t_all = p.qg.tok(chn)
            p.qs.wait(t_all)
            for pj in range(3):
                ts_ = p.qs.emit(lambda e, pj=pj, od=od, buf=buf: e.dma_start(
                    out=self.d_qkv_bf[pj, od, :, :],
                    in_=self.wbuf[:, buf * 4096 + pj * D:buf * 4096 + (pj + 1) * D]), ch="wS0" if buf == 0 else "wS1")
                toks.append(ts_)
            return toks

        def qkv_project(src_tiles, src_toks, width, dst, bias_names=("b_bq", "b_bk", "b_bv"),
                        wtg_guard=None):
            wt = {0: load_qkv_w(0, 0, extra=(wtg_guard,)), 1: load_qkv_w(1, 1, extra=(wtg_guard,))}
            ev_toks = [[None] * NKD for _ in range(3)]
            for od in range(NKD):
                buf = od % 2
                lastmm = None
                for pj in range(3):
                    bank = (od * 3 + pj) % 5
                    p.te.waits(wt[od] + [g.get(("bank", bank))])
                    for k in range(NKD):
                        st_ = src_toks[k]
                        p.te.waits(st_ if isinstance(st_, list) else [st_])
                        lastmm = p.te.emit(lambda e, pj=pj, k=k, bank=bank, buf=buf: e.matmul(
                            self.bankap(bank, w=width),
                            self.wbuf[:, buf * 4096 + pj * D + k * 128:buf * 4096 + pj * D + (k + 1) * 128],
                            src_tiles[k], start=(k == 0), stop=(k == NKD - 1)), inc=(k == NKD - 1))
                    dstap = dst[pj][od][:]
                    p.se.waits([lastmm] + bar)
                    tev = p.se.emit(lambda e, dstap=dstap, bank=bank, pj=pj, od=od, bias_names=bias_names:
                                    e.activation(dstap, self.bankap(bank, w=width), AF.Identity,
                                                 bias=self.sm(bias_names[pj], od, 1)))
                    g[("bank", bank)] = tev
                    ev_toks[pj][od] = tev
                qkv_wuser[buf] = lastmm
                if od + 2 < NKD:
                    wt[od + 2] = load_qkv_w(od + 2, (od + 2) % 2)
            return ev_toks

        def attend(qt_n, kw, q_tiles, k_tiles, v_tiles, qkv_toks, out_fn, gslice, gtok):
            p_toks = []
            for qt in range(qt_n):
                bank = qt % 5
                p.te.waits([g.get(("bank", bank))])
                last = None
                for dk in range(NKD):
                    p.te.waits([qkv_toks[0][dk], qkv_toks[1][dk]])
                    last = p.te.emit(lambda e, dk=dk, qt=qt, bank=bank: e.matmul(
                        self.bankap(bank, w=kw),
                        q_tiles[dk][:, qt * 128:(qt + 1) * 128],
                        k_tiles[dk][:, 0:kw], start=(dk == 0), stop=(dk == NKD - 1)),
                        inc=(dk == NKD - 1))
                rmax = scr(4 + qt, 1)
                p.ve.waits([last, g.get(("smx", qt))])
                t_m = p.ve.emit(lambda e, rmax=rmax, bank=bank, kw=kw: e.reduce_max(
                    rmax, self.bankap(bank, w=kw), axis=AX))
                p.ve.wait(t_m)
                t_b = p.ve.emit(lambda e, rmax=rmax, qt=qt: e.tensor_scalar(
                    scr(8 + qt, 1), rmax, -SC, None, op0=ALU.mult))
                pdst = self.pP[qt][:, 0:kw]
                p.se.waits([t_b, g.get(("pP", qt))] + bar)
                t_e = p.se.emit(lambda e, pdst=pdst, bank=bank, qt=qt, kw=kw: e.activation(
                    pdst, self.bankap(bank, w=kw), AF.Exp, bias=scr(8 + qt, 1), scale=SC,
                    accum_out=scr(12 + qt, 1)))
                g[("bank", bank)] = t_e
                p.ve.wait(t_e)
                t_r = p.ve.emit(lambda e, qt=qt: e.reciprocal(scr(8 + qt, 1), scr(12 + qt, 1)))
                p.ve.wait(t_r)
                t_n = p.ve.emit(lambda e, pdst=pdst, qt=qt: e.tensor_scalar(
                    pdst, pdst, scr(8 + qt, 1), None, op0=ALU.mult))
                g[("smx", qt)] = t_n
                p_toks.append(t_n)
            jn = kw // 128
            tr_toks = []
            ev_ps = g.get(("psb",))
            ntr = 0
            pend = []
            pT_tok = [None] * jn
            for qt in range(qt_n):
                for j in range(jn):
                    p.te.waits([p_toks[qt], self.tok_eye, ev_ps])
                    ttr = p.te.emit(lambda e, qt=qt, j=j, ntr=ntr: e.transpose(
                        self.psb[:, (ntr % 8) * 128:(ntr % 8 + 1) * 128],
                        self.pP[qt][:, j * 128:(j + 1) * 128], self.eye_bf[:]))
                    pend.append((ttr, qt, j, ntr % 8))
                    ntr += 1
                    if ntr % 8 == 0 or (qt == qt_n - 1 and j == jn - 1):
                        batch_last = pend[-1][0]
                        for (tt, q2, j2, s2) in pend:
                            dstp = self.pTt[j2][:, q2 * 128:(q2 + 1) * 128]
                            p.se.waits([batch_last, g.get(("pT", j2))] + bar)
                            tev = p.se.emit(lambda e, dstp=dstp, s2=s2: e.activation(
                                dstp, self.psb[:, s2 * 128:(s2 + 1) * 128], AF.Copy))
                            pT_tok[j2] = tev
                        ev_ps = tev
                        pend = []
            for qt in range(qt_n):
                g[("pP", qt)] = ev_ps
            g[("psb",)] = ev_ps
            vT_tok = [None] * jn
            pend = []
            ntr = 0
            for dv in range(NKD):
                for j in range(jn):
                    p.te.waits([qkv_toks[2][dv], self.tok_eye, ev_ps])
                    ttr = p.te.emit(lambda e, dv=dv, j=j, ntr=ntr: e.transpose(
                        self.psb[:, (ntr % 8) * 128:(ntr % 8 + 1) * 128],
                        v_tiles[dv][:, j * 128:(j + 1) * 128], self.eye_bf[:]))
                    pend.append((ttr, dv, j, ntr % 8))
                    ntr += 1
                    if ntr % 8 == 0 or (dv == NKD - 1 and j == jn - 1):
                        batch_last = pend[-1][0]
                        for (tt, d2, j2, s2) in pend:
                            dstv = self.vT[j2][:, d2 * 128:(d2 + 1) * 128]
                            p.se.waits([batch_last, g.get(("vT", j2))] + bar)
                            tev = p.se.emit(lambda e, dstv=dstv, s2=s2: e.activation(
                                dstv, self.psb[:, s2 * 128:(s2 + 1) * 128], AF.Copy))
                            vT_tok[j2] = tev
                        ev_ps = tev
                        pend = []
            g[("psb",)] = ev_ps
            out_toks = []
            qw = qt_n * 128
            for dv in range(NKD):
                bank = dv % 5
                p.te.waits([g.get(("bank", bank))])
                last = None
                for j in range(jn):
                    p.te.waits([vT_tok[j], pT_tok[j]])
                    last = p.te.emit(lambda e, dv=dv, j=j, bank=bank, qw=qw: e.matmul(
                        self.bankap(bank, w=qw),
                        self.vT[j][:, dv * 128:(dv + 1) * 128],
                        self.pTt[j][:, 0:qw], start=(j == 0), stop=(j == jn - 1)),
                        inc=(j == jn - 1))
                dst, dguard = out_fn(dv)
                p.ve.waits([last, gtok, dguard] + bar)
                tev = p.ve.emit(lambda e, dst=dst, bank=bank, gslice=gslice, qw=qw: e.tensor_tensor(
                    dst, self.bankap(bank, w=qw), gslice, op=ALU.mult))
                g[("bank", bank)] = tev
                g[("pT", None)] = tev
                out_toks.append(tev)
            for j in range(jn):
                g[("pT", j)] = out_toks[-1]
                g[("vT", j)] = out_toks[-1]
            return out_toks

        for tc in range(NTC):
            bank = tc % 5
            p.te.waits([sc_ev, self.tok_ones1, g.get(("bank", bank))])
            tb = p.te.emit(lambda e, tc=tc, bank=bank: e.matmul(
                self.bankap(bank), self.ones_f1[0:1, :],
                self.srow[0:1, tc * TCW:(tc + 1) * TCW], start=True, stop=True))
            dst = self.scb[:, tc * TCW:(tc + 1) * TCW]
            p.ve.waits([tb] + bar)
            tev = p.ve.emit(lambda e, dst=dst, bank=bank: e.tensor_copy(dst, self.bankap(bank)))
            g[("bank", bank)] = tev
            g[("scb", tc)] = tev
        p.te.waits([sc_ev, self.tok_eye_f, g.get(("bank", 5)), g.get(("bank", 6))] + self.rd56)
        ttrs = []
        for k in range(16):
            ttrs.append(p.te.emit(lambda e, k=k: e.transpose(
                self.psf[:, 5 * TCW + k:5 * TCW + k + 1],
                self.srow[0:1, k * 128:(k + 1) * 128], self.ones_f1[0:1, 0:1])))
        p.se.waits(ttrs + bar)
        t_sT = p.se.emit(lambda e: e.activation(self.musig[:, 0:16], self.psf[:, 5 * TCW:5 * TCW + 16], AF.Copy))
        self.rd56 = [t_sT, self.rd56[1]]
        rkA = self.rkA[:]
        rk_dep = [t_sT] + [g[("scb", tc)] for tc in range(NTC)] + bar
        for k in range(16):
            p.ve.waits(rk_dep + [g.get(("rk",))])
            t1 = p.ve.emit(lambda e, k=k: e.tensor_scalar(
                rkA, self.scb[:], self.musig[:, k:k + 1], 0.0, op0=ALU.is_gt, op1=ALU.add,
                accum_out=self.musig[:, 64 + k:65 + k]))
            g[("rk",)] = t1
        p.ve.wait(g[("rk",)])
        tm = p.ve.emit(lambda e: e.tensor_scalar(self.musig[:, 16:32], self.musig[:, 64:80],
                                                 512.0, None, op0=ALU.is_lt))
        wc_user = None
        wc_tok = {}

        def wc_issue(od):
            buf = od % 2
            chn = "wA" if buf == 0 else "wB"
            p.qg.waits([wc_user, g.get(("wtg",))] + bar)
            wc_tok[od] = [p.qg.emit(lambda e, od=od, buf=buf: e.dma_start(
                out=self.wbuf[:, buf * 4096:(buf + 1) * 4096], in_=self.d_b_Wc[od, :, :]), ch=chn)]

        wc_issue(0)
        wc_issue(1)
        cmp_toks = [None] * NKD
        for od in range(NKD):
            buf = od % 2
            bank = od % 5
            p.te.waits(wc_tok[od] + [g.get(("bank", bank)), g.get(("wtg",))])
            last = None
            for kk in range(NKH):
                u, c = kk // NKD, kk % NKD
                p.te.wait(g[("lnF3", c, 0)])
                p.te.wait(g[("lnF3", c, 1)])
                p.te.wait(g[("lnF3", c, 2)])
                p.te.wait(g[("lnF3", c, 3)])
                rhs = self.lnF[c][:, u::4]
                last = p.te.emit(lambda e, kk=kk, bank=bank, buf=buf, rhs=rhs: e.matmul(
                    self.bankap(bank), self.wbuf[:, buf * 4096 + kk * 128:buf * 4096 + (kk + 1) * 128],
                    rhs, start=(kk == 0), stop=(kk == NKH - 1)), inc=(kk == NKH - 1))
            wc_user = last
            if od + 2 < NKD:
                wc_issue(od + 2)
            dst = self.cmp[od][:]
            p.se.waits([last] + bar)
            tev = p.se.emit(lambda e, dst=dst, bank=bank, od=od: e.activation(
                dst, self.bankap(bank), AF.Identity, bias=self.sm("b_bc", od, 1)))
            g[("bank", bank)] = tev
            cmp_toks[od] = tev
        self.dump_bf(self.cmp[0][:], cmp_toks, 2048, TCW)
        qkv_toks = qkv_project([self.cmp[k][:] for k in range(NKD)], cmp_toks, TCW,
                               self.qkv, wtg_guard=wc_user)
        self.qkv_cached = True
        self.dump_bf(self.qkv[0][0][:], [qkv_toks[0][0]], 2560, TCW)
        q_t = [self.qkv[0][k] for k in range(NKD)]
        k_t = [self.qkv[1][k] for k in range(NKD)]
        v_t = [self.qkv[2][k] for k in range(NKD)]

        def comp_out(dv):
            return self.comp_g[:, dv * TCW:(dv + 1) * TCW], None
        comp_done = attend(4, TCW, q_t, k_t, v_t, qkv_toks, comp_out,
                           self.gn[0][:, 0:TCW], gtok[0])

        self.dump_bf(self.comp_g[:], comp_done, 3072, 8 * TCW)
        p.ve.wait(tm)
        p.te.waits([tm, self.tok_tri, g.get(("bank", 5)), self.rd56[0]])
        t_pr = p.te.emit(lambda e: e.matmul(self.bankap(5, w=16), self.tri_f[:],
                                            self.musig[:, 16:32], start=True, stop=True))
        p.te.waits([tm, self.tok_ones1])
        t_cs = p.te.emit(lambda e: e.matmul(self.psf[0:1, 6 * TCW:6 * TCW + 16], self.ones_f1[0:1, :].transpose([0, 1]) if False else self.ones_f32_col[:, 0:1],
                                            self.musig[:, 16:32], start=True, stop=True))
        p.se.waits([t_pr] + bar)
        t_pre = p.se.emit(lambda e: e.activation(self.musig[:, 48:64], self.bankap(5, w=16), AF.Copy))
        self.rd56 = [t_pre, self.rd56[1]]
        p.se.wait(t_cs)
        t_cse = p.se.emit(lambda e: e.activation(self.musig[0:1, 96:112], self.psf[0:1, 6 * TCW:6 * TCW + 16], AF.Copy))
        p.ve.waits([t_cse, self.tok_ones_row])
        t_sc1 = p.ve.emit(lambda e: e.tensor_tensor_scan(
            self.musig[0:1, 112:128], self.ones_row[0:1, 0:16], self.musig[0:1, 96:112],
            0.0, op0=ALU.mult, op1=ALU.add))
        p.ve.wait(t_sc1)
        t_sc2 = p.ve.emit(lambda e: e.tensor_tensor(self.musig[0:1, 112:128], self.musig[0:1, 112:128],
                                                    self.musig[0:1, 96:112], op=ALU.subtract))
        p.te.waits([t_sc2, self.tok_ones1, g.get(("bank", 6)), self.rd56[1]])
        t_bc = p.te.emit(lambda e: e.matmul(self.bankap(6, w=16), self.ones_f1[0:1, :],
                                            self.musig[0:1, 112:128], start=True, stop=True))
        p.ve.waits([t_bc, t_pre])
        t_slot = p.ve.emit(lambda e: e.tensor_tensor(self.musig[:, 32:48], self.musig[:, 48:64],
                                                     self.bankap(6, w=16), op=ALU.add))
        self.rd56 = [t_pre, t_slot]
        self.dump(self.musig[:, 0:96], [t_slot, tm], 7168, 96)
        oh_toks = []
        for k in range(16):
            p.ve.waits([t_slot, self.tok_iota, comp_done[-1], g.get(("oh", k))])
            oh_toks.append(p.ve.emit(lambda e, k=k: e.tensor_scalar(
                self.oh[k][:], self.iota_f[:], self.musig[:, 32 + k:33 + k],
                self.musig[:, 16 + k:17 + k], op0=ALU.is_equal, op1=ALU.mult)))
        ln3T_tok = [None] * 16
        ev_ps = g.get(("psb",))
        ln3T_extra = list(gtok) + [g.get(("rk",)), tm]
        for tt in range(16):
            trs = []
            for c in range(NKD):
                p.te.waits([g[("lnF3", c, tt // 4)], self.tok_eye, ev_ps])
                trs.append(p.te.emit(lambda e, tt=tt, c=c: e.transpose(
                    self.psb[:, c * 128:(c + 1) * 128],
                    self.lnF[c][:, tt * 128:(tt + 1) * 128], self.eye_bf[:])))
            p.se.waits(trs + [g.get(("ln3T", tt))] + ln3T_extra + bar)
            ln3T_tok[tt] = p.se.emit(lambda e, tt=tt: e.activation(
                self.ln3T[tt][:], self.psb[:], AF.Copy))
            ev_ps = ln3T_tok[tt]
        g[("psb",)] = ev_ps
        selT_tok = [None] * 4
        for rt in range(4):
            for half in range(2):
                bank = (rt * 2 + half) % 5
                p.te.waits([g.get(("bank", bank))])
                last = None
                for k in range(16):
                    p.te.waits([oh_toks[k], ln3T_tok[k]])
                    last = p.te.emit(lambda e, rt=rt, half=half, k=k, bank=bank: e.matmul(
                        self.bankap(bank),
                        self.oh[k][:, rt * 128:(rt + 1) * 128],
                        self.ln3T[k][:, half * TCW:(half + 1) * TCW],
                        start=(k == 0), stop=(k == 15)), inc=(k == 15))
                dst = self.selT[rt][:, half * TCW:(half + 1) * TCW]
                p.se.waits([last] + bar)
                tev = p.se.emit(lambda e, dst=dst, bank=bank: e.activation(
                    dst, self.bankap(bank), AF.Copy))
                g[("bank", bank)] = tev
                selT_tok[rt] = tev
        sel_tok = [None] * NKD
        ev_ps = g.get(("psb",))
        for ct in range(NKD):
            trs = []
            for rt in range(4):
                p.te.waits([selT_tok[rt], self.tok_eye, ev_ps, g.get(("rk2",))])
                trs.append(p.te.emit(lambda e, ct=ct, rt=rt: e.transpose(
                    self.psb[:, rt * 128:(rt + 1) * 128],
                    self.selT[rt][:, ct * 128:(ct + 1) * 128], self.eye_bf[:])))
            p.se.waits(trs + [cmp_toks[ct], g.get(("selc", ct))] + bar)
            sel_tok[ct] = p.se.emit(lambda e, ct=ct: e.activation(
                self.cmp[ct][:], self.psb[:, 0:TCW], AF.Copy))
            ev_ps = sel_tok[ct]
        g[("psb",)] = ev_ps
        qkv_toks2 = qkv_project([self.cmp[k][:] for k in range(NKD)], sel_tok, TCW,
                                self.qkv, wtg_guard=selT_tok[3])

        def sel_out(dv):
            return self.sel_g[:, dv * TCW:(dv + 1) * TCW], None
        sel_done = attend(4, TCW, [self.qkv[0][k] for k in range(NKD)],
                          [self.qkv[1][k] for k in range(NKD)],
                          [self.qkv[2][k] for k in range(NKD)], qkv_toks2, sel_out,
                          self.gn[1][:, 0:TCW], gtok[1])

        self.dump_bf(self.sel_g[:], sel_done, 7264, 8 * TCW)
        self.dump(self.scb[:, 0:1536], [g[("scb", tc)] for tc in range(NTC)] + [t_sT], 0, 1536)
        woc = self.scb[:, 0:1536]
        woc_bf = self.nc.alloc_sbuf_tensor_at("woc_bf", [128, 3 * D], BF16,
                                              offset=self.nc.lookup_mloc(self.scb).addr - 0 + 0)             if False else None
        prev_st = None
        rank_done = g.get(("rk",))
        for ot in range(NKD):
            p.qg.waits([prev_st, rank_done, t_slot] + bar)
            tl = p.qg.emit(lambda e, ot=ot: e.dma_start(
                out=self.wocast[:], in_=self.d_b_Wo[ot, :, :]), ch="woL")
            p.qs.wait(tl)
            prev_st = p.qs.emit(lambda e, ot=ot: e.dma_start(
                out=self.d_wo_bf[ot, :, :], in_=self.wocast[:]), ch="woS")
        self.tok_wo_cache = prev_st
        for Q in range(4):
            wq_tiles = self.wq if Q % 2 == 0 else self.wqB
            start = 256 * Q
            src_tiles = [self.lnF[k][:, start:start + 384] for k in range(NKD)]
            tc_a, tc_b = start // TCW, (start + 383) // TCW
            src_toks = [[g[("lnF3", k, tc_a)], g[("lnF3", k, tc_b)]] for k in range(NKD)]
            wg_guard = g.get(("wqg", Q % 2))
            if Q == 0:
                wg_guard = sel_done[-1]
            elif Q == 1:
                wg_guard = self.tok_wo_cache
            qkv_toksw = qkv_project(src_tiles, src_toks, 384, wq_tiles, wtg_guard=wg_guard)
            for wloc in range(2):
                wg = 2 * Q + wloc
                off = 128 * wloc
                qw = [wq_tiles[0][k][:, off:off + 256] for k in range(NKD)]
                kw_ = [wq_tiles[1][k][:, off:off + 256] for k in range(NKD)]
                vw = [wq_tiles[2][k][:, off:off + 256] for k in range(NKD)]

                def win_out(dv, wg=wg):
                    return self.win_g[dv][:, wg * 256:(wg + 1) * 256], selT_tok[3]
                wdone = attend(2, 256, qw, kw_, vw, qkv_toksw, win_out,
                               self.gn[2][:, wg * 256:(wg + 1) * 256], gtok[2])
                g[("wqg", Q % 2)] = wdone[-1]
                for dv in range(NKD):
                    g[("win", dv, wg)] = wdone[dv]

        self.dump_bf(self.win_g[0][:], [g[("win", 0, wg)] for wg in range(NWIN)], 11360, T)
        wo_user = None
        wo_tok = {}

        def wo_issue(idx):
            buf = idx % 2
            p.qs.waits([wo_user, g.get(("wqg", 0)), g.get(("wqg", 1)), self.tok_wo_cache] + bar)
            wo_tok[idx] = [p.qs.emit(lambda e, idx=idx, buf=buf: e.dma_start(
                out=self.wbuf[:, buf * 4096:buf * 4096 + 3 * D], in_=self.d_wo_bf[idx % NKD, :, :]),
                ch="woB0" if buf == 0 else "woB1")]

        self.tok_x3 = [None] * NKD
        self.tok_x3_hbm = [[None] * NTC for _ in range(NKD)]
        comp_all = comp_done[-1]
        sel_all = sel_done[-1]
        win_all = [g[("win", dv, wg)] for dv in range(NKD) for wg in range(NWIN)]
        wo_issue(0)
        wo_issue(1)
        widx = 0
        for tc in range(NTC):
            c0 = tc * TCW
            xbuf = self.x2c if tc % 2 == 0 else self.x2cB
            xch = "x2" if tc % 2 == 0 else "x2b"
            ld_toks = []
            for k in range(NKD):
                xs = self.d_xres_b[k, :, c0:c0 + TCW]
                dst = xbuf[:, k * TCW:(k + 1) * TCW]
                extra0 = [g.get(("rk",)), oh_toks[-1], sel_done[-1], win_all[-1]] + win_all if tc <= 1 else []
                p.qg.waits(bar + [g.get(("x2cc", tc))] + extra0)
                ld_toks.append(p.qg.emit(lambda e, xs=xs, dst=dst: e.dma_start(out=dst, in_=xs), ch=xch))
            ld_all = p.qg.tok(xch)
            z4_toks = []
            POOL5 = [0, 1, 2, 5, 6]
            for ot in range(NKD):
                buf = widx % 2
                bank = POOL5[widx % 5]
                p.te.waits(wo_tok[widx] + [g.get(("bank", bank))])
                last = None
                klist = list(range(24)) if tc == 0 else list(range(16, 24))
                for i2, kk in enumerate(klist):
                    if kk < 8:
                        rhs = self.comp_g[:, kk * TCW:(kk + 1) * TCW]
                        p.te.wait(comp_all)
                    elif kk < 16:
                        rhs = self.sel_g[:, (kk - 8) * TCW:(kk - 7) * TCW]
                        p.te.wait(sel_all)
                    else:
                        rhs = self.win_g[kk - 16][:, c0:c0 + TCW]
                        p.te.waits(win_all)
                    st_f = (i2 == 0)
                    sp_f = (i2 == len(klist) - 1)
                    last = p.te.emit(lambda e, kk=kk, bank=bank, buf=buf, rhs=rhs, st_f=st_f, sp_f=sp_f: e.matmul(
                        self.bankap(bank), self.wbuf[:, buf * 4096 + kk * 128:buf * 4096 + (kk + 1) * 128],
                        rhs, start=st_f, stop=sp_f), inc=sp_f)
                wo_user = last
                widx += 1
                if widx + 2 <= NTC * NKD:
                    wo_issue(widx + 1)
                m1 = self.tmp[:, (2 + ot % 2) * TCW:(3 + ot % 2) * TCW]
                p.se.waits([last, self.tok_small, g.get(("m1", ot % 2))] + bar)
                t_m1 = p.se.emit(lambda e, m1=m1, bank=bank, ot=ot: e.activation(
                    m1, self.bankap(bank), AF.Identity, bias=self.sm("b_bo", ot, 1)))
                g[("bank", bank)] = t_m1
                z4d = self.z4[:, ot * TCW:(ot + 1) * TCW]
                lnsl = self.lnF[ot][:, c0:c0 + TCW]
                p.ve.waits([t_m1, g.get(("z4r", ot))] + bar)
                t_z4 = p.ve.emit(lambda e, z4d=z4d, m1=m1, lnsl=lnsl: e.tensor_tensor(
                    z4d, m1, lnsl, op=ALU.add))
                g[("m1", ot % 2)] = t_z4
                z4_toks.append((z4d, t_z4))
            tA = tB = None
            for e_i, (z4d, t_z4) in enumerate(z4_toks):
                zb = self.stg[:, e_i * TCW:(e_i + 1) * TCW]
                zq = self.stgB[:, (e_i % 2) * TCW:(e_i % 2 + 1) * TCW]
                p.qg.waits([t_z4, g.get(("zbslot", e_i))], )
                t_zb = p.qg.emit(lambda e, zb=zb, z4d=z4d: e.tensor_copy(zb, z4d), ch="gpc")
                p.qg.waits([g.get(("zqslot", e_i % 2))])
                t_sq = p.qg.emit(lambda e, zq=zq, z4d=z4d: e.tensor_tensor(zq, z4d, z4d, op=ALU.mult), ch="gpc")
                p.te.waits([t_zb, self.tok_ones])
                if e_i == 0:
                    p.te.waits(self.rdbk.get(3, []) + self.rdbk.get(4, []) + [g.get(("bank", 3)), g.get(("bank", 4))])
                tA = p.te.emit(lambda e, zb=zb, e_i=e_i: e.matmul(
                    self.bankap(3), self.ones_bf[:], zb, start=(e_i == 0), stop=(e_i == NKD - 1)))
                p.te.wait(t_sq)
                tB = p.te.emit(lambda e, zq=zq, e_i=e_i: e.matmul(
                    self.bankap(4), self.ones_bf[:], zq, start=(e_i == 0), stop=(e_i == NKD - 1)))
                g[("zbslot", e_i)] = tA
                g[("zqslot", e_i % 2)] = tB
            self.stg_guard = tB
            self.stgB_guard = tB
            tok_mu4, tok_r4 = self.ln_finalize(tA, tB, LN4_EPS, 3, 4)
            o_ap = self.tmp[:, 6 * TCW:7 * TCW]
            for ot in range(NKD):
                z4d = self.z4[:, ot * TCW:(ot + 1) * TCW]
                x_sl = xbuf[:, ot * TCW:(ot + 1) * TCW]
                p.ve.waits([tok_mu4, tok_r4, ld_all])
                tz1 = p.ve.emit(lambda e, z4d=z4d, o_ap=o_ap: e.tensor_tensor(
                    o_ap, z4d, self.musig[:, 0:TCW], op=ALU.subtract))
                p.ve.wait(tz1)
                tz2 = p.ve.emit(lambda e, o_ap=o_ap: e.tensor_tensor(
                    o_ap, o_ap, self.musig[:, TCW:2 * TCW], op=ALU.mult))
                p.ve.wait(tz2)
                tx = p.ve.emit(lambda e, x_sl=x_sl, o_ap=o_ap: e.tensor_tensor(
                    x_sl, x_sl, o_ap, op=ALU.add))
                g[("z4r", ot)] = tx
                self.tok_x3[ot] = tx
                if is_final:
                    xr = self.d_out[ot * 128:(ot + 1) * 128, c0:c0 + TCW]
                    chn = "out"
                else:
                    xr = self.d_xres_a[ot, :, c0:c0 + TCW]
                    chn = "x"
                p.qs.wait(tx)
                self.tok_x3_hbm[ot][tc] = p.qs.emit(lambda e, xr=xr, x_sl=x_sl: e.dma_start(out=xr, in_=x_sl), ch=chn)
            g[("sq",)] = tx
            self.tmp6_guard = tx
            g[("x2cc", tc + 2)] = self.tok_x3_hbm[NKD - 1][tc]
        return self.tok_x3


    def finish_hbm_out(self, hbm_toks):
        p = self.p
        for k in range(NKD):
            p.qs.waits(hbm_toks[k])
        for key, v in list(p.qs.n.items()) + list(p.qg.n.items()):
            if key.endswith(".out") or key.endswith(".dbg"):
                p.qs.ops.append(lambda e, sems, key=key, v=v: e.wait_ge(sems[key], v))

    def finish_output(self, src_tiles, src_toks):
        p = self.p
        for k in range(NKD):
            p.qs.wait(src_toks[k])
            p.qs.emit(lambda e, k=k: e.dma_start(
                out=self.d_out[k * 128:(k + 1) * 128, :], in_=src_tiles[k][:]), ch="out")
        for key, v in list(p.qs.n.items()) + list(p.qg.n.items()):
            if key.endswith(".out") or key.endswith(".dbg"):
                p.qs.ops.append(lambda e, sems, key=key, v=v: e.wait_ge(sems[key], v))

    def run_block(self):
        nc = self.nc
        p = self.p
        from contextlib import ExitStack
        chans = set()
        for s in p.streams.values():
            chans.update(s.n.keys())
            chans.add(s._ch(None))
        with ExitStack() as es:
            sems = {nm: es.enter_context(nc.semaphore(nm.replace(".", "_"))) for nm in sorted(chans)}
            block = es.enter_context(nc.Block())

            def runner(stream):
                def f(e):
                    for op in stream.ops:
                        op(e, sems)
                return f
            block.tensor(runner(p.te))
            block.vector(runner(p.ve))
            block.scalar(runner(p.se))
            block.gpsimd(runner(p.qg))
            block.sync(runner(p.qs))


def build():
    nc = bass.Bass("TRN2", target_bir_lowering=False)
    pr = Prog(nc)
    pr.setup()
    pr.attn_a()
    cur_toks = [pr.tok_x1[k][NTC - 1] for k in range(NKD)]
    if PHASES >= 2:
        is_final = (PHASES == 2)
        cur_toks, hbm_toks = pr.ffn(cur_toks, pr.d_xres_a, pr.d_xres_b, is_final=is_final)
        if is_final:
            pr.finish_hbm_out(hbm_toks)
            pr.run_block()
            return nc
    if PHASES >= 3:
        x2_tok = pr.p.qs.tok("out")
        cur_toks = pr.attn_b(x2_tok, is_final=(PHASES == 3))
        if PHASES == 3:
            pr.finish_hbm_out(pr.tok_x3_hbm)
            pr.run_block()
            return nc
    if PHASES >= 4:
        cur_toks, hbm_toks = pr.ffn(None, pr.d_xres_a, pr.d_xres_b, is_final=True, x_from_hbm=True)
        pr.finish_hbm_out(hbm_toks)
    pr.run_block()
    return nc


_NC_CACHE = None

def kernel(**inputs):
    global _NC_CACHE
    P = host_prep(inputs)
    x_full = np.asarray(inputs["L0_attn_a_x"], dtype=np.float32)
    if _NC_CACHE is None:
        _NC_CACHE = build()
    nc = _NC_CACHE
    in_maps = []
    for b in range(8):
        m = {"x_in": np.ascontiguousarray(x_full[b].T), "small": P["small"]}
        for nm in ["a_Wi", "a_Wf", "a_Wo", "a_Ws", "f_w1", "f_w2", "f_h1", "f_h2", "f_w3",
                   "b_Wq", "b_Wk", "b_Wv", "b_Wo", "b_Wc"]:
            m[nm] = P[nm]
        in_maps.append(m)
    res = run_bass_kernel_spmd(nc, in_maps, core_ids=list(range(8)))
    return np.stack([res.results[b]["x_out"].T for b in range(8)])


# version 07
# speedup vs baseline: 1.2183x; 1.0809x over previous
import os
import numpy as np
import concourse.bass as bass
import concourse.mybir as mybir
from concourse.bass_utils import run_bass_kernel_spmd

F32 = mybir.dt.float32
BF16 = mybir.dt.bfloat16
AF = mybir.ActivationFunctionType
ALU = mybir.AluOpType
AX = mybir.AxisListType.X

D = 1024
T = 2048
NKD = 8
NTC = 4
TCW = 512
DH = 4096
NKH = 32
LN_EPS = 1e-5
LN4_EPS = 1e-6
ATTN_SCALE = 0.125
NWIN = 8
KB = 1024

PHASES = int(os.environ.get("MK_PHASES", "4"))
SELFWAIT = bool(int(os.environ.get("MK_SELFWAIT", "0")))


ALL_CHANNELS = set()
CHAN_N = {}


class Stream:

    def __init__(self, name, step):
        self.name = name
        self.step = step
        self.n = {}
        self.ops = []
        self.wm = {}

    def _ch(self, ch):
        key = self.name if ch is None else f"{self.name}.{ch}"
        ALL_CHANNELS.add(key)
        return key

    def wait(self, tok):
        if tok is None:
            return
        nm, val = tok
        if val <= 0:
            return
        if "." in nm:
            val = max(val, CHAN_N.get(nm, val))
        if self.wm.get(nm, 0) >= val:
            return
        self.wm[nm] = val
        self.ops.append(lambda e, sems, nm=nm, val=val: e.wait_ge(sems[nm], val))

    def waits(self, toks):
        best = {}
        for t in toks:
            if t is None:
                continue
            nm, val = t
            if nm not in best or val > best[nm]:
                best[nm] = val
        for nm, val in best.items():
            self.wait((nm, val))

    def emit(self, fn, inc=True, ch=None):
        key = self._ch(ch)
        if SELFWAIT and self.name in ("ve", "se") and ch is None:
            self.wait(self.tok())
        if inc:
            self.n[key] = self.n.get(key, 0) + self.step
            CHAN_N[key] = self.n[key]
            n = self.n[key]
            self.ops.append(lambda e, sems, fn=fn, key=key: fn(e).then_inc(sems[key], self.step))
            return (key, n)
        self.ops.append(lambda e, sems, fn=fn: fn(e))
        return None

    def tok(self, ch=None):
        key = self._ch(ch)
        return (key, self.n.get(key, 0))

    def bar(self):
        return [(k, v) for k, v in self.n.items()]


class Plan:
    def __init__(self):
        CHAN_N.clear()
        self.te = Stream("te", 1)
        self.ve = Stream("ve", 1)
        self.se = Stream("se", 1)
        self.qg = Stream("qg", 16)
        self.qs = Stream("qs", 16)
        self.streams = dict(te=self.te, ve=self.ve, se=self.se, qg=self.qg, qs=self.qs)

    def barrier(self):
        out = []
        for s in self.streams.values():
            out.extend(s.bar())
        return out


def prep_lhsT(w_in_out):
    K, M = w_in_out.shape
    nk, no = K // 128, M // 128
    r = w_in_out.reshape(nk, 128, no, 128).transpose(2, 1, 0, 3)
    return np.ascontiguousarray(r.reshape(no, 128, nk * 128))


def vec_oT(v):
    return np.ascontiguousarray(v.reshape(-1, 128).T)


SM = {}
_sm_off = 0
def _sm(name, w):
    global _sm_off
    SM[name] = (_sm_off, w)
    _sm_off += w
for _nm, _w in [
    ("a_bi", 8), ("a_bf", 8), ("a_bo", 8), ("a_bs", 8), ("a_gamma", 8), ("a_beta", 8),
    ("f_s1", 32), ("f_s2", 32), ("f_h1s", 32), ("f_h2s", 32), ("f_s3", 8),
    ("b_bq", 8), ("b_bk", 8), ("b_bv", 8), ("b_bo", 8), ("b_bc", 8),
    ("b_bg", 3), ("b_bs", 1), ("b_wg", 24), ("b_ws", 8),
]:
    _sm(_nm, _w)
SMALL_W = _sm_off
_sm("a_hst", 8)
_sm("b_bo_half", 8)
_sm("pglob", 16)
_sm("scr", 48)
SMALL_W_ALL = _sm_off


def host_prep(inputs):
    g = lambda k: np.asarray(inputs[k], dtype=np.float32)
    P = {}
    for nm in ["Wi", "Wf", "Wo", "Ws"]:
        P["a_" + nm] = prep_lhsT(g("L0_attn_a_" + nm))
    P["f_w1"] = prep_lhsT(g("L0_ffn_w1_w").T)
    P["f_w2"] = prep_lhsT(g("L0_ffn_w2_w").T)
    P["f_h1"] = prep_lhsT(g("L0_ffn_h1_w").T)
    P["f_h2"] = prep_lhsT(g("L0_ffn_h2_w").T)
    P["f_w3"] = prep_lhsT(g("L0_ffn_w3_w").T)
    for nm in ["Wq", "Wk", "Wv", "Wo", "Wc"]:
        P["b_" + nm] = prep_lhsT(g("L0_attn_b_" + nm))
    sp = np.zeros((128, SMALL_W), np.float32)
    def put(name, arr):
        o, w = SM[name]
        sp[:, o:o + w] = arr
    put("a_bi", vec_oT(g("L0_attn_a_bi")));  put("a_bf", vec_oT(g("L0_attn_a_bf")))
    put("a_bo", vec_oT(g("L0_attn_a_bo")));  put("a_bs", vec_oT(g("L0_attn_a_bs")))
    put("a_gamma", vec_oT(g("L0_attn_a_gamma"))); put("a_beta", vec_oT(g("L0_attn_a_beta")))
    put("f_s1", vec_oT(g("L0_ffn_w1_s")));   put("f_s2", vec_oT(g("L0_ffn_w2_s")))
    put("f_h1s", vec_oT(g("L0_ffn_h1_s")));  put("f_h2s", vec_oT(g("L0_ffn_h2_s")))
    put("f_s3", vec_oT(g("L0_ffn_w3_s")))
    put("b_bq", vec_oT(g("L0_attn_b_bq")));  put("b_bk", vec_oT(g("L0_attn_b_bk")))
    put("b_bv", vec_oT(g("L0_attn_b_bv")));  put("b_bo", vec_oT(g("L0_attn_b_bo")))
    put("b_bc", vec_oT(g("L0_attn_b_bc")))
    put("b_bg", np.broadcast_to(g("L0_attn_b_bg"), (128, 3)))
    put("b_bs", np.broadcast_to(g("L0_attn_b_bs"), (128, 1)))
    put("b_wg", g("L0_attn_b_Wg").reshape(8, 128, 3).transpose(1, 0, 2).reshape(128, 24))
    put("b_ws", g("L0_attn_b_Ws").reshape(8, 128).T)
    P["small"] = np.ascontiguousarray(sp)
    return P


class Prog:
    def __init__(self, nc, dbg=0):
        self.nc = nc
        self.dbg = dbg
        self.p = Plan()
        self.rd56 = [None, None]
        self.rdbk = {}
        self.guard = {}
        self.alloc_dram()
        self.alloc_sbuf()

    def alloc_dram(self):
        nc = self.nc
        self.d_x = nc.dram_tensor("x_in", [D, T], F32, kind="ExternalInput").ap()
        self.d_small = nc.dram_tensor("small", [128, SMALL_W], F32, kind="ExternalInput").ap()
        for nm, sh in [
            ("a_Wi", [NKD, 128, D]), ("a_Wf", [NKD, 128, D]), ("a_Wo", [NKD, 128, D]), ("a_Ws", [NKD, 128, D]),
            ("f_w1", [NKH, 128, D]), ("f_w2", [NKH, 128, D]),
            ("f_h1", [NKH, 128, DH]), ("f_h2", [NKH, 128, DH]),
            ("f_w3", [NKD, 128, DH]),
            ("b_Wq", [NKD, 128, D]), ("b_Wk", [NKD, 128, D]), ("b_Wv", [NKD, 128, D]),
            ("b_Wo", [NKD, 128, 3 * D]), ("b_Wc", [NKD, 128, DH]),
        ]:
            setattr(self, "d_" + nm, nc.dram_tensor(nm, sh, F32, kind="ExternalInput").ap())
        self.d_out = nc.dram_tensor("x_out", [D, T], F32, kind="ExternalOutput").ap()
        self.dbg_on = bool(int(os.environ.get("MK_DBG", "0")))
        if self.dbg_on:
            self.d_dbg = nc.dram_tensor("dbg", [128, 16384], F32, kind="ExternalOutput").ap()
        self.d_hbuf_a = nc.dram_tensor("hbuf_a", [NKH, 128, T], BF16, kind="Internal").ap()
        self.d_hbuf_b = nc.dram_tensor("hbuf_b", [NKH, 128, T], BF16, kind="Internal").ap()
        self.d_xres_a = nc.dram_tensor("xres_a", [NKD, 128, T], F32, kind="Internal").ap()
        self.d_wo_bf = nc.dram_tensor("wo_bf", [NKD, 128, 3 * D], BF16, kind="Internal").ap()
        self.d_qkv_bf = nc.dram_tensor("qkv_bf", [3, NKD, 128, D], BF16, kind="Internal").ap()
        self.d_xres_b = nc.dram_tensor("xres_b", [NKD, 128, T], F32, kind="Internal").ap()

    def alloc_sbuf(self):
        nc = self.nc
        base = (nc.sbuf_base + 31) & ~31
        top = nc.sbuf_top

        def at(name, shape, dtype, off_b):
            nbytes = shape[1] * {F32: 4, BF16: 2}[dtype]
            assert base + off_b + nbytes <= top, f"{name} overflows sbuf: {off_b}+{nbytes} vs {top-base}"
            return nc.alloc_sbuf_tensor_at(name, shape, dtype, offset=base + off_b)

        assert SMALL_W_ALL * 4 <= 1536
        self.wbuf = at("wbuf", [128, 8192], BF16, 0)
        self.stg = at("stg", [128, 4096], BF16, 16 * KB)
        self.small = at("small", [128, SMALL_W_ALL], F32, 24 * KB)
        self.musig = at("musig", [128, 1024], F32, 25 * KB + 512)
        self.tmp = at("tmp", [128, 3584], F32, 29 * KB + 512)
        self.iota_f = at("iota_f", [128, 512], F32, 43 * KB + 512)
        self.eye_bf = at("eye_bf", [128, 128], BF16, 45 * KB + 512)
        self.ones_bf = at("ones_bf", [128, 128], BF16, 45 * KB + 768)
        self.ones_f1 = at("ones_f1", [1, 128], F32, 46 * KB)
        self.tri_f = at("tri_f", [128, 128], F32, 46 * KB + 512)
        self.eye_f = at("eye_f", [128, 128], F32, 47 * KB)
        self.ones_f32_col = at("ones_f32_col", [128, 1], F32, 25 * KB + 352)
        self.ones_row = at("ones_row", [1, 16], F32, 25 * KB + 384)
        assert SMALL_W_ALL * 4 <= 1376
        AR = 47 * KB + 512
        self.x = [at(f"x{i}", [128, T], F32, AR + i * 8 * KB) for i in range(NKD)]
        self.ln_c_bf = at("ln_c_bf", [128, 8 * TCW], BF16, AR + 64 * KB)
        self.ln_c_f = at("ln_c_f", [128, 8 * TCW], F32, AR + 72 * KB)
        self.z_c = at("z_c", [128, 8 * TCW], F32, AR + 88 * KB)
        self.hA = [at(f"hA{k}", [128, T], BF16, AR + k * 4 * KB) for k in range(NKH)]
        self.lnF = [at(f"lnF{k}", [128, T], BF16, AR + 128 * KB + k * 4 * KB) for k in range(NKD)]
        AB = AR
        self.gn = [at(f"gn{b}", [128, T], F32, AB + b * 8 * KB) for b in range(3)]
        self.z4 = at("z4", [128, 8 * TCW], F32, AB + 0 * KB)
        self.comp_g = at("comp_g", [128, 8 * TCW], BF16, AB + 24 * KB)
        self.srow = at("srow", [1, T], F32, AB + 32 * KB)
        self.sel_g = at("sel_g", [128, 8 * TCW], BF16, AB + 32 * KB)
        self.rgsum = at("rgsum", [128, T], F32, AB + 40 * KB)
        self.ln3T = [at(f"ln3T{k}", [128, D], BF16, AB + 40 * KB + k * 2 * KB) for k in range(16)]
        self.win_g = [at(f"win_g{k}", [128, T], BF16, AB + 40 * KB + k * 4 * KB) for k in range(NKD)]
        self.x2c = at("x2c", [128, 8 * TCW], F32, AB + 72 * KB)
        self.cmp = [at(f"cmp{k}", [128, TCW], BF16, AB + 72 * KB + k * KB) for k in range(NKD)]
        self.wq = []
        for pj in range(3):
            self.wq.append([at(f"wq{pj}_{k}", [128, 384], BF16, AB + 72 * KB + (pj * 8 + k) * 768)
                            for k in range(NKD)])
        _wqb_off = [AB + i * 768 for i in range(21)] + [AB + 120 * KB + i * 768 for i in range(3)]
        self.wqB = []
        for pj in range(3):
            self.wqB.append([at(f"wqB{pj}_{k}", [128, 384], BF16, _wqb_off[pj * 8 + k])
                             for k in range(NKD)])
        self.qkv = []
        for pj in range(3):
            self.qkv.append([at(f"qkv{pj}_{k}", [128, TCW], BF16, AB + 80 * KB + pj * 8 * KB + k * KB)
                             for k in range(NKD)])
        self.rkA = at("rkA", [128, T], F32, AB + 48 * KB)
        self.x2cB = at("x2cB", [128, 8 * TCW], F32, AB + 88 * KB)
        self.stgB = at("stgB", [128, 1024], BF16, AB + 104 * KB)
        self.stgC = at("stgC", [128, 1024], BF16, AB + 106 * KB)
        self.oh = [at(f"oh{k}", [128, TCW], BF16, AB + 80 * KB + k * KB) for k in range(16)]
        self.selT = [at(f"selT{j}", [128, D], BF16, AB + 104 * KB + j * 2 * KB) for j in range(4)]
        self.vT = [at(f"vT{j}", [128, D], BF16, AB + 104 * KB + j * 2 * KB) for j in range(4)]
        self.pP = [at(f"pP{j}", [128, TCW], BF16, AB + 112 * KB + j * KB) for j in range(4)]
        self.pTt = [at(f"pT{j}", [128, TCW], BF16, AB + 116 * KB + j * KB) for j in range(4)]
        self.scb = at("scb", [128, T], F32, AB + 120 * KB)
        self.wocast = at("wocast", [128, 3 * D], BF16, AB + 120 * KB)
        self.psf = nc.alloc_psum_tensor("psf", [128, 3584], F32)
        self.psb = nc.alloc_psum_tensor("psb", [128, 1024], BF16)

    def bankap(self, i, w=TCW, off=0):
        return self.psf[:, i * TCW + off:i * TCW + off + w]

    def bank_wait(self, bank):
        return [self.guard.get(("bank", bank))] + self.rdbk.get(bank, [])

    def dump(self, ap, deps, col, ncols):
        if not self.dbg_on:
            return None
        self.p.qs.waits(deps)
        return self.p.qs.emit(lambda e, ap=ap, col=col, ncols=ncols: e.dma_start(
            out=self.d_dbg[:, col:col + ncols], in_=ap), ch="dbg")

    def dump_bf(self, ap, deps, col, ncols):
        if not self.dbg_on:
            return None
        self.p.qg.waits(deps)
        return self.p.qg.emit(lambda e, ap=ap, col=col, ncols=ncols: e.dma_start(
            out=self.d_dbg[:, col:col + ncols], in_=ap), ch="dbg")

    def sm(self, name, i=0, w=1):
        o, wd = SM[name]
        assert i + w <= wd
        return self.small[:, o + i:o + i + w]

    def setup(self):
        p = self.p
        for k in range(NKD):
            p.qs.emit(lambda e, k=k: e.dma_start(
                out=self.x[k][:], in_=self.d_x[k * 128:(k + 1) * 128, :]), ch="ld")
        p.qs.emit(lambda e: e.dma_start(out=self.small[:, 0:SMALL_W], in_=self.d_small[:, :]), ch="ld")
        ld_all = p.qs.tok("ld")
        self.tok_x = [ld_all] * NKD
        self.tok_small = ld_all
        t_io = p.qg.emit(lambda e: e.iota(self.iota_f[:], pattern=[[1, 512]], base=0,
                                          channel_multiplier=0, allow_small_or_imprecise_dtypes=True), ch="misc")
        self.tok_pglob = p.qg.emit(lambda e: e.iota(self.sm("pglob", 0, 16), pattern=[[128, 16]], base=0,
                                                    channel_multiplier=1, allow_small_or_imprecise_dtypes=True), ch="misc")
        t_pi = p.qg.emit(lambda e: e.iota(self.sm("scr", 0, 1), pattern=[[0, 1]], base=0,
                                          channel_multiplier=1, allow_small_or_imprecise_dtypes=True), ch="misc")
        self.tok_iota = t_io
        p.ve.waits([t_io, t_pi])
        t_eye_f = p.ve.emit(lambda e: e.tensor_scalar(self.tmp[:, 0:128], self.iota_f[:, 0:128],
                                                      self.sm("scr", 0, 1), None, op0=ALU.is_equal))
        p.ve.wait(t_eye_f)
        self.tok_eye = p.ve.emit(lambda e: e.tensor_copy(self.eye_bf[:], self.tmp[:, 0:128]))
        self.tok_eye_f = p.ve.emit(lambda e: e.tensor_copy(self.eye_f[:], self.tmp[:, 0:128]))
        self.tok_tri = p.ve.emit(lambda e: e.tensor_scalar(self.tri_f[:], self.iota_f[:, 0:128],
                                                           self.sm("scr", 0, 1), None, op0=ALU.is_gt))
        self.tok_ones = p.ve.emit(lambda e: e.memset(self.ones_bf[:], 1.0))
        self.tok_ones1 = p.ve.emit(lambda e: e.memset(self.ones_f1[:], 1.0))
        self.tok_ones_col = p.ve.emit(lambda e: e.memset(self.ones_f32_col[:], 1.0))
        self.tok_ones_row = p.ve.emit(lambda e: e.memset(self.ones_row[:], 1.0))
        p.ve.wait(self.tok_small)
        self.tok_bo_half = p.ve.emit(lambda e: e.tensor_scalar(self.sm("b_bo_half", 0, 8),
                                                               self.sm("b_bo", 0, 8), 0.5, None, op0=ALU.mult))
        self.tok_eps5 = p.ve.emit(lambda e: e.memset(self.sm("scr", 1, 1), LN_EPS))
        self.tok_eps6 = p.ve.emit(lambda e: e.memset(self.sm("scr", 2, 1), LN4_EPS))

    def ln_stats_chunk(self, tiles, eps, bA=5, bB=6):
        p = self.p
        tok_mmA = tok_mmB = None
        for e_i, (ap, tk) in enumerate(tiles):
            sl = self.stgB[:, (e_i % 4) * 256:(e_i % 4) * 256 + 256] if False else None
            sl = self.stgB[:, (e_i % 2) * TCW:(e_i % 2 + 1) * TCW]
            sq = self.stgC[:, (e_i % 2) * TCW:(e_i % 2 + 1) * TCW]
            p.se.waits([tk, self.guard.get(("stgBsl", e_i % 2))])
            t_c = p.se.emit(lambda e, ap=ap, sl=sl: e.activation(sl, ap, AF.Copy))
            p.qg.waits([tk, self.guard.get(("stgCsl", e_i % 2))])
            t_q = p.qg.emit(lambda e, ap=ap, sq=sq: e.tensor_tensor(sq, ap, ap, op=ALU.mult), ch="gpc")
            p.te.waits([t_c, self.tok_ones])
            if e_i == 0:
                p.te.waits(self.rdbk.get(bA, []) + self.rdbk.get(bB, [])
                           + [self.guard.get(("bank", bA)), self.guard.get(("bank", bB))])
            tok_mmA = p.te.emit(lambda e, sl=sl, e_i=e_i, bA=bA: e.matmul(
                self.bankap(bA), self.ones_bf[:], sl, start=(e_i == 0), stop=(e_i == NKD - 1)))
            self.guard[("stgBsl", e_i % 2)] = tok_mmA
            p.te.wait(t_q)
            tok_mmB = p.te.emit(lambda e, sq=sq, e_i=e_i, bB=bB: e.matmul(
                self.bankap(bB), self.ones_bf[:], sq, start=(e_i == 0), stop=(e_i == NKD - 1)))
            self.guard[("stgCsl", e_i % 2)] = tok_mmB
        self.stgB_guard = tok_mmB
        return self.ln_finalize(tok_mmA, tok_mmB, eps, bA, bB)

    def ln_finalize(self, tok_mmA, tok_mmB, eps, bA=5, bB=6):
        p = self.p
        mu = self.musig[:, 0:TCW]
        r = self.musig[:, TCW:2 * TCW]
        t6 = self.tmp[:, 6 * TCW:7 * TCW]
        p.ve.wait(tok_mmA)
        tok_mu = p.ve.emit(lambda e, bA=bA: e.tensor_scalar(mu, self.bankap(bA), 1.0 / D, None, op0=ALU.mult))
        p.ve.wait(tok_mmB)
        t_v1 = p.ve.emit(lambda e, bB=bB: e.tensor_scalar(t6, self.bankap(bB), 1.0 / D, None, op0=ALU.mult))
        p.ve.wait(tok_mu)
        t_v2 = p.ve.emit(lambda e: e.tensor_tensor(r, mu, mu, op=ALU.mult))
        p.ve.waits([t_v1, t_v2])
        t_v3 = p.ve.emit(lambda e: e.tensor_tensor(t6, t6, r, op=ALU.subtract))
        eps_ap = self.sm("scr", 1 if eps == LN_EPS else 2, 1)
        p.se.waits([t_v3, self.tok_eps5, self.tok_eps6])
        t_l = p.se.emit(lambda e: e.activation(r, t6, AF.Ln, bias=eps_ap))
        p.se.wait(t_l)
        tok_r = p.se.emit(lambda e: e.activation(r, r, AF.Exp, scale=-0.5))
        self.rdbk[bA] = [tok_mu]
        self.rdbk[bB] = [t_v1]
        self.rd56 = [tok_mu, t_v1]
        return tok_mu, tok_r

    def attn_a(self):
        p = self.p
        W = [self.d_a_Wi, self.d_a_Wf, self.d_a_Wo, self.d_a_Ws]
        BIAS = ["a_bi", "a_bf", "a_bo", "a_bs"]
        FUNC = [AF.Sigmoid, AF.Sigmoid, AF.Sigmoid, AF.Tanh]
        wtok = {}
        wbuf_user = {}
        h_upd = [None] * NKD
        self.tok_x1 = [[None] * NTC for _ in range(NKD)]
        self.tok_x1_hbm = [[None] * NTC for _ in range(NKD)]
        last_proj_mm = None
        last_zread = None
        guard = self.guard

        def issue_w(gi):
            tc, e_i, buf = gi // NKD, gi % NKD, gi % 2
            toks = []
            for pj in range(4):
                p.qg.wait(wbuf_user.get((buf, pj)))
                toks.append(p.qg.emit(lambda e, pj=pj, e_i=e_i, buf=buf: e.dma_start(
                    out=self.wbuf[:, buf * 4096 + pj * D:buf * 4096 + (pj + 1) * D],
                    in_=W[pj][e_i, :, :]), ch="wA" if buf == 0 else "wB"))
            wtok[gi] = toks

        issue_w(0)
        issue_w(1)

        for tc in range(NTC):
            c0 = tc * TCW
            tiles = [(self.x[k][:, c0:c0 + TCW], self.tok_x[k]) for k in range(NKD)]
            tok_mu, tok_r = self.ln_stats_chunk(tiles, LN_EPS)
            if tc == 0:
                self.dump(self.x[0][:, 0:TCW], [self.tok_x[0]], 13312, TCW)
                self.dump(self.musig[:], [tok_mu, tok_r], 14336, 1024)
            ln_toks = []
            for k in range(NKD):
                lf = self.ln_c_f[:, k * TCW:(k + 1) * TCW]
                lb = self.ln_c_bf[:, k * TCW:(k + 1) * TCW]
                p.ve.waits([tok_mu, self.tok_x[k]])
                x_sl = self.x[k][:, c0:c0 + TCW]
                t1 = p.ve.emit(lambda e, lf=lf, x_sl=x_sl: e.tensor_tensor(
                    lf, x_sl, self.musig[:, 0:TCW], op=ALU.subtract))
                p.ve.waits([t1, tok_r])
                t2 = p.ve.emit(lambda e, lf=lf: e.tensor_tensor(lf, lf, self.musig[:, TCW:2 * TCW], op=ALU.mult))
                p.se.waits([t2, last_proj_mm])
                t3 = p.se.emit(lambda e, lb=lb, lf=lf: e.activation(lb, lf, AF.Copy))
                ln_toks.append((t2, t3))
                if tc == 0:
                    self.dump(lf, [t2], k * TCW, TCW)

            zb_toks = []
            for e_i in range(NKD):
                gi = tc * NKD + e_i
                buf = gi % 2
                mm_tok = [None] * 4
                for pj in range(4):
                    bank = (gi * 4 + pj) % 5
                    p.te.waits(wtok[gi])
                    p.te.waits(self.bank_wait(bank))
                    for k in range(NKD):
                        p.te.wait(ln_toks[k][1])
                        tk = p.te.emit(lambda e, pj=pj, k=k, bank=bank, buf=buf: e.matmul(
                            self.bankap(bank),
                            self.wbuf[:, buf * 4096 + pj * D + k * 128:buf * 4096 + pj * D + (k + 1) * 128],
                            self.ln_c_bf[:, k * TCW:(k + 1) * TCW],
                            start=(k == 0), stop=(k == NKD - 1)), inc=(k == NKD - 1))
                        if tk is not None:
                            mm_tok[pj] = tk
                    wbuf_user[(buf, pj)] = mm_tok[pj]
                last_proj_mm = mm_tok[3]
                if gi + 2 < NTC * NKD:
                    issue_w(gi + 2)
                slots = {}
                ev = {}
                for pj, slot in [(0, 0), (3, 1), (1, 2), (2, 3)]:
                    bank = (gi * 4 + pj) % 5
                    ap = self.tmp[:, slot * TCW:(slot + 1) * TCW]
                    p.se.waits([mm_tok[pj], guard.get(("tmp", slot))])
                    ev[pj] = p.se.emit(lambda e, ap=ap, pj=pj, bank=bank, e_i=e_i: e.activation(
                        ap, self.bankap(bank), FUNC[pj], bias=self.sm(BIAS[pj], e_i, 1)))
                    guard[("bank", bank)] = ev[pj]
                    slots[pj] = ap
                a_ap = self.tmp[:, 4 * TCW:5 * TCW]
                p.ve.waits([ev[0], ev[3], guard.get(("tmp", 4))])
                t_a = p.ve.emit(lambda e, a_ap=a_ap, s0=slots[0], s3=slots[3]: e.tensor_tensor(
                    a_ap, s0, s3, op=ALU.mult))
                guard[("tmp", 0)] = t_a
                guard[("tmp", 1)] = t_a
                h_ap = self.tmp[:, 5 * TCW:6 * TCW]
                p.ve.waits([ev[1], t_a, h_upd[e_i], guard.get(("tmp", 5))])
                if tc == 0:
                    t_h = p.ve.emit(lambda e, h_ap=h_ap, s1=slots[1], a_ap=a_ap: e.tensor_tensor_scan(
                        h_ap, s1, a_ap, 0.0, op0=ALU.mult, op1=ALU.add))
                else:
                    t_h = p.ve.emit(lambda e, h_ap=h_ap, s1=slots[1], a_ap=a_ap, e_i=e_i: e.tensor_tensor_scan(
                        h_ap, s1, a_ap, self.sm("a_hst", e_i, 1), op0=ALU.mult, op1=ALU.add))
                guard[("tmp", 2)] = t_h
                guard[("tmp", 4)] = t_h
                p.ve.wait(t_h)
                h_upd[e_i] = p.ve.emit(lambda e, h_ap=h_ap, e_i=e_i: e.tensor_copy(
                    self.sm("a_hst", e_i, 1), h_ap[:, TCW - 1:TCW]))
                p.ve.waits([ev[2], h_upd[e_i]])
                t_y = p.ve.emit(lambda e, h_ap=h_ap, s2=slots[2]: e.tensor_tensor(h_ap, s2, h_ap, op=ALU.mult))
                guard[("tmp", 3)] = t_y
                z_ap = self.z_c[:, e_i * TCW:(e_i + 1) * TCW]
                p.ve.waits([t_y, ln_toks[e_i][0], last_zread])
                t_z = p.ve.emit(lambda e, z_ap=z_ap, h_ap=h_ap, e_i=e_i: e.tensor_tensor(
                    z_ap, self.ln_c_f[:, e_i * TCW:(e_i + 1) * TCW], h_ap, op=ALU.add))
                guard[("tmp", 5)] = t_z
                if tc == 0 and e_i == 0:
                    d1 = self.dump(slots[0], [ev[0]], 4096, TCW)
                    d2 = self.dump(slots[3], [ev[3]], 4608, TCW)
                    d3 = self.dump(slots[1], [ev[1]], 5120, TCW)
                    d4 = self.dump(slots[2], [ev[2]], 5632, TCW)
                    d5 = self.dump(a_ap, [t_a], 6144, TCW)
                    d6 = self.dump(h_ap, [t_y], 6656, TCW)
                    for dd, sl in [(d1,0),(d2,1),(d3,2),(d4,3),(d5,4),(d6,5)]:
                        if dd is not None:
                            guard[("tmp", sl)] = dd
                zb = self.stg[:, e_i * TCW:(e_i + 1) * TCW]
                p.se.waits([t_z, guard.get(("zbslot", e_i))])
                t_zb = p.se.emit(lambda e, zb=zb, z_ap=z_ap: e.activation(zb, z_ap, AF.Copy))
                zb_toks.append((zb, z_ap, t_zb))
                guard[("stg67w", e_i)] = t_zb

            tA = tB = None
            for e_i, (zb, z_ap, t_zb) in enumerate(zb_toks):
                p.te.waits([t_zb, self.tok_ones])
                if e_i == 0:
                    p.te.waits(self.rdbk.get(5, []) + self.rdbk.get(6, []))
                tA = p.te.emit(lambda e, zb=zb, e_i=e_i: e.matmul(
                    self.bankap(5), self.ones_bf[:], zb, start=(e_i == 0), stop=(e_i == NKD - 1)))
                sq = self.tmp[:, 6 * TCW:7 * TCW]
                p.se.waits([guard.get(("sq",)), t_zb])
                t_q = p.se.emit(lambda e, sq=sq, z_ap=z_ap: e.activation(sq, z_ap, AF.Square))
                p.se.wait(tA)
                t_c2 = p.se.emit(lambda e, zb=zb, sq=sq: e.activation(zb, sq, AF.Copy))
                guard[("sq",)] = t_c2
                p.te.wait(t_c2)
                tB = p.te.emit(lambda e, zb=zb, e_i=e_i: e.matmul(
                    self.bankap(6), self.ones_bf[:], zb, start=(e_i == 0), stop=(e_i == NKD - 1)))
                guard[("zbslot", e_i)] = tB
            self.stg_guard = tB
            if tc == 0:
                self.dump(self.z_c[:], [t for (_, _, t) in zb_toks], 8192, 4096)
            tok_mu2, tok_r2 = self.ln_finalize(tA, tB, LN_EPS)
            if tc == 0:
                self.dump(self.musig[:], [tok_mu2, tok_r2], 12288, 1024)

            o_ap = self.tmp[:, 6 * TCW:7 * TCW]
            for e_i in range(NKD):
                z_ap = self.z_c[:, e_i * TCW:(e_i + 1) * TCW]
                p.ve.waits([tok_mu2, tok_r2])
                tz1 = p.ve.emit(lambda e, z_ap=z_ap, o_ap=o_ap: e.tensor_tensor(
                    o_ap, z_ap, self.musig[:, 0:TCW], op=ALU.subtract))
                p.ve.wait(tz1)
                tz2 = p.ve.emit(lambda e, o_ap=o_ap: e.tensor_tensor(
                    o_ap, o_ap, self.musig[:, TCW:2 * TCW], op=ALU.mult))
                p.ve.wait(tz2)
                tz3 = p.ve.emit(lambda e, o_ap=o_ap, e_i=e_i: e.tensor_scalar(
                    o_ap, o_ap, self.sm("a_gamma", e_i, 1), self.sm("a_beta", e_i, 1),
                    op0=ALU.mult, op1=ALU.add))
                p.ve.wait(tz3)
                x_sl = self.x[e_i][:, c0:c0 + TCW]
                tx = p.ve.emit(lambda e, o_ap=o_ap, x_sl=x_sl: e.tensor_tensor(
                    x_sl, x_sl, o_ap, op=ALU.add))
                self.tok_x1[e_i][tc] = tx
                p.qs.wait(tx)
                xr_sl = self.d_xres_a[e_i, :, c0:c0 + TCW]
                self.tok_x1_hbm[e_i][tc] = p.qs.emit(lambda e, xr_sl=xr_sl, x_sl=x_sl: e.dma_start(
                    out=xr_sl, in_=x_sl), ch="x")
            guard[("sq",)] = tx
            self.tmp6_guard = tx
            last_zread = tok_r2
            last_zread = guard.get(("sq",))


    def ffn(self, x_toks, d_xres_in, d_xres_out, is_final=False, x_from_hbm=False):
        p = self.p
        bar = p.barrier()
        g = self.guard

        for tc in range(NTC):
            c0 = tc * TCW
            if x_from_hbm:
                for k in range(NKD):
                    xs = d_xres_in[k, :, c0:c0 + TCW]
                    dst = self.x2c[:, k * TCW:(k + 1) * TCW]
                    p.qg.waits(bar + [g.get(("fx2c", tc))])
                    p.qg.emit(lambda e, xs=xs, dst=dst: e.dma_start(out=dst, in_=xs), ch="x2")
                ld_all = p.qg.tok("x2")
                xtile = lambda k: self.x2c[:, k * TCW:(k + 1) * TCW]
                xtok = lambda k: ld_all
            else:
                xtile = lambda k: self.x[k][:, c0:c0 + TCW]
                xtok = lambda k: x_toks[k]
            tiles = [(xtile(k), xtok(k)) for k in range(NKD)]
            pair = (5, 6) if tc % 2 == 0 else (3, 4)
            tok_mu, tok_r = self.ln_stats_chunk(tiles, LN_EPS, *pair)
            t2 = None
            for k in range(NKD):
                sl = self.tmp[:, (k % 2) * TCW:(k % 2 + 1) * TCW]
                lb = self.lnF[k][:, c0:c0 + TCW]
                x_sl = xtile(k)
                p.ve.waits([tok_mu, xtok(k), g.get(("lntmp", k % 2))] + bar)
                t1 = p.ve.emit(lambda e, sl=sl, x_sl=x_sl: e.tensor_tensor(
                    sl, x_sl, self.musig[:, 0:TCW], op=ALU.subtract))
                p.ve.waits([t1, tok_r])
                t2 = p.ve.emit(lambda e, lb=lb, sl=sl: e.tensor_tensor(lb, sl, self.musig[:, TCW:2 * TCW], op=ALU.mult))
                g[("lntmp", k % 2)] = t2
                g[("lnF", k, tc)] = t2
            g[("fx2c", tc + 1)] = t2
        ln_bar = [p.ve.tok(), p.se.tok()]
        self.dump_bf(self.lnF[0][:], [g[("lnF", 0, tc)] for tc in range(NTC)], 0, T)

        g[("bank", 5)] = self.rd56[0]
        rot = [0]

        def stage_mm_tc(o, nk, wslice, rhs_fn, rhs_toks, wtoks, tc):
            bank = rot[0] % 6
            rot[0] += 1
            p.te.waits(wtoks + self.bank_wait(bank))
            last = None
            for k in range(nk):
                p.te.wait(rhs_toks(k, tc))
                last = p.te.emit(lambda e, k=k, tc=tc, bank=bank, wslice=wslice, rhs_fn=rhs_fn: e.matmul(
                    self.bankap(bank), wslice(k), rhs_fn(k, tc),
                    start=(k == 0), stop=(k == nk - 1)), inc=(k == nk - 1))
            return [(tc, bank, last)]

        def stage_mm(o, nk, wslice, rhs_fn, rhs_toks, wtoks, extra_mm_wait=None, korder=None):
            out = []
            ks = korder if korder is not None else list(range(nk))
            for tc in range(NTC):
                bank = rot[0] % 6
                rot[0] += 1
                p.te.waits(wtoks + self.bank_wait(bank))
                if extra_mm_wait:
                    p.te.waits(extra_mm_wait)
                last = None
                for i_k, k in enumerate(ks):
                    p.te.wait(rhs_toks(k, tc))
                    st_f, sp_f = (i_k == 0), (i_k == len(ks) - 1)
                    last = p.te.emit(lambda e, k=k, tc=tc, bank=bank, wslice=wslice, rhs_fn=rhs_fn,
                                     st_f=st_f, sp_f=sp_f: e.matmul(
                        self.bankap(bank), wslice(k), rhs_fn(k, tc),
                        start=st_f, stop=sp_f), inc=sp_f)
                out.append((tc, bank, last))
            return out

        wuser = {}
        s1_wtok = {}

        def s1_issue(o):
            buf = o % 2
            p.qg.waits([wuser.get((buf, 0)), wuser.get((buf, 1))] + bar)
            chn = "wA" if buf == 0 else "wB"
            tw1 = p.qg.emit(lambda e, o=o, buf=buf: e.dma_start(
                out=self.wbuf[:, buf * 4096:buf * 4096 + D], in_=self.d_f_w1[o, :, :]), ch=chn)
            tw2 = p.qg.emit(lambda e, o=o, buf=buf: e.dma_start(
                out=self.wbuf[:, buf * 4096 + D:buf * 4096 + 2 * D], in_=self.d_f_w2[o, :, :]), ch=chn)
            s1_wtok[o] = [tw1, tw2]

        s1_issue(0)
        s1_issue(1)
        hA_tok = [[None] * NTC for _ in range(NKH)]
        for o in range(NKH):
            buf = o % 2
            w1s = lambda k, buf=buf: self.wbuf[:, buf * 4096 + k * 128:buf * 4096 + (k + 1) * 128]
            w2s = lambda k, buf=buf: self.wbuf[:, buf * 4096 + D + k * 128:buf * 4096 + D + (k + 1) * 128]
            rhs = lambda k, tc: self.lnF[k][:, tc * TCW:(tc + 1) * TCW]
            rtok = lambda k, tc: g[("lnF", k, tc)]
            last1 = last2 = None
            for tc in range(NTC):
                ((_, b1, t1k),) = stage_mm_tc(o, NKD, w1s, rhs, rtok, s1_wtok[o], tc)
                ((_, b2, t2k),) = stage_mm_tc(o, NKD, w2s, rhs, rtok, s1_wtok[o], tc)
                last1, last2 = t1k, t2k
                t1s = self.tmp[:, (2 + tc % 2) * TCW:(3 + tc % 2) * TCW]
                p.se.waits([t1k, g.get(("s1tmp", tc % 2))] + bar)
                ts = p.se.emit(lambda e, t1s=t1s, b1=b1, o=o: e.activation(
                    t1s, self.bankap(b1), AF.Silu, scale=self.sm("f_s1", o, 1)))
                g[("bank", b1)] = ts
                dst = self.hA[o][:, tc * TCW:(tc + 1) * TCW]
                p.ve.waits([t2k, ts] + bar + ln_bar)
                th = p.ve.emit(lambda e, dst=dst, b2=b2, o=o, t1s=t1s: e.scalar_tensor_tensor(
                    dst, self.bankap(b2), self.sm("f_s2", o, 1), t1s, op0=ALU.mult, op1=ALU.mult))
                g[("bank", b2)] = th
                g[("s1tmp", tc % 2)] = th
                hA_tok[o][tc] = th
            wuser[(buf, 0)] = last1
            wuser[(buf, 1)] = last2
            if o + 2 < NKH:
                s1_issue(o + 2)

        self.dump_bf(self.hA[0][:], hA_tok[0], 2048, T)
        def big_stage(d_w, s_name, d_spill, in_toks, final_stage=False, d_xin=None, d_xout=None,
                      in_res=None, resident_out=False, korder=None):
            wtok = {}

            def issue(o):
                buf = o % 2
                p.qg.waits([wuser.get((buf, "big")), wuser.get((buf, 0)), wuser.get((buf, 1))] + bar)
                wtok[o] = [p.qg.emit(lambda e, o=o, buf=buf: e.dma_start(
                    out=self.wbuf[:, buf * 4096:(buf + 1) * 4096], in_=d_w[o, :, :]),
                    ch="wA" if buf == 0 else "wB")]

            issue(0)
            issue(1)
            no = NKD if final_stage else NKH
            spill_tok = [[None] * NTC for _ in range(no)]
            out_toks = [[None] * NTC for _ in range(no)]
            res_out = {}
            scale_nm = s_name
            for o in range(no):
                buf = o % 2
                ws = lambda k, buf=buf: self.wbuf[:, buf * 4096 + k * 128:buf * 4096 + (k + 1) * 128]
                if in_res:
                    rhs = lambda k, tc: (in_res[k][0](tc) if k in in_res
                                         else self.hA[k][:, tc * TCW:(tc + 1) * TCW])
                    rtok = lambda k, tc: (in_res[k][1] if k in in_res else in_toks[k][tc])
                else:
                    rhs = lambda k, tc: self.hA[k][:, tc * TCW:(tc + 1) * TCW]
                    rtok = lambda k, tc: in_toks[k][tc]
                mm = stage_mm(o, NKH, ws, rhs, rtok, wtok[o], korder=korder)
                wuser[(buf, "big")] = mm[-1][2]
                if o + 2 < no:
                    issue(o + 2)
                for tc, bank, mtk in mm:
                    if not final_stage:
                        if resident_out and o >= 24:
                            rdst = self.lnF[o - 24][:, tc * TCW:(tc + 1) * TCW]
                            p.se.waits([mtk, g.get(("resout", o, tc))] + bar)
                            tev = p.se.emit(lambda e, rdst=rdst, bank=bank, o=o: e.activation(
                                rdst, self.bankap(bank), AF.Gelu, scale=self.sm(scale_nm, o, 1)))
                            g[("bank", bank)] = tev
                            res_out[(o, tc)] = tev
                            spill_tok[o][tc] = tev
                            out_toks[o][tc] = tev
                            continue
                        slot = (o * NTC + tc) % 4
                        sdst = self.stg[:, slot * TCW:(slot + 1) * TCW]
                        p.se.waits([mtk, g.get(("stgslot", slot))] + bar)
                        tev = p.se.emit(lambda e, sdst=sdst, bank=bank, o=o: e.activation(
                            sdst, self.bankap(bank), AF.Gelu, scale=self.sm(scale_nm, o, 1)))
                        g[("bank", bank)] = tev
                        dsp = d_spill[o, :, tc * TCW:(tc + 1) * TCW]
                        p.qs.wait(tev)
                        tsp = p.qs.emit(lambda e, dsp=dsp, sdst=sdst: e.dma_start(out=dsp, in_=sdst), ch="sp")
                        g[("stgslot", slot)] = tsp
                        spill_tok[o][tc] = tsp
                    else:
                        slot = (o * NTC + tc) % 2
                        xo = self.tmp[:, slot * TCW:(slot + 1) * TCW]
                        xsrc = d_xin[o, :, tc * TCW:(tc + 1) * TCW]
                        p.qg.waits([g.get(("xold", slot))] + bar)
                        txo = p.qg.emit(lambda e, xo=xo, xsrc=xsrc: e.dma_start(out=xo, in_=xsrc),
                                        ch="xo0" if slot == 0 else "xo1")
                        dst = self.tmp[:, (2 + slot) * TCW:(3 + slot) * TCW]
                        p.ve.waits([mtk, txo, g.get(("xnew", slot))] + bar)
                        tx = p.ve.emit(lambda e, dst=dst, bank=bank, o=o, xo=xo: e.scalar_tensor_tensor(
                            dst, self.bankap(bank), self.sm("f_s3", o, 1), xo, op0=ALU.mult, op1=ALU.add))
                        g[("bank", bank)] = tx
                        g[("xold", slot)] = tx
                        p.qs.wait(tx)
                        if is_final:
                            dd = self.d_out[o * 128:(o + 1) * 128, tc * TCW:(tc + 1) * TCW]
                        else:
                            dd = d_xout[o, :, tc * TCW:(tc + 1) * TCW]
                        tw = p.qs.emit(lambda e, dd=dd, dst=dst: e.dma_start(out=dd, in_=dst), ch="out")
                        g[("xnew", slot)] = tw
                        out_toks[o][tc] = (tx, tw)
            return spill_tok, out_toks, mm[-1][2]

        def chased_reload(d_src, last_mm, sp_toks, n_tiles, phase_tag):
            toks = [None] * n_tiles
            grp = (n_tiles + 3) // 4
            for gi4 in range(4):
                ks = range(gi4 * grp, min((gi4 + 1) * grp, n_tiles))
                q = p.qs if gi4 % 2 == 0 else p.qg
                chn = f"rl{phase_tag}{gi4}"
                for k in ks:
                    q.waits([last_mm, p.qs.tok("sp")])
                    q.emit(lambda e, k=k: e.dma_start(out=self.hA[k][:], in_=d_src[k, :, :]), ch=chn)
                t = q.tok(chn)
                for k in ks:
                    toks[k] = t
            return toks

        KORD = list(range(24, NKH)) + list(range(24))
        sp_a, s2out, s2_last = big_stage(self.d_f_h1, "f_h1s", self.d_hbuf_a, hA_tok,
                                         resident_out=True)
        rel = chased_reload(self.d_hbuf_a, s2_last, None, 24, "a" + str(is_final))
        rel_tok = [[rel[k]] * NTC if k < 24 else None for k in range(24)]
        in_res2 = {k: ((lambda tc, kk=k: self.lnF[kk - 24][:, tc * TCW:(tc + 1) * TCW]), s2out[k][NTC - 1])
                   for k in range(24, NKH)}
        rel_full = [[rel[k]] * NTC for k in range(24)] + [[None] * NTC for _ in range(8)]
        sp_b, _, s3_last = big_stage(self.d_f_h2, "f_h2s", self.d_hbuf_b, rel_full,
                                     in_res=in_res2, korder=KORD)
        rel2 = chased_reload(self.d_hbuf_b, s3_last, None, NKH, "b" + str(is_final))
        rel2_full = [[rel2[k]] * NTC for k in range(NKH)]
        _, out_toks, _ = big_stage(self.d_f_w3, "f_s3", None, rel2_full, final_stage=True,
                                   d_xin=d_xres_in, d_xout=d_xres_out)
        new_hbm_toks = [[out_toks[k][tc][1] for tc in range(NTC)] for k in range(NKD)]
        return None, new_hbm_toks


    def attn_b(self, x2_hbm_tok, is_final=False):
        p = self.p
        g = self.guard
        bar = p.barrier() + [x2_hbm_tok]
        SC = ATTN_SCALE

        def scr(i, w=1):
            return self.sm("scr", i, w)

        sc_ev = None
        for tc in range(NTC):
            c0 = tc * TCW
            ld_toks = []
            for k in range(NKD):
                xs = self.d_xres_b[k, :, c0:c0 + TCW]
                dst = self.x2c[:, k * TCW:(k + 1) * TCW]
                p.qg.waits(bar + [g.get(("x2c", tc))])
                ld_toks.append(p.qg.emit(lambda e, xs=xs, dst=dst: e.dma_start(out=dst, in_=xs), ch="x2"))
            ld_all = p.qg.tok("x2")
            tiles = [(self.x2c[:, k * TCW:(k + 1) * TCW], ld_all) for k in range(NKD)]
            pair = (5, 6) if tc % 2 == 0 else (3, 4)
            tok_mu, tok_r = self.ln_stats_chunk(tiles, LN_EPS, *pair)
            if tc == 0:
                p.te.waits([self.tok_small, self.tok_ones_col, g.get(("bank", 4))])
                for k in range(NKD):
                    t_sw = p.te.emit(lambda e, k=k: e.matmul(
                        self.psf[0:1, 4 * TCW:4 * TCW + 1], self.sm("b_ws", k, 1),
                        self.ones_f32_col[:, 0:1], start=(k == 0), stop=(k == NKD - 1)),
                        inc=(k == NKD - 1))
                p.se.wait(t_sw)
                self.tok_sws = p.se.emit(lambda e: e.activation(
                    self.small[0:1, SM["scr"][0] + 16:SM["scr"][0] + 17],
                    self.psf[0:1, 4 * TCW:4 * TCW + 1], AF.Copy))
                g[("bank", 4)] = self.tok_sws
            for k in range(NKD):
                sl = self.tmp[:, (k % 2) * TCW:(k % 2 + 1) * TCW]
                lb = self.lnF[k][:, c0:c0 + TCW]
                x_sl = self.x2c[:, k * TCW:(k + 1) * TCW]
                p.ve.waits([tok_mu, ld_all, g.get(("lntmp", k % 2))] + bar)
                t1 = p.ve.emit(lambda e, sl=sl, x_sl=x_sl: e.tensor_tensor(
                    sl, x_sl, self.musig[:, 0:TCW], op=ALU.subtract))
                p.ve.waits([t1, tok_r])
                t2 = p.ve.emit(lambda e, lb=lb, sl=sl: e.tensor_tensor(lb, sl, self.musig[:, TCW:2 * TCW], op=ALU.mult))
                g[("lnF3", k, tc)] = t2
                g[("lntmp", k % 2)] = t2
                p.te.waits([ld_all, self.tok_small] + (self.bank_wait(4) if k == 0 else []))
                tmm = p.te.emit(lambda e, x_sl=x_sl, k=k: e.matmul(
                    self.psf[0:1, 4 * TCW:5 * TCW], self.sm("b_ws", k, 1), x_sl,
                    start=(k == 0), stop=(k == NKD - 1)))
            srow_sl = self.srow[0:1, c0:c0 + TCW]
            rowt = self.tmp[0:1, 6 * TCW:7 * TCW]
            mu_row = self.musig[0:1, 0:TCW]
            r_row = self.musig[0:1, TCW:2 * TCW]
            p.ve.waits([tmm, tok_mu, tok_r, self.tok_sws, g.get(("srow", tc))])
            tv1 = p.ve.emit(lambda e, srow_sl=srow_sl, r_row=r_row: e.tensor_tensor(
                srow_sl, self.psf[0:1, 4 * TCW:5 * TCW], r_row, op=ALU.mult))
            tv2 = p.ve.emit(lambda e, rowt=rowt, mu_row=mu_row, r_row=r_row: e.tensor_tensor(
                rowt, mu_row, r_row, op=ALU.mult))
            p.ve.wait(tv2)
            tv3 = p.ve.emit(lambda e, rowt=rowt: e.tensor_scalar(
                rowt, rowt, self.small[0:1, SM["scr"][0] + 16:SM["scr"][0] + 17], None, op0=ALU.mult))
            p.ve.waits([tv1, tv3])
            tv4 = p.ve.emit(lambda e, srow_sl=srow_sl, rowt=rowt: e.tensor_tensor(
                srow_sl, srow_sl, rowt, op=ALU.subtract))
            p.ve.wait(tv4)
            sc_ev = p.ve.emit(lambda e, srow_sl=srow_sl: e.tensor_scalar(
                srow_sl, srow_sl, self.small[0:1, SM["b_bs"][0]:SM["b_bs"][0] + 1], None, op0=ALU.add))
            g[("bank", 4)] = sc_ev
            g[("x2c", tc + 1)] = sc_ev
        self.dump_bf(self.lnF[0][:, 0:512], [g[("lnF3", 0, 0)]], 1536, 512)
        self.dump(self.x2c[:, 0:512], [p.te.tok()], 15456, 512)
        if self.dbg_on:
            self.p.qs.waits([sc_ev])
            self.p.qs.emit(lambda e: e.dma_start(out=self.d_dbg[0:1, 15968:16384], in_=self.srow[0:1, 0:416]), ch="dbg")
        ln3_done = [p.ve.tok(), p.se.tok(), p.te.tok()]

        wt_toks = []
        p.ve.waits([self.tok_ones, self.tok_small] + bar + [g.get(("bank", 0))])
        for b3 in range(3):
            for k in range(NKD):
                col = (b3 * NKD + k) * 128
                wcol = self.sm("b_wg", k * 3 + b3, 1)
                p.ve.waits([g.get(("wtg",))])
                wt_toks.append(p.ve.emit(lambda e, col=col, wcol=wcol: e.tensor_scalar(
                    self.wbuf[:, col:col + 128], self.ones_bf[:], wcol, None, op0=ALU.mult)))
        for b3 in range(3):
            for tc in range(NTC):
                bank = (b3 * NTC + tc) % 5
                p.te.waits(wt_toks + self.bank_wait(bank))
                for k in range(NKD):
                    p.te.wait(g[("lnF3", k, tc)])
                    tmm = p.te.emit(lambda e, b3=b3, k=k, tc=tc, bank=bank: e.matmul(
                        self.bankap(bank),
                        self.wbuf[:, (b3 * NKD + k) * 128:(b3 * NKD + k + 1) * 128],
                        self.lnF[k][:, tc * TCW:(tc + 1) * TCW],
                        start=(k == 0), stop=(k == NKD - 1)), inc=(k == NKD - 1))
                g[("wtg",)] = tmm
                dst = self.gn[b3][:, tc * TCW:(tc + 1) * TCW]
                p.se.waits([tmm] + bar)
                tev = p.se.emit(lambda e, dst=dst, bank=bank, b3=b3: e.activation(
                    dst, self.bankap(bank), AF.Sigmoid, bias=self.sm("b_bg", b3, 1)))
                g[("bank", bank)] = tev
                g[("gn", b3, tc)] = tev
        gn_toks = [g[("gn", b3, tc)] for b3 in range(3) for tc in range(NTC)]
        p.ve.waits(gn_toks + bar)
        ts1 = p.ve.emit(lambda e: e.tensor_tensor(self.rgsum[:], self.gn[0][:], self.gn[1][:], op=ALU.add))
        p.ve.wait(ts1)
        ts2 = p.ve.emit(lambda e: e.tensor_tensor(self.rgsum[:], self.rgsum[:], self.gn[2][:], op=ALU.add))
        p.ve.wait(ts2)
        ts3 = p.ve.emit(lambda e: e.tensor_scalar(self.rgsum[:], self.rgsum[:], 1e-6, None, op0=ALU.add))
        p.ve.wait(ts3)
        ts4 = p.ve.emit(lambda e: e.reciprocal(self.rgsum[:], self.rgsum[:]))
        gtok = []
        for b3 in range(3):
            p.ve.wait(ts4)
            gtok.append(p.ve.emit(lambda e, b3=b3: e.tensor_tensor(
                self.gn[b3][:], self.gn[b3][:], self.rgsum[:], op=ALU.mult)))
        self.gn_tok = gtok

        self.dump(self.gn[0][:], gtok, 13408, 2048)
        qkv_wuser = {}
        self.qkv_cached = getattr(self, "qkv_cached", False)

        def load_qkv_w(od, buf, extra=()):
            chn = "wA" if buf == 0 else "wB"
            toks = []
            p.qg.waits([qkv_wuser.get(buf)] + list(extra) + bar)
            if self.qkv_cached:
                p.qs.waits([qkv_wuser.get(buf)] + list(extra) + bar)
                for pj in range(3):
                    toks.append(p.qs.emit(lambda e, pj=pj, od=od, buf=buf: e.dma_start(
                        out=self.wbuf[:, buf * 4096 + pj * D:buf * 4096 + (pj + 1) * D],
                        in_=self.d_qkv_bf[pj, od, :, :]), ch="wC0" if buf == 0 else "wC1"))
                return toks
            for pj, dw in enumerate([self.d_b_Wq, self.d_b_Wk, self.d_b_Wv]):
                p.qg.emit(lambda e, pj=pj, dw=dw, od=od, buf=buf: e.dma_start(
                    out=self.wbuf[:, buf * 4096 + pj * D:buf * 4096 + (pj + 1) * D],
                    in_=dw[od, :, :]), ch=chn)
            t_all = p.qg.tok(chn)
            p.qs.wait(t_all)
            for pj in range(3):
                ts_ = p.qs.emit(lambda e, pj=pj, od=od, buf=buf: e.dma_start(
                    out=self.d_qkv_bf[pj, od, :, :],
                    in_=self.wbuf[:, buf * 4096 + pj * D:buf * 4096 + (pj + 1) * D]), ch="wS0" if buf == 0 else "wS1")
                toks.append(ts_)
            return toks

        def qkv_project(src_tiles, src_toks, width, dst, bias_names=("b_bq", "b_bk", "b_bv"),
                        wtg_guard=None):
            wt = {0: load_qkv_w(0, 0, extra=(wtg_guard,)), 1: load_qkv_w(1, 1, extra=(wtg_guard,))}
            ev_toks = [[None] * NKD for _ in range(3)]
            for od in range(NKD):
                buf = od % 2
                lastmm = None
                for pj in range(3):
                    bank = (od * 3 + pj) % 5
                    p.te.waits(wt[od] + self.bank_wait(bank))
                    for k in range(NKD):
                        st_ = src_toks[k]
                        p.te.waits(st_ if isinstance(st_, list) else [st_])
                        lastmm = p.te.emit(lambda e, pj=pj, k=k, bank=bank, buf=buf: e.matmul(
                            self.bankap(bank, w=width),
                            self.wbuf[:, buf * 4096 + pj * D + k * 128:buf * 4096 + pj * D + (k + 1) * 128],
                            src_tiles[k], start=(k == 0), stop=(k == NKD - 1)), inc=(k == NKD - 1))
                    dstap = dst[pj][od][:]
                    p.se.waits([lastmm] + bar)
                    tev = p.se.emit(lambda e, dstap=dstap, bank=bank, pj=pj, od=od, bias_names=bias_names:
                                    e.activation(dstap, self.bankap(bank, w=width), AF.Identity,
                                                 bias=self.sm(bias_names[pj], od, 1)))
                    g[("bank", bank)] = tev
                    ev_toks[pj][od] = tev
                qkv_wuser[buf] = lastmm
                if od + 2 < NKD:
                    wt[od + 2] = load_qkv_w(od + 2, (od + 2) % 2)
            return ev_toks

        def attend(qt_n, kw, q_tiles, k_tiles, v_tiles, qkv_toks, out_fn, gslice, gtok):
            p_toks = []
            for qt in range(qt_n):
                bank = qt % 5
                p.te.waits(self.bank_wait(bank))
                last = None
                for dk in range(NKD):
                    p.te.waits([qkv_toks[0][dk], qkv_toks[1][dk]])
                    last = p.te.emit(lambda e, dk=dk, qt=qt, bank=bank: e.matmul(
                        self.bankap(bank, w=kw),
                        q_tiles[dk][:, qt * 128:(qt + 1) * 128],
                        k_tiles[dk][:, 0:kw], start=(dk == 0), stop=(dk == NKD - 1)),
                        inc=(dk == NKD - 1))
                rmax = scr(4 + qt, 1)
                p.ve.waits([last, g.get(("smx", qt))])
                t_m = p.ve.emit(lambda e, rmax=rmax, bank=bank, kw=kw: e.reduce_max(
                    rmax, self.bankap(bank, w=kw), axis=AX))
                p.ve.wait(t_m)
                t_b = p.ve.emit(lambda e, rmax=rmax, qt=qt: e.tensor_scalar(
                    scr(8 + qt, 1), rmax, -SC, None, op0=ALU.mult))
                pdst = self.pP[qt][:, 0:kw]
                p.se.waits([t_b, g.get(("pP", qt))] + bar)
                t_e = p.se.emit(lambda e, pdst=pdst, bank=bank, qt=qt, kw=kw: e.activation(
                    pdst, self.bankap(bank, w=kw), AF.Exp, bias=scr(8 + qt, 1), scale=SC,
                    accum_out=scr(12 + qt, 1)))
                g[("bank", bank)] = t_e
                p.ve.wait(t_e)
                t_r = p.ve.emit(lambda e, qt=qt: e.reciprocal(scr(8 + qt, 1), scr(12 + qt, 1)))
                p.ve.wait(t_r)
                t_n = p.ve.emit(lambda e, pdst=pdst, qt=qt: e.tensor_scalar(
                    pdst, pdst, scr(8 + qt, 1), None, op0=ALU.mult))
                g[("smx", qt)] = t_n
                p_toks.append(t_n)
            jn = kw // 128
            tr_toks = []
            ev_ps = g.get(("psb",))
            ntr = 0
            pend = []
            pT_tok = [None] * jn
            for qt in range(qt_n):
                for j in range(jn):
                    p.te.waits([p_toks[qt], self.tok_eye, ev_ps])
                    ttr = p.te.emit(lambda e, qt=qt, j=j, ntr=ntr: e.transpose(
                        self.psb[:, (ntr % 8) * 128:(ntr % 8 + 1) * 128],
                        self.pP[qt][:, j * 128:(j + 1) * 128], self.eye_bf[:]))
                    pend.append((ttr, qt, j, ntr % 8))
                    ntr += 1
                    if ntr % 8 == 0 or (qt == qt_n - 1 and j == jn - 1):
                        batch_last = pend[-1][0]
                        for (tt, q2, j2, s2) in pend:
                            dstp = self.pTt[j2][:, q2 * 128:(q2 + 1) * 128]
                            p.se.waits([batch_last, g.get(("pT", j2))] + bar)
                            tev = p.se.emit(lambda e, dstp=dstp, s2=s2: e.activation(
                                dstp, self.psb[:, s2 * 128:(s2 + 1) * 128], AF.Copy))
                            pT_tok[j2] = tev
                        ev_ps = tev
                        pend = []
            for qt in range(qt_n):
                g[("pP", qt)] = ev_ps
            g[("psb",)] = ev_ps
            vT_tok = [None] * jn
            pend = []
            ntr = 0
            for dv in range(NKD):
                for j in range(jn):
                    p.te.waits([qkv_toks[2][dv], self.tok_eye, ev_ps])
                    ttr = p.te.emit(lambda e, dv=dv, j=j, ntr=ntr: e.transpose(
                        self.psb[:, (ntr % 8) * 128:(ntr % 8 + 1) * 128],
                        v_tiles[dv][:, j * 128:(j + 1) * 128], self.eye_bf[:]))
                    pend.append((ttr, dv, j, ntr % 8))
                    ntr += 1
                    if ntr % 8 == 0 or (dv == NKD - 1 and j == jn - 1):
                        batch_last = pend[-1][0]
                        for (tt, d2, j2, s2) in pend:
                            dstv = self.vT[j2][:, d2 * 128:(d2 + 1) * 128]
                            p.se.waits([batch_last, g.get(("vT", j2))] + bar)
                            tev = p.se.emit(lambda e, dstv=dstv, s2=s2: e.activation(
                                dstv, self.psb[:, s2 * 128:(s2 + 1) * 128], AF.Copy))
                            vT_tok[j2] = tev
                        ev_ps = tev
                        pend = []
            g[("psb",)] = ev_ps
            out_toks = []
            qw = qt_n * 128
            for dv in range(NKD):
                bank = dv % 5
                p.te.waits(self.bank_wait(bank))
                last = None
                for j in range(jn):
                    p.te.waits([vT_tok[j], pT_tok[j]])
                    last = p.te.emit(lambda e, dv=dv, j=j, bank=bank, qw=qw: e.matmul(
                        self.bankap(bank, w=qw),
                        self.vT[j][:, dv * 128:(dv + 1) * 128],
                        self.pTt[j][:, 0:qw], start=(j == 0), stop=(j == jn - 1)),
                        inc=(j == jn - 1))
                dst, dguard = out_fn(dv)
                p.ve.waits([last, gtok, dguard] + bar)
                tev = p.ve.emit(lambda e, dst=dst, bank=bank, gslice=gslice, qw=qw: e.tensor_tensor(
                    dst, self.bankap(bank, w=qw), gslice, op=ALU.mult))
                g[("bank", bank)] = tev
                g[("pT", None)] = tev
                out_toks.append(tev)
            for j in range(jn):
                g[("pT", j)] = out_toks[-1]
                g[("vT", j)] = out_toks[-1]
            return out_toks

        for tc in range(NTC):
            bank = tc % 5
            p.te.waits([sc_ev, self.tok_ones1] + self.bank_wait(bank))
            tb = p.te.emit(lambda e, tc=tc, bank=bank: e.matmul(
                self.bankap(bank), self.ones_f1[0:1, :],
                self.srow[0:1, tc * TCW:(tc + 1) * TCW], start=True, stop=True))
            dst = self.scb[:, tc * TCW:(tc + 1) * TCW]
            p.ve.waits([tb] + bar)
            tev = p.ve.emit(lambda e, dst=dst, bank=bank: e.tensor_copy(dst, self.bankap(bank)))
            g[("bank", bank)] = tev
            g[("scb", tc)] = tev
        p.te.waits([sc_ev, self.tok_eye_f] + self.bank_wait(5) + self.bank_wait(6))
        ttrs = []
        for k in range(16):
            ttrs.append(p.te.emit(lambda e, k=k: e.transpose(
                self.psf[:, 5 * TCW + k:5 * TCW + k + 1],
                self.srow[0:1, k * 128:(k + 1) * 128], self.ones_f1[0:1, 0:1])))
        p.se.waits(ttrs + bar)
        t_sT = p.se.emit(lambda e: e.activation(self.musig[:, 0:16], self.psf[:, 5 * TCW:5 * TCW + 16], AF.Copy))
        self.rdbk[5] = [t_sT]
        rkA = self.rkA[:]
        rk_dep = [t_sT] + [g[("scb", tc)] for tc in range(NTC)] + bar
        for k in range(16):
            p.ve.waits(rk_dep + [g.get(("rk",))])
            t1 = p.ve.emit(lambda e, k=k: e.tensor_scalar(
                rkA, self.scb[:], self.musig[:, k:k + 1], 0.0, op0=ALU.is_gt, op1=ALU.add,
                accum_out=self.musig[:, 64 + k:65 + k]))
            g[("rk",)] = t1
        p.ve.wait(g[("rk",)])
        tm = p.ve.emit(lambda e: e.tensor_scalar(self.musig[:, 16:32], self.musig[:, 64:80],
                                                 512.0, None, op0=ALU.is_lt))
        wc_user = None
        wc_tok = {}

        def wc_issue(od):
            buf = od % 2
            chn = "wA" if buf == 0 else "wB"
            p.qg.waits([wc_user, g.get(("wtg",))] + bar)
            wc_tok[od] = [p.qg.emit(lambda e, od=od, buf=buf: e.dma_start(
                out=self.wbuf[:, buf * 4096:(buf + 1) * 4096], in_=self.d_b_Wc[od, :, :]), ch=chn)]

        wc_issue(0)
        wc_issue(1)
        cmp_toks = [None] * NKD
        for od in range(NKD):
            buf = od % 2
            bank = od % 5
            p.te.waits(wc_tok[od] + self.bank_wait(bank) + [g.get(("wtg",))])
            last = None
            for kk in range(NKH):
                u, c = kk // NKD, kk % NKD
                p.te.wait(g[("lnF3", c, 0)])
                p.te.wait(g[("lnF3", c, 1)])
                p.te.wait(g[("lnF3", c, 2)])
                p.te.wait(g[("lnF3", c, 3)])
                rhs = self.lnF[c][:, u::4]
                last = p.te.emit(lambda e, kk=kk, bank=bank, buf=buf, rhs=rhs: e.matmul(
                    self.bankap(bank), self.wbuf[:, buf * 4096 + kk * 128:buf * 4096 + (kk + 1) * 128],
                    rhs, start=(kk == 0), stop=(kk == NKH - 1)), inc=(kk == NKH - 1))
            wc_user = last
            if od + 2 < NKD:
                wc_issue(od + 2)
            dst = self.cmp[od][:]
            p.se.waits([last] + bar)
            tev = p.se.emit(lambda e, dst=dst, bank=bank, od=od: e.activation(
                dst, self.bankap(bank), AF.Identity, bias=self.sm("b_bc", od, 1)))
            g[("bank", bank)] = tev
            cmp_toks[od] = tev
        self.dump_bf(self.cmp[0][:], cmp_toks, 2048, TCW)
        qkv_toks = qkv_project([self.cmp[k][:] for k in range(NKD)], cmp_toks, TCW,
                               self.qkv, wtg_guard=wc_user)
        self.qkv_cached = True
        self.dump_bf(self.qkv[0][0][:], [qkv_toks[0][0]], 2560, TCW)
        q_t = [self.qkv[0][k] for k in range(NKD)]
        k_t = [self.qkv[1][k] for k in range(NKD)]
        v_t = [self.qkv[2][k] for k in range(NKD)]

        def comp_out(dv):
            return self.comp_g[:, dv * TCW:(dv + 1) * TCW], None
        comp_done = attend(4, TCW, q_t, k_t, v_t, qkv_toks, comp_out,
                           self.gn[0][:, 0:TCW], gtok[0])

        self.dump_bf(self.comp_g[:], comp_done, 3072, 8 * TCW)
        p.ve.wait(tm)
        p.te.waits([tm, self.tok_tri] + self.bank_wait(5))
        t_pr = p.te.emit(lambda e: e.matmul(self.bankap(5, w=16), self.tri_f[:],
                                            self.musig[:, 16:32], start=True, stop=True))
        p.te.waits([tm, self.tok_ones1])
        t_cs = p.te.emit(lambda e: e.matmul(self.psf[0:1, 6 * TCW:6 * TCW + 16], self.ones_f1[0:1, :].transpose([0, 1]) if False else self.ones_f32_col[:, 0:1],
                                            self.musig[:, 16:32], start=True, stop=True))
        p.se.waits([t_pr] + bar)
        t_pre = p.se.emit(lambda e: e.activation(self.musig[:, 48:64], self.bankap(5, w=16), AF.Copy))
        self.rdbk[5] = [t_pre]
        p.se.wait(t_cs)
        t_cse = p.se.emit(lambda e: e.activation(self.musig[0:1, 96:112], self.psf[0:1, 6 * TCW:6 * TCW + 16], AF.Copy))
        p.ve.waits([t_cse, self.tok_ones_row])
        t_sc1 = p.ve.emit(lambda e: e.tensor_tensor_scan(
            self.musig[0:1, 112:128], self.ones_row[0:1, 0:16], self.musig[0:1, 96:112],
            0.0, op0=ALU.mult, op1=ALU.add))
        p.ve.wait(t_sc1)
        t_sc2 = p.ve.emit(lambda e: e.tensor_tensor(self.musig[0:1, 112:128], self.musig[0:1, 112:128],
                                                    self.musig[0:1, 96:112], op=ALU.subtract))
        p.te.waits([t_sc2, self.tok_ones1] + self.bank_wait(6))
        t_bc = p.te.emit(lambda e: e.matmul(self.bankap(6, w=16), self.ones_f1[0:1, :],
                                            self.musig[0:1, 112:128], start=True, stop=True))
        p.ve.waits([t_bc, t_pre])
        t_slot = p.ve.emit(lambda e: e.tensor_tensor(self.musig[:, 32:48], self.musig[:, 48:64],
                                                     self.bankap(6, w=16), op=ALU.add))
        self.rdbk[6] = [t_slot]
        self.dump(self.musig[:, 0:96], [t_slot, tm], 7168, 96)
        oh_toks = []
        for k in range(16):
            p.ve.waits([t_slot, self.tok_iota, comp_done[-1], g.get(("oh", k))])
            oh_toks.append(p.ve.emit(lambda e, k=k: e.tensor_scalar(
                self.oh[k][:], self.iota_f[:], self.musig[:, 32 + k:33 + k],
                self.musig[:, 16 + k:17 + k], op0=ALU.is_equal, op1=ALU.mult)))
        ln3T_tok = [None] * 16
        ev_ps = g.get(("psb",))
        ln3T_extra = list(gtok) + [g.get(("rk",)), tm]
        for tt in range(16):
            trs = []
            for c in range(NKD):
                p.te.waits([g[("lnF3", c, tt // 4)], self.tok_eye, ev_ps])
                trs.append(p.te.emit(lambda e, tt=tt, c=c: e.transpose(
                    self.psb[:, c * 128:(c + 1) * 128],
                    self.lnF[c][:, tt * 128:(tt + 1) * 128], self.eye_bf[:])))
            p.se.waits(trs + [g.get(("ln3T", tt))] + ln3T_extra + bar)
            ln3T_tok[tt] = p.se.emit(lambda e, tt=tt: e.activation(
                self.ln3T[tt][:], self.psb[:], AF.Copy))
            ev_ps = ln3T_tok[tt]
        g[("psb",)] = ev_ps
        selT_tok = [None] * 4
        for rt in range(4):
            for half in range(2):
                bank = (rt * 2 + half) % 5
                p.te.waits(self.bank_wait(bank))
                last = None
                for k in range(16):
                    p.te.waits([oh_toks[k], ln3T_tok[k]])
                    last = p.te.emit(lambda e, rt=rt, half=half, k=k, bank=bank: e.matmul(
                        self.bankap(bank),
                        self.oh[k][:, rt * 128:(rt + 1) * 128],
                        self.ln3T[k][:, half * TCW:(half + 1) * TCW],
                        start=(k == 0), stop=(k == 15)), inc=(k == 15))
                dst = self.selT[rt][:, half * TCW:(half + 1) * TCW]
                p.se.waits([last] + bar)
                tev = p.se.emit(lambda e, dst=dst, bank=bank: e.activation(
                    dst, self.bankap(bank), AF.Copy))
                g[("bank", bank)] = tev
                selT_tok[rt] = tev
        sel_tok = [None] * NKD
        ev_ps = g.get(("psb",))
        for ct in range(NKD):
            trs = []
            for rt in range(4):
                p.te.waits([selT_tok[rt], self.tok_eye, ev_ps, g.get(("rk2",))])
                trs.append(p.te.emit(lambda e, ct=ct, rt=rt: e.transpose(
                    self.psb[:, rt * 128:(rt + 1) * 128],
                    self.selT[rt][:, ct * 128:(ct + 1) * 128], self.eye_bf[:])))
            p.se.waits(trs + [cmp_toks[ct], g.get(("selc", ct))] + bar)
            sel_tok[ct] = p.se.emit(lambda e, ct=ct: e.activation(
                self.cmp[ct][:], self.psb[:, 0:TCW], AF.Copy))
            ev_ps = sel_tok[ct]
        g[("psb",)] = ev_ps
        qkv_toks2 = qkv_project([self.cmp[k][:] for k in range(NKD)], sel_tok, TCW,
                                self.qkv, wtg_guard=selT_tok[3])

        def sel_out(dv):
            return self.sel_g[:, dv * TCW:(dv + 1) * TCW], None
        sel_done = attend(4, TCW, [self.qkv[0][k] for k in range(NKD)],
                          [self.qkv[1][k] for k in range(NKD)],
                          [self.qkv[2][k] for k in range(NKD)], qkv_toks2, sel_out,
                          self.gn[1][:, 0:TCW], gtok[1])

        self.dump_bf(self.sel_g[:], sel_done, 7264, 8 * TCW)
        self.dump(self.scb[:, 0:1536], [g[("scb", tc)] for tc in range(NTC)] + [t_sT], 0, 1536)
        woc = self.scb[:, 0:1536]
        woc_bf = self.nc.alloc_sbuf_tensor_at("woc_bf", [128, 3 * D], BF16,
                                              offset=self.nc.lookup_mloc(self.scb).addr - 0 + 0)             if False else None
        prev_st = None
        rank_done = g.get(("rk",))
        for ot in range(NKD):
            p.qg.waits([prev_st, rank_done, t_slot] + bar)
            tl = p.qg.emit(lambda e, ot=ot: e.dma_start(
                out=self.wocast[:], in_=self.d_b_Wo[ot, :, :]), ch="woL")
            p.qs.wait(tl)
            prev_st = p.qs.emit(lambda e, ot=ot: e.dma_start(
                out=self.d_wo_bf[ot, :, :], in_=self.wocast[:]), ch="woS")
        self.tok_wo_cache = prev_st
        for Q in range(4):
            wq_tiles = self.wq if Q % 2 == 0 else self.wqB
            start = 256 * Q
            src_tiles = [self.lnF[k][:, start:start + 384] for k in range(NKD)]
            tc_a, tc_b = start // TCW, (start + 383) // TCW
            src_toks = [[g[("lnF3", k, tc_a)], g[("lnF3", k, tc_b)]] for k in range(NKD)]
            wg_guard = g.get(("wqg", Q % 2))
            if Q == 0:
                wg_guard = sel_done[-1]
            elif Q == 1:
                wg_guard = self.tok_wo_cache
            qkv_toksw = qkv_project(src_tiles, src_toks, 384, wq_tiles, wtg_guard=wg_guard)
            for wloc in range(2):
                wg = 2 * Q + wloc
                off = 128 * wloc
                qw = [wq_tiles[0][k][:, off:off + 256] for k in range(NKD)]
                kw_ = [wq_tiles[1][k][:, off:off + 256] for k in range(NKD)]
                vw = [wq_tiles[2][k][:, off:off + 256] for k in range(NKD)]

                def win_out(dv, wg=wg):
                    return self.win_g[dv][:, wg * 256:(wg + 1) * 256], selT_tok[3]
                wdone = attend(2, 256, qw, kw_, vw, qkv_toksw, win_out,
                               self.gn[2][:, wg * 256:(wg + 1) * 256], gtok[2])
                g[("wqg", Q % 2)] = wdone[-1]
                for dv in range(NKD):
                    g[("win", dv, wg)] = wdone[dv]

        self.dump_bf(self.win_g[0][:], [g[("win", 0, wg)] for wg in range(NWIN)], 11360, T)
        wo_user = None
        wo_tok = {}

        def wo_issue(idx):
            buf = idx % 2
            p.qs.waits([wo_user, g.get(("wqg", 0)), g.get(("wqg", 1)), self.tok_wo_cache] + bar)
            wo_tok[idx] = [p.qs.emit(lambda e, idx=idx, buf=buf: e.dma_start(
                out=self.wbuf[:, buf * 4096:buf * 4096 + 3 * D], in_=self.d_wo_bf[idx % NKD, :, :]),
                ch="woB0" if buf == 0 else "woB1")]

        self.tok_x3 = [None] * NKD
        self.tok_x3_hbm = [[None] * NTC for _ in range(NKD)]
        comp_all = comp_done[-1]
        sel_all = sel_done[-1]
        win_all = [g[("win", dv, wg)] for dv in range(NKD) for wg in range(NWIN)]
        wo_issue(0)
        wo_issue(1)
        widx = 0
        for tc in range(NTC):
            c0 = tc * TCW
            xbuf = self.x2c if tc % 2 == 0 else self.x2cB
            xch = "x2" if tc % 2 == 0 else "x2b"
            ld_toks = []
            for k in range(NKD):
                xs = self.d_xres_b[k, :, c0:c0 + TCW]
                dst = xbuf[:, k * TCW:(k + 1) * TCW]
                extra0 = [g.get(("rk",)), oh_toks[-1], sel_done[-1], win_all[-1]] + win_all if tc <= 1 else []
                p.qg.waits(bar + [g.get(("x2cc", tc))] + extra0)
                ld_toks.append(p.qg.emit(lambda e, xs=xs, dst=dst: e.dma_start(out=dst, in_=xs), ch=xch))
            ld_all = p.qg.tok(xch)
            z4_toks = []
            POOL5 = [0, 1, 2, 5, 6]
            for ot in range(NKD):
                buf = widx % 2
                bank = POOL5[widx % 5]
                p.te.waits(wo_tok[widx] + self.bank_wait(bank))
                last = None
                klist = list(range(24)) if tc == 0 else list(range(16, 24))
                for i2, kk in enumerate(klist):
                    if kk < 8:
                        rhs = self.comp_g[:, kk * TCW:(kk + 1) * TCW]
                        p.te.wait(comp_all)
                    elif kk < 16:
                        rhs = self.sel_g[:, (kk - 8) * TCW:(kk - 7) * TCW]
                        p.te.wait(sel_all)
                    else:
                        rhs = self.win_g[kk - 16][:, c0:c0 + TCW]
                        p.te.waits(win_all)
                    st_f = (i2 == 0)
                    sp_f = (i2 == len(klist) - 1)
                    last = p.te.emit(lambda e, kk=kk, bank=bank, buf=buf, rhs=rhs, st_f=st_f, sp_f=sp_f: e.matmul(
                        self.bankap(bank), self.wbuf[:, buf * 4096 + kk * 128:buf * 4096 + (kk + 1) * 128],
                        rhs, start=st_f, stop=sp_f), inc=sp_f)
                wo_user = last
                widx += 1
                if widx + 2 <= NTC * NKD:
                    wo_issue(widx + 1)
                m1 = self.tmp[:, (2 + ot % 2) * TCW:(3 + ot % 2) * TCW]
                p.se.waits([last, self.tok_small, g.get(("m1", ot % 2))] + bar)
                t_m1 = p.se.emit(lambda e, m1=m1, bank=bank, ot=ot: e.activation(
                    m1, self.bankap(bank), AF.Identity, bias=self.sm("b_bo", ot, 1)))
                g[("bank", bank)] = t_m1
                z4d = self.z4[:, ot * TCW:(ot + 1) * TCW]
                lnsl = self.lnF[ot][:, c0:c0 + TCW]
                p.ve.waits([t_m1, g.get(("z4r", ot))] + bar)
                t_z4 = p.ve.emit(lambda e, z4d=z4d, m1=m1, lnsl=lnsl: e.tensor_tensor(
                    z4d, m1, lnsl, op=ALU.add))
                g[("m1", ot % 2)] = t_z4
                z4_toks.append((z4d, t_z4))
            tA = tB = None
            for e_i, (z4d, t_z4) in enumerate(z4_toks):
                zb = self.stg[:, e_i * TCW:(e_i + 1) * TCW]
                zq = self.stgC[:, (e_i % 2) * TCW:(e_i % 2 + 1) * TCW]
                p.qg.waits([t_z4, g.get(("zbslot", e_i))], )
                t_zb = p.qg.emit(lambda e, zb=zb, z4d=z4d: e.tensor_copy(zb, z4d), ch="gpc")
                p.qg.waits([g.get(("stgCsl", e_i % 2))])
                t_sq = p.qg.emit(lambda e, zq=zq, z4d=z4d: e.tensor_tensor(zq, z4d, z4d, op=ALU.mult), ch="gpc")
                p.te.waits([t_zb, self.tok_ones])
                if e_i == 0:
                    p.te.waits(self.rdbk.get(3, []) + self.rdbk.get(4, []) + [g.get(("bank", 3)), g.get(("bank", 4))])
                tA = p.te.emit(lambda e, zb=zb, e_i=e_i: e.matmul(
                    self.bankap(3), self.ones_bf[:], zb, start=(e_i == 0), stop=(e_i == NKD - 1)))
                p.te.wait(t_sq)
                tB = p.te.emit(lambda e, zq=zq, e_i=e_i: e.matmul(
                    self.bankap(4), self.ones_bf[:], zq, start=(e_i == 0), stop=(e_i == NKD - 1)))
                g[("zbslot", e_i)] = tA
                g[("stgCsl", e_i % 2)] = tB
            self.stg_guard = tB
            self.stgB_guard = tB
            tok_mu4, tok_r4 = self.ln_finalize(tA, tB, LN4_EPS, 3, 4)
            o_ap = self.tmp[:, 6 * TCW:7 * TCW]
            for ot in range(NKD):
                z4d = self.z4[:, ot * TCW:(ot + 1) * TCW]
                x_sl = xbuf[:, ot * TCW:(ot + 1) * TCW]
                p.ve.waits([tok_mu4, tok_r4, ld_all])
                tz1 = p.ve.emit(lambda e, z4d=z4d, o_ap=o_ap: e.tensor_tensor(
                    o_ap, z4d, self.musig[:, 0:TCW], op=ALU.subtract))
                p.ve.wait(tz1)
                tz2 = p.ve.emit(lambda e, o_ap=o_ap: e.tensor_tensor(
                    o_ap, o_ap, self.musig[:, TCW:2 * TCW], op=ALU.mult))
                p.ve.wait(tz2)
                tx = p.ve.emit(lambda e, x_sl=x_sl, o_ap=o_ap: e.tensor_tensor(
                    x_sl, x_sl, o_ap, op=ALU.add))
                g[("z4r", ot)] = tx
                self.tok_x3[ot] = tx
                if is_final:
                    xr = self.d_out[ot * 128:(ot + 1) * 128, c0:c0 + TCW]
                    chn = "out"
                else:
                    xr = self.d_xres_a[ot, :, c0:c0 + TCW]
                    chn = "x"
                p.qs.wait(tx)
                self.tok_x3_hbm[ot][tc] = p.qs.emit(lambda e, xr=xr, x_sl=x_sl: e.dma_start(out=xr, in_=x_sl), ch=chn)
            g[("sq",)] = tx
            self.tmp6_guard = tx
            g[("x2cc", tc + 2)] = self.tok_x3_hbm[NKD - 1][tc]
        return self.tok_x3


    def finish_hbm_out(self, hbm_toks):
        p = self.p
        for k in range(NKD):
            p.qs.waits(hbm_toks[k])
        for key, v in list(p.qs.n.items()) + list(p.qg.n.items()):
            if key.endswith(".out") or key.endswith(".dbg"):
                p.qs.ops.append(lambda e, sems, key=key, v=v: e.wait_ge(sems[key], v))

    def finish_output(self, src_tiles, src_toks):
        p = self.p
        for k in range(NKD):
            p.qs.wait(src_toks[k])
            p.qs.emit(lambda e, k=k: e.dma_start(
                out=self.d_out[k * 128:(k + 1) * 128, :], in_=src_tiles[k][:]), ch="out")
        for key, v in list(p.qs.n.items()) + list(p.qg.n.items()):
            if key.endswith(".out") or key.endswith(".dbg"):
                p.qs.ops.append(lambda e, sems, key=key, v=v: e.wait_ge(sems[key], v))

    def run_block(self):
        nc = self.nc
        p = self.p
        from contextlib import ExitStack
        chans = set()
        for s in p.streams.values():
            chans.update(s.n.keys())
            chans.add(s._ch(None))
        with ExitStack() as es:
            sems = {nm: es.enter_context(nc.semaphore(nm.replace(".", "_"))) for nm in sorted(chans)}
            block = es.enter_context(nc.Block())

            def runner(stream):
                def f(e):
                    for op in stream.ops:
                        op(e, sems)
                return f
            block.tensor(runner(p.te))
            block.vector(runner(p.ve))
            block.scalar(runner(p.se))
            block.gpsimd(runner(p.qg))
            block.sync(runner(p.qs))


def build():
    nc = bass.Bass("TRN2", target_bir_lowering=False)
    pr = Prog(nc)
    pr.setup()
    pr.attn_a()
    cur_toks = [pr.tok_x1[k][NTC - 1] for k in range(NKD)]
    if PHASES >= 2:
        is_final = (PHASES == 2)
        cur_toks, hbm_toks = pr.ffn(cur_toks, pr.d_xres_a, pr.d_xres_b, is_final=is_final)
        if is_final:
            pr.finish_hbm_out(hbm_toks)
            pr.run_block()
            return nc
    if PHASES >= 3:
        x2_tok = pr.p.qs.tok("out")
        cur_toks = pr.attn_b(x2_tok, is_final=(PHASES == 3))
        if PHASES == 3:
            pr.finish_hbm_out(pr.tok_x3_hbm)
            pr.run_block()
            return nc
    if PHASES >= 4:
        cur_toks, hbm_toks = pr.ffn(None, pr.d_xres_a, pr.d_xres_b, is_final=True, x_from_hbm=True)
        pr.finish_hbm_out(hbm_toks)
    pr.run_block()
    return nc


_NC_CACHE = None

def kernel(**inputs):
    global _NC_CACHE
    P = host_prep(inputs)
    x_full = np.asarray(inputs["L0_attn_a_x"], dtype=np.float32)
    if _NC_CACHE is None:
        _NC_CACHE = build()
    nc = _NC_CACHE
    in_maps = []
    for b in range(8):
        m = {"x_in": np.ascontiguousarray(x_full[b].T), "small": P["small"]}
        for nm in ["a_Wi", "a_Wf", "a_Wo", "a_Ws", "f_w1", "f_w2", "f_h1", "f_h2", "f_w3",
                   "b_Wq", "b_Wk", "b_Wv", "b_Wo", "b_Wc"]:
            m[nm] = P[nm]
        in_maps.append(m)
    res = run_bass_kernel_spmd(nc, in_maps, core_ids=list(range(8)))
    return np.stack([res.results[b]["x_out"].T for b in range(8)])


# version 08
# speedup vs baseline: 1.2347x; 1.0134x over previous
import os
import numpy as np
import concourse.bass as bass
import concourse.mybir as mybir
from concourse.bass_utils import run_bass_kernel_spmd

F32 = mybir.dt.float32
BF16 = mybir.dt.bfloat16
AF = mybir.ActivationFunctionType
ALU = mybir.AluOpType
AX = mybir.AxisListType.X

D = 1024
T = 2048
NKD = 8
NTC = 4
TCW = 512
DH = 4096
NKH = 32
LN_EPS = 1e-5
LN4_EPS = 1e-6
ATTN_SCALE = 0.125
NWIN = 8
KB = 1024

PHASES = int(os.environ.get("MK_PHASES", "4"))
SELFWAIT = bool(int(os.environ.get("MK_SELFWAIT", "0")))


ALL_CHANNELS = set()
CHAN_N = {}


class Stream:

    def __init__(self, name, step):
        self.name = name
        self.step = step
        self.n = {}
        self.ops = []
        self.wm = {}

    def _ch(self, ch):
        key = self.name if ch is None else f"{self.name}.{ch}"
        ALL_CHANNELS.add(key)
        return key

    def wait(self, tok):
        if tok is None:
            return
        nm, val = tok
        if val <= 0:
            return
        if "." in nm:
            val = max(val, CHAN_N.get(nm, val))
        if self.wm.get(nm, 0) >= val:
            return
        self.wm[nm] = val
        self.ops.append(lambda e, sems, nm=nm, val=val: e.wait_ge(sems[nm], val))

    def waits(self, toks):
        best = {}
        for t in toks:
            if t is None:
                continue
            nm, val = t
            if nm not in best or val > best[nm]:
                best[nm] = val
        for nm, val in best.items():
            self.wait((nm, val))

    def emit(self, fn, inc=True, ch=None):
        key = self._ch(ch)
        if SELFWAIT and self.name in ("ve", "se") and ch is None:
            self.wait(self.tok())
        if inc:
            self.n[key] = self.n.get(key, 0) + self.step
            CHAN_N[key] = self.n[key]
            n = self.n[key]
            self.ops.append(lambda e, sems, fn=fn, key=key: fn(e).then_inc(sems[key], self.step))
            return (key, n)
        self.ops.append(lambda e, sems, fn=fn: fn(e))
        return None

    def tok(self, ch=None):
        key = self._ch(ch)
        return (key, self.n.get(key, 0))

    def bar(self):
        return [(k, v) for k, v in self.n.items()]


class Plan:
    def __init__(self):
        CHAN_N.clear()
        self.te = Stream("te", 1)
        self.ve = Stream("ve", 1)
        self.se = Stream("se", 1)
        self.qg = Stream("qg", 16)
        self.qs = Stream("qs", 16)
        self.streams = dict(te=self.te, ve=self.ve, se=self.se, qg=self.qg, qs=self.qs)

    def barrier(self):
        out = []
        for s in self.streams.values():
            out.extend(s.bar())
        return out


def prep_lhsT(w_in_out):
    K, M = w_in_out.shape
    nk, no = K // 128, M // 128
    r = w_in_out.reshape(nk, 128, no, 128).transpose(2, 1, 0, 3)
    return np.ascontiguousarray(r.reshape(no, 128, nk * 128))


def vec_oT(v):
    return np.ascontiguousarray(v.reshape(-1, 128).T)


SM = {}
_sm_off = 0
def _sm(name, w):
    global _sm_off
    SM[name] = (_sm_off, w)
    _sm_off += w
for _nm, _w in [
    ("a_bi", 8), ("a_bf", 8), ("a_bo", 8), ("a_bs", 8), ("a_gamma", 8), ("a_beta", 8),
    ("f_s1", 32), ("f_s2", 32), ("f_h1s", 32), ("f_h2s", 32), ("f_s3", 8),
    ("b_bq", 8), ("b_bk", 8), ("b_bv", 8), ("b_bo", 8), ("b_bc", 8),
    ("b_bg", 3), ("b_bs", 1), ("b_wg", 24), ("b_ws", 8),
]:
    _sm(_nm, _w)
SMALL_W = _sm_off
_sm("a_hst", 8)
_sm("b_bo_half", 8)
_sm("pglob", 16)
_sm("scr", 48)
SMALL_W_ALL = _sm_off


def host_prep(inputs):
    g = lambda k: np.asarray(inputs[k], dtype=np.float32)
    P = {}
    for nm in ["Wi", "Wf", "Wo", "Ws"]:
        P["a_" + nm] = prep_lhsT(g("L0_attn_a_" + nm))
    P["f_w1"] = prep_lhsT(g("L0_ffn_w1_w").T)
    P["f_w2"] = prep_lhsT(g("L0_ffn_w2_w").T)
    P["f_h1"] = prep_lhsT(g("L0_ffn_h1_w").T)
    P["f_h2"] = prep_lhsT(g("L0_ffn_h2_w").T)
    P["f_w3"] = prep_lhsT(g("L0_ffn_w3_w").T)
    for nm in ["Wq", "Wk", "Wv", "Wo", "Wc"]:
        P["b_" + nm] = prep_lhsT(g("L0_attn_b_" + nm))
    sp = np.zeros((128, SMALL_W), np.float32)
    def put(name, arr):
        o, w = SM[name]
        sp[:, o:o + w] = arr
    put("a_bi", vec_oT(g("L0_attn_a_bi")));  put("a_bf", vec_oT(g("L0_attn_a_bf")))
    put("a_bo", vec_oT(g("L0_attn_a_bo")));  put("a_bs", vec_oT(g("L0_attn_a_bs")))
    put("a_gamma", vec_oT(g("L0_attn_a_gamma"))); put("a_beta", vec_oT(g("L0_attn_a_beta")))
    put("f_s1", vec_oT(g("L0_ffn_w1_s")));   put("f_s2", vec_oT(g("L0_ffn_w2_s")))
    put("f_h1s", vec_oT(g("L0_ffn_h1_s")));  put("f_h2s", vec_oT(g("L0_ffn_h2_s")))
    put("f_s3", vec_oT(g("L0_ffn_w3_s")))
    put("b_bq", vec_oT(g("L0_attn_b_bq")));  put("b_bk", vec_oT(g("L0_attn_b_bk")))
    put("b_bv", vec_oT(g("L0_attn_b_bv")));  put("b_bo", vec_oT(g("L0_attn_b_bo")))
    put("b_bc", vec_oT(g("L0_attn_b_bc")))
    put("b_bg", np.broadcast_to(g("L0_attn_b_bg"), (128, 3)))
    put("b_bs", np.broadcast_to(g("L0_attn_b_bs"), (128, 1)))
    put("b_wg", g("L0_attn_b_Wg").reshape(8, 128, 3).transpose(1, 0, 2).reshape(128, 24))
    put("b_ws", g("L0_attn_b_Ws").reshape(8, 128).T)
    P["small"] = np.ascontiguousarray(sp)
    return P


class Prog:
    def __init__(self, nc, dbg=0):
        self.nc = nc
        self.dbg = dbg
        self.p = Plan()
        self.rd56 = [None, None]
        self.rdbk = {}
        self.guard = {}
        self.alloc_dram()
        self.alloc_sbuf()

    def alloc_dram(self):
        nc = self.nc
        self.d_x = nc.dram_tensor("x_in", [D, T], F32, kind="ExternalInput").ap()
        self.d_small = nc.dram_tensor("small", [128, SMALL_W], F32, kind="ExternalInput").ap()
        for nm, sh in [
            ("a_Wi", [NKD, 128, D]), ("a_Wf", [NKD, 128, D]), ("a_Wo", [NKD, 128, D]), ("a_Ws", [NKD, 128, D]),
            ("f_w1", [NKH, 128, D]), ("f_w2", [NKH, 128, D]),
            ("f_h1", [NKH, 128, DH]), ("f_h2", [NKH, 128, DH]),
            ("f_w3", [NKD, 128, DH]),
            ("b_Wq", [NKD, 128, D]), ("b_Wk", [NKD, 128, D]), ("b_Wv", [NKD, 128, D]),
            ("b_Wo", [NKD, 128, 3 * D]), ("b_Wc", [NKD, 128, DH]),
        ]:
            setattr(self, "d_" + nm, nc.dram_tensor(nm, sh, F32, kind="ExternalInput").ap())
        self.d_out = nc.dram_tensor("x_out", [D, T], F32, kind="ExternalOutput").ap()
        self.dbg_on = bool(int(os.environ.get("MK_DBG", "0")))
        if self.dbg_on:
            self.d_dbg = nc.dram_tensor("dbg", [128, 16384], F32, kind="ExternalOutput").ap()
        self.d_hbuf_a = nc.dram_tensor("hbuf_a", [NKH, 128, T], BF16, kind="Internal").ap()
        self.d_hbuf_b = nc.dram_tensor("hbuf_b", [NKH, 128, T], BF16, kind="Internal").ap()
        self.d_xres_a = nc.dram_tensor("xres_a", [NKD, 128, T], F32, kind="Internal").ap()
        self.d_wo_bf = nc.dram_tensor("wo_bf", [NKD, 128, 3 * D], BF16, kind="Internal").ap()
        self.d_qkv_bf = nc.dram_tensor("qkv_bf", [3, NKD, 128, D], BF16, kind="Internal").ap()
        self.d_xres_b = nc.dram_tensor("xres_b", [NKD, 128, T], F32, kind="Internal").ap()

    def alloc_sbuf(self):
        nc = self.nc
        base = (nc.sbuf_base + 31) & ~31
        top = nc.sbuf_top

        def at(name, shape, dtype, off_b):
            nbytes = shape[1] * {F32: 4, BF16: 2}[dtype]
            assert base + off_b + nbytes <= top, f"{name} overflows sbuf: {off_b}+{nbytes} vs {top-base}"
            return nc.alloc_sbuf_tensor_at(name, shape, dtype, offset=base + off_b)

        assert SMALL_W_ALL * 4 <= 1536
        self.wbuf = at("wbuf", [128, 8192], BF16, 0)
        self.stg = at("stg", [128, 4096], BF16, 16 * KB)
        self.small = at("small", [128, SMALL_W_ALL], F32, 24 * KB)
        self.musig = at("musig", [128, 1024], F32, 25 * KB + 512)
        self.tmp = at("tmp", [128, 3584], F32, 29 * KB + 512)
        self.iota_f = at("iota_f", [128, 512], F32, 43 * KB + 512)
        self.eye_bf = at("eye_bf", [128, 128], BF16, 45 * KB + 512)
        self.ones_bf = at("ones_bf", [128, 128], BF16, 45 * KB + 768)
        self.ones_f1 = at("ones_f1", [1, 128], F32, 46 * KB)
        self.tri_f = at("tri_f", [128, 128], F32, 46 * KB + 512)
        self.eye_f = at("eye_f", [128, 128], F32, 47 * KB)
        self.ones_f32_col = at("ones_f32_col", [128, 1], F32, 25 * KB + 352)
        self.ones_row = at("ones_row", [1, 16], F32, 25 * KB + 384)
        assert SMALL_W_ALL * 4 <= 1376
        AR = 47 * KB + 512
        self.x = [at(f"x{i}", [128, T], F32, AR + i * 8 * KB) for i in range(NKD)]
        self.ln_c_bf = at("ln_c_bf", [128, 8 * TCW], BF16, AR + 64 * KB)
        self.ln_c_f = at("ln_c_f", [128, 8 * TCW], F32, AR + 72 * KB)
        self.z_c = at("z_c", [128, 8 * TCW], F32, AR + 88 * KB)
        self.hA = [at(f"hA{k}", [128, T], BF16, AR + k * 4 * KB) for k in range(NKH)]
        self.lnF = [at(f"lnF{k}", [128, T], BF16, AR + 128 * KB + k * 4 * KB) for k in range(NKD)]
        AB = AR
        self.gn = [at(f"gn{b}", [128, T], F32, AB + b * 8 * KB) for b in range(3)]
        self.z4 = at("z4", [128, 8 * TCW], F32, AB + 0 * KB)
        self.comp_g = at("comp_g", [128, 8 * TCW], BF16, AB + 24 * KB)
        self.srow = at("srow", [1, T], F32, AB + 32 * KB)
        self.sel_g = at("sel_g", [128, 8 * TCW], BF16, AB + 32 * KB)
        self.rgsum = at("rgsum", [128, T], F32, AB + 40 * KB)
        self.ln3T = [at(f"ln3T{k}", [128, D], BF16, AB + 40 * KB + k * 2 * KB) for k in range(16)]
        self.win_g = [at(f"win_g{k}", [128, T], BF16, AB + 40 * KB + k * 4 * KB) for k in range(NKD)]
        self.x2c = at("x2c", [128, 8 * TCW], F32, AB + 72 * KB)
        self.cmp = [at(f"cmp{k}", [128, TCW], BF16, AB + 72 * KB + k * KB) for k in range(NKD)]
        self.wq = []
        for pj in range(3):
            self.wq.append([at(f"wq{pj}_{k}", [128, 384], BF16, AB + 72 * KB + (pj * 8 + k) * 768)
                            for k in range(NKD)])
        _wqb_off = [AB + i * 768 for i in range(21)] + [AB + 120 * KB + i * 768 for i in range(3)]
        self.wqB = []
        for pj in range(3):
            self.wqB.append([at(f"wqB{pj}_{k}", [128, 384], BF16, _wqb_off[pj * 8 + k])
                             for k in range(NKD)])
        self.qkv = []
        for pj in range(3):
            self.qkv.append([at(f"qkv{pj}_{k}", [128, TCW], BF16, AB + 80 * KB + pj * 8 * KB + k * KB)
                             for k in range(NKD)])
        self.rkA = at("rkA", [128, T], F32, AB + 48 * KB)
        self.x2cB = at("x2cB", [128, 8 * TCW], F32, AB + 88 * KB)
        self.stgB = at("stgB", [128, 1024], BF16, AB + 104 * KB)
        self.stgC = at("stgC", [128, 1024], BF16, AB + 106 * KB)
        self.oh = [at(f"oh{k}", [128, TCW], BF16, AB + 80 * KB + k * KB) for k in range(16)]
        self.selT = [at(f"selT{j}", [128, D], BF16, AB + 104 * KB + j * 2 * KB) for j in range(4)]
        self.vT = [at(f"vT{j}", [128, D], BF16, AB + 104 * KB + j * 2 * KB) for j in range(4)]
        self.pP = [at(f"pP{j}", [128, TCW], BF16, AB + 112 * KB + j * KB) for j in range(4)]
        self.pTt = [at(f"pT{j}", [128, TCW], BF16, AB + 116 * KB + j * KB) for j in range(4)]
        self.scb = at("scb", [128, T], F32, AB + 120 * KB)
        self.wocast = at("wocast", [128, 3 * D], BF16, AB + 120 * KB)
        self.psf = nc.alloc_psum_tensor("psf", [128, 3584], F32)
        self.psb = nc.alloc_psum_tensor("psb", [128, 1024], BF16)

    def bankap(self, i, w=TCW, off=0):
        return self.psf[:, i * TCW + off:i * TCW + off + w]

    def bank_wait(self, bank):
        return [self.guard.get(("bank", bank))] + self.rdbk.get(bank, [])

    def dump(self, ap, deps, col, ncols):
        if not self.dbg_on:
            return None
        self.p.qs.waits(deps)
        return self.p.qs.emit(lambda e, ap=ap, col=col, ncols=ncols: e.dma_start(
            out=self.d_dbg[:, col:col + ncols], in_=ap), ch="dbg")

    def dump_bf(self, ap, deps, col, ncols):
        if not self.dbg_on:
            return None
        self.p.qg.waits(deps)
        return self.p.qg.emit(lambda e, ap=ap, col=col, ncols=ncols: e.dma_start(
            out=self.d_dbg[:, col:col + ncols], in_=ap), ch="dbg")

    def sm(self, name, i=0, w=1):
        o, wd = SM[name]
        assert i + w <= wd
        return self.small[:, o + i:o + i + w]

    def setup(self):
        p = self.p
        for k in range(NKD):
            p.qs.emit(lambda e, k=k: e.dma_start(
                out=self.x[k][:], in_=self.d_x[k * 128:(k + 1) * 128, :]), ch="ld")
        p.qs.emit(lambda e: e.dma_start(out=self.small[:, 0:SMALL_W], in_=self.d_small[:, :]), ch="ld")
        ld_all = p.qs.tok("ld")
        self.tok_x = [ld_all] * NKD
        self.tok_small = ld_all
        t_io = p.qg.emit(lambda e: e.iota(self.iota_f[:], pattern=[[1, 512]], base=0,
                                          channel_multiplier=0, allow_small_or_imprecise_dtypes=True), ch="misc")
        self.tok_pglob = p.qg.emit(lambda e: e.iota(self.sm("pglob", 0, 16), pattern=[[128, 16]], base=0,
                                                    channel_multiplier=1, allow_small_or_imprecise_dtypes=True), ch="misc")
        t_pi = p.qg.emit(lambda e: e.iota(self.sm("scr", 0, 1), pattern=[[0, 1]], base=0,
                                          channel_multiplier=1, allow_small_or_imprecise_dtypes=True), ch="misc")
        self.tok_iota = t_io
        p.ve.waits([t_io, t_pi])
        t_eye_f = p.ve.emit(lambda e: e.tensor_scalar(self.tmp[:, 0:128], self.iota_f[:, 0:128],
                                                      self.sm("scr", 0, 1), None, op0=ALU.is_equal))
        p.ve.wait(t_eye_f)
        self.tok_eye = p.ve.emit(lambda e: e.tensor_copy(self.eye_bf[:], self.tmp[:, 0:128]))
        self.tok_eye_f = p.ve.emit(lambda e: e.tensor_copy(self.eye_f[:], self.tmp[:, 0:128]))
        self.tok_tri = p.ve.emit(lambda e: e.tensor_scalar(self.tri_f[:], self.iota_f[:, 0:128],
                                                           self.sm("scr", 0, 1), None, op0=ALU.is_gt))
        self.tok_ones = p.ve.emit(lambda e: e.memset(self.ones_bf[:], 1.0))
        self.tok_ones1 = p.ve.emit(lambda e: e.memset(self.ones_f1[:], 1.0))
        self.tok_ones_col = p.ve.emit(lambda e: e.memset(self.ones_f32_col[:], 1.0))
        self.tok_ones_row = p.ve.emit(lambda e: e.memset(self.ones_row[:], 1.0))
        p.ve.wait(self.tok_small)
        self.tok_bo_half = p.ve.emit(lambda e: e.tensor_scalar(self.sm("b_bo_half", 0, 8),
                                                               self.sm("b_bo", 0, 8), 0.5, None, op0=ALU.mult))
        self.tok_eps5 = p.ve.emit(lambda e: e.memset(self.sm("scr", 1, 1), LN_EPS))
        self.tok_eps6 = p.ve.emit(lambda e: e.memset(self.sm("scr", 2, 1), LN4_EPS))

    def ln_stats_chunk(self, tiles, eps, bA=5, bB=6):
        p = self.p
        tok_mmA = tok_mmB = None
        for e_i, (ap, tk) in enumerate(tiles):
            sl = self.stgB[:, (e_i % 4) * 256:(e_i % 4) * 256 + 256] if False else None
            sl = self.stgB[:, (e_i % 2) * TCW:(e_i % 2 + 1) * TCW]
            sq = self.stgC[:, (e_i % 2) * TCW:(e_i % 2 + 1) * TCW]
            p.se.waits([tk, self.guard.get(("stgBsl", e_i % 2))])
            t_c = p.se.emit(lambda e, ap=ap, sl=sl: e.activation(sl, ap, AF.Copy))
            p.qg.waits([tk, self.guard.get(("stgCsl", e_i % 2))])
            t_q = p.qg.emit(lambda e, ap=ap, sq=sq: e.tensor_tensor(sq, ap, ap, op=ALU.mult), ch="gpc")
            p.te.waits([t_c, self.tok_ones])
            if e_i == 0:
                p.te.waits(self.rdbk.get(bA, []) + self.rdbk.get(bB, [])
                           + [self.guard.get(("bank", bA)), self.guard.get(("bank", bB))])
            tok_mmA = p.te.emit(lambda e, sl=sl, e_i=e_i, bA=bA: e.matmul(
                self.bankap(bA), self.ones_bf[:], sl, start=(e_i == 0), stop=(e_i == NKD - 1)))
            self.guard[("stgBsl", e_i % 2)] = tok_mmA
            p.te.wait(t_q)
            tok_mmB = p.te.emit(lambda e, sq=sq, e_i=e_i, bB=bB: e.matmul(
                self.bankap(bB), self.ones_bf[:], sq, start=(e_i == 0), stop=(e_i == NKD - 1)))
            self.guard[("stgCsl", e_i % 2)] = tok_mmB
        self.stgB_guard = tok_mmB
        return self.ln_finalize(tok_mmA, tok_mmB, eps, bA, bB)

    def ln_finalize(self, tok_mmA, tok_mmB, eps, bA=5, bB=6):
        p = self.p
        mu = self.musig[:, 0:TCW]
        r = self.musig[:, TCW:2 * TCW]
        t6 = self.tmp[:, 6 * TCW:7 * TCW]
        p.ve.wait(tok_mmA)
        tok_mu = p.ve.emit(lambda e, bA=bA: e.tensor_scalar(mu, self.bankap(bA), 1.0 / D, None, op0=ALU.mult))
        p.ve.wait(tok_mmB)
        t_v1 = p.ve.emit(lambda e, bB=bB: e.tensor_scalar(t6, self.bankap(bB), 1.0 / D, None, op0=ALU.mult))
        p.ve.wait(tok_mu)
        t_v2 = p.ve.emit(lambda e: e.tensor_tensor(r, mu, mu, op=ALU.mult))
        p.ve.waits([t_v1, t_v2])
        t_v3 = p.ve.emit(lambda e: e.tensor_tensor(t6, t6, r, op=ALU.subtract))
        eps_ap = self.sm("scr", 1 if eps == LN_EPS else 2, 1)
        p.se.waits([t_v3, self.tok_eps5, self.tok_eps6])
        t_l = p.se.emit(lambda e: e.activation(r, t6, AF.Ln, bias=eps_ap))
        p.se.wait(t_l)
        tok_r = p.se.emit(lambda e: e.activation(r, r, AF.Exp, scale=-0.5))
        self.rdbk[bA] = [tok_mu]
        self.rdbk[bB] = [t_v1]
        self.rd56 = [tok_mu, t_v1]
        return tok_mu, tok_r

    def emit_lnF_chunk(self, tc, xtiles, xtoks, pair):
        p = self.p
        g = self.guard
        c0 = tc * TCW
        tiles = [(xtiles[k], xtoks[k]) for k in range(NKD)]
        tok_mu, tok_r = self.ln_stats_chunk(tiles, LN_EPS, *pair)
        t2 = None
        for k in range(NKD):
            sl = self.tmp[:, (k % 2) * TCW:(k % 2 + 1) * TCW]
            lb = self.lnF[k][:, c0:c0 + TCW]
            p.ve.waits([tok_mu, xtoks[k], g.get(("lntmp", k % 2)), g.get(("tmp", k % 2))])
            t1 = p.ve.emit(lambda e, sl=sl, x_sl=xtiles[k]: e.tensor_tensor(
                sl, x_sl, self.musig[:, 0:TCW], op=ALU.subtract))
            p.ve.waits([t1, tok_r])
            t2 = p.ve.emit(lambda e, lb=lb, sl=sl: e.tensor_tensor(
                lb, sl, self.musig[:, TCW:2 * TCW], op=ALU.mult))
            g[("lntmp", k % 2)] = t2
            g[("tmp", k % 2)] = t2
            g[("lnF", k, tc)] = t2

    def attn_a(self, ln_hook=None):
        p = self.p
        W = [self.d_a_Wi, self.d_a_Wf, self.d_a_Wo, self.d_a_Ws]
        BIAS = ["a_bi", "a_bf", "a_bo", "a_bs"]
        FUNC = [AF.Sigmoid, AF.Sigmoid, AF.Sigmoid, AF.Tanh]
        wtok = {}
        wbuf_user = {}
        h_upd = [None] * NKD
        self.tok_x1 = [[None] * NTC for _ in range(NKD)]
        self.tok_x1_hbm = [[None] * NTC for _ in range(NKD)]
        last_proj_mm = None
        last_zread = None
        guard = self.guard

        def issue_w(gi):
            tc, e_i, buf = gi // NKD, gi % NKD, gi % 2
            toks = []
            for pj in range(4):
                p.qg.wait(wbuf_user.get((buf, pj)))
                toks.append(p.qg.emit(lambda e, pj=pj, e_i=e_i, buf=buf: e.dma_start(
                    out=self.wbuf[:, buf * 4096 + pj * D:buf * 4096 + (pj + 1) * D],
                    in_=W[pj][e_i, :, :]), ch="wA" if buf == 0 else "wB"))
            wtok[gi] = toks

        issue_w(0)
        issue_w(1)

        for tc in range(NTC):
            c0 = tc * TCW
            tiles = [(self.x[k][:, c0:c0 + TCW], self.tok_x[k]) for k in range(NKD)]
            tok_mu, tok_r = self.ln_stats_chunk(tiles, LN_EPS)
            if tc == 0:
                self.dump(self.x[0][:, 0:TCW], [self.tok_x[0]], 13312, TCW)
                self.dump(self.musig[:], [tok_mu, tok_r], 14336, 1024)
            ln_toks = []
            for k in range(NKD):
                lf = self.ln_c_f[:, k * TCW:(k + 1) * TCW]
                lb = self.ln_c_bf[:, k * TCW:(k + 1) * TCW]
                p.ve.waits([tok_mu, self.tok_x[k]])
                x_sl = self.x[k][:, c0:c0 + TCW]
                t1 = p.ve.emit(lambda e, lf=lf, x_sl=x_sl: e.tensor_tensor(
                    lf, x_sl, self.musig[:, 0:TCW], op=ALU.subtract))
                p.ve.waits([t1, tok_r])
                t2 = p.ve.emit(lambda e, lf=lf: e.tensor_tensor(lf, lf, self.musig[:, TCW:2 * TCW], op=ALU.mult))
                p.se.waits([t2, last_proj_mm])
                t3 = p.se.emit(lambda e, lb=lb, lf=lf: e.activation(lb, lf, AF.Copy))
                ln_toks.append((t2, t3))
                if tc == 0:
                    self.dump(lf, [t2], k * TCW, TCW)

            zb_toks = []
            for e_i in range(NKD):
                gi = tc * NKD + e_i
                buf = gi % 2
                mm_tok = [None] * 4
                slots = {}
                ev = {}
                SLOT_OF = {0: 0, 3: 1, 1: 2, 2: 3}
                for pj in range(4):
                    bank = (gi * 4 + pj) % 3
                    p.te.waits(wtok[gi])
                    p.te.waits(self.bank_wait(bank))
                    for k in range(NKD):
                        p.te.wait(ln_toks[k][1])
                        tk = p.te.emit(lambda e, pj=pj, k=k, bank=bank, buf=buf: e.matmul(
                            self.bankap(bank),
                            self.wbuf[:, buf * 4096 + pj * D + k * 128:buf * 4096 + pj * D + (k + 1) * 128],
                            self.ln_c_bf[:, k * TCW:(k + 1) * TCW],
                            start=(k == 0), stop=(k == NKD - 1)), inc=(k == NKD - 1))
                        if tk is not None:
                            mm_tok[pj] = tk
                    wbuf_user[(buf, pj)] = mm_tok[pj]
                    slot = SLOT_OF[pj]
                    ap = self.tmp[:, slot * TCW:(slot + 1) * TCW]
                    p.se.waits([mm_tok[pj], guard.get(("tmp", slot))])
                    ev[pj] = p.se.emit(lambda e, ap=ap, pj=pj, bank=bank, e_i=e_i: e.activation(
                        ap, self.bankap(bank), FUNC[pj], bias=self.sm(BIAS[pj], e_i, 1)))
                    guard[("bank", bank)] = ev[pj]
                    slots[pj] = ap
                last_proj_mm = mm_tok[3]
                if gi + 2 < NTC * NKD:
                    issue_w(gi + 2)
                a_ap = self.tmp[:, 4 * TCW:5 * TCW]
                p.ve.waits([ev[0], ev[3], guard.get(("tmp", 4))])
                t_a = p.ve.emit(lambda e, a_ap=a_ap, s0=slots[0], s3=slots[3]: e.tensor_tensor(
                    a_ap, s0, s3, op=ALU.mult))
                guard[("tmp", 0)] = t_a
                guard[("tmp", 1)] = t_a
                h_ap = self.tmp[:, 5 * TCW:6 * TCW]
                p.ve.waits([ev[1], t_a, h_upd[e_i], guard.get(("tmp", 5))])
                if tc == 0:
                    t_h = p.ve.emit(lambda e, h_ap=h_ap, s1=slots[1], a_ap=a_ap: e.tensor_tensor_scan(
                        h_ap, s1, a_ap, 0.0, op0=ALU.mult, op1=ALU.add))
                else:
                    t_h = p.ve.emit(lambda e, h_ap=h_ap, s1=slots[1], a_ap=a_ap, e_i=e_i: e.tensor_tensor_scan(
                        h_ap, s1, a_ap, self.sm("a_hst", e_i, 1), op0=ALU.mult, op1=ALU.add))
                guard[("tmp", 2)] = t_h
                guard[("tmp", 4)] = t_h
                p.ve.wait(t_h)
                h_upd[e_i] = p.ve.emit(lambda e, h_ap=h_ap, e_i=e_i: e.tensor_copy(
                    self.sm("a_hst", e_i, 1), h_ap[:, TCW - 1:TCW]))
                p.ve.waits([ev[2], h_upd[e_i]])
                t_y = p.ve.emit(lambda e, h_ap=h_ap, s2=slots[2]: e.tensor_tensor(h_ap, s2, h_ap, op=ALU.mult))
                guard[("tmp", 3)] = t_y
                z_ap = self.z_c[:, e_i * TCW:(e_i + 1) * TCW]
                p.ve.waits([t_y, ln_toks[e_i][0], last_zread])
                t_z = p.ve.emit(lambda e, z_ap=z_ap, h_ap=h_ap, e_i=e_i: e.tensor_tensor(
                    z_ap, self.ln_c_f[:, e_i * TCW:(e_i + 1) * TCW], h_ap, op=ALU.add))
                guard[("tmp", 5)] = t_z
                if tc == 0 and e_i == 0:
                    d1 = self.dump(slots[0], [ev[0]], 4096, TCW)
                    d2 = self.dump(slots[3], [ev[3]], 4608, TCW)
                    d3 = self.dump(slots[1], [ev[1]], 5120, TCW)
                    d4 = self.dump(slots[2], [ev[2]], 5632, TCW)
                    d5 = self.dump(a_ap, [t_a], 6144, TCW)
                    d6 = self.dump(h_ap, [t_y], 6656, TCW)
                    for dd, sl in [(d1,0),(d2,1),(d3,2),(d4,3),(d5,4),(d6,5)]:
                        if dd is not None:
                            guard[("tmp", sl)] = dd
                zb = self.stg[:, e_i * TCW:(e_i + 1) * TCW]
                p.se.waits([t_z, guard.get(("zbslot", e_i))])
                t_zb = p.se.emit(lambda e, zb=zb, z_ap=z_ap: e.activation(zb, z_ap, AF.Copy))
                zb_toks.append((zb, z_ap, t_zb))
                guard[("stg67w", e_i)] = t_zb

            tA = tB = None
            for e_i, (zb, z_ap, t_zb) in enumerate(zb_toks):
                p.te.waits([t_zb, self.tok_ones])
                if e_i == 0:
                    p.te.waits(self.rdbk.get(5, []) + self.rdbk.get(6, []))
                tA = p.te.emit(lambda e, zb=zb, e_i=e_i: e.matmul(
                    self.bankap(5), self.ones_bf[:], zb, start=(e_i == 0), stop=(e_i == NKD - 1)))
                sq = self.tmp[:, 6 * TCW:7 * TCW]
                p.se.waits([guard.get(("sq",)), t_zb])
                t_q = p.se.emit(lambda e, sq=sq, z_ap=z_ap: e.activation(sq, z_ap, AF.Square))
                p.se.wait(tA)
                t_c2 = p.se.emit(lambda e, zb=zb, sq=sq: e.activation(zb, sq, AF.Copy))
                guard[("sq",)] = t_c2
                p.te.wait(t_c2)
                tB = p.te.emit(lambda e, zb=zb, e_i=e_i: e.matmul(
                    self.bankap(6), self.ones_bf[:], zb, start=(e_i == 0), stop=(e_i == NKD - 1)))
                guard[("zbslot", e_i)] = tB
            self.stg_guard = tB
            if tc == 0:
                self.dump(self.z_c[:], [t for (_, _, t) in zb_toks], 8192, 4096)
            tok_mu2, tok_r2 = self.ln_finalize(tA, tB, LN_EPS)
            if tc == 0:
                self.dump(self.musig[:], [tok_mu2, tok_r2], 12288, 1024)

            o_ap = self.tmp[:, 6 * TCW:7 * TCW]
            for e_i in range(NKD):
                z_ap = self.z_c[:, e_i * TCW:(e_i + 1) * TCW]
                p.ve.waits([tok_mu2, tok_r2])
                tz1 = p.ve.emit(lambda e, z_ap=z_ap, o_ap=o_ap: e.tensor_tensor(
                    o_ap, z_ap, self.musig[:, 0:TCW], op=ALU.subtract))
                p.ve.wait(tz1)
                tz2 = p.ve.emit(lambda e, o_ap=o_ap: e.tensor_tensor(
                    o_ap, o_ap, self.musig[:, TCW:2 * TCW], op=ALU.mult))
                p.ve.wait(tz2)
                tz3 = p.ve.emit(lambda e, o_ap=o_ap, e_i=e_i: e.tensor_scalar(
                    o_ap, o_ap, self.sm("a_gamma", e_i, 1), self.sm("a_beta", e_i, 1),
                    op0=ALU.mult, op1=ALU.add))
                p.ve.wait(tz3)
                x_sl = self.x[e_i][:, c0:c0 + TCW]
                tx = p.ve.emit(lambda e, o_ap=o_ap, x_sl=x_sl: e.tensor_tensor(
                    x_sl, x_sl, o_ap, op=ALU.add))
                self.tok_x1[e_i][tc] = tx
                p.qs.wait(tx)
                xr_sl = self.d_xres_a[e_i, :, c0:c0 + TCW]
                self.tok_x1_hbm[e_i][tc] = p.qs.emit(lambda e, xr_sl=xr_sl, x_sl=x_sl: e.dma_start(
                    out=xr_sl, in_=x_sl), ch="x")
            guard[("sq",)] = tx
            self.tmp6_guard = tx
            if ln_hook is not None:
                xt = [self.x[k][:, c0:c0 + TCW] for k in range(NKD)]
                xk = [self.tok_x1[k][tc] for k in range(NKD)]
                ln_hook(tc, xt, xk, (3, 4))
            last_zread = guard.get(("sq",))


    def ffn(self, x_toks, d_xres_in, d_xres_out, is_final=False, x_from_hbm=False, ln_pre=False):
        p = self.p
        bar = p.barrier()
        g = self.guard

        for tc in range(NTC) if not ln_pre else []:
            c0 = tc * TCW
            if x_from_hbm:
                for k in range(NKD):
                    xs = d_xres_in[k, :, c0:c0 + TCW]
                    dst = self.x2c[:, k * TCW:(k + 1) * TCW]
                    p.qg.waits(bar + [g.get(("fx2c", tc))])
                    p.qg.emit(lambda e, xs=xs, dst=dst: e.dma_start(out=dst, in_=xs), ch="x2")
                ld_all = p.qg.tok("x2")
                xtile = lambda k: self.x2c[:, k * TCW:(k + 1) * TCW]
                xtok = lambda k: ld_all
            else:
                xtile = lambda k: self.x[k][:, c0:c0 + TCW]
                xtok = lambda k: x_toks[k]
            tiles = [(xtile(k), xtok(k)) for k in range(NKD)]
            pair = (5, 6) if tc % 2 == 0 else (3, 4)
            tok_mu, tok_r = self.ln_stats_chunk(tiles, LN_EPS, *pair)
            t2 = None
            for k in range(NKD):
                sl = self.tmp[:, (k % 2) * TCW:(k % 2 + 1) * TCW]
                lb = self.lnF[k][:, c0:c0 + TCW]
                x_sl = xtile(k)
                p.ve.waits([tok_mu, xtok(k), g.get(("lntmp", k % 2))] + bar)
                t1 = p.ve.emit(lambda e, sl=sl, x_sl=x_sl: e.tensor_tensor(
                    sl, x_sl, self.musig[:, 0:TCW], op=ALU.subtract))
                p.ve.waits([t1, tok_r])
                t2 = p.ve.emit(lambda e, lb=lb, sl=sl: e.tensor_tensor(lb, sl, self.musig[:, TCW:2 * TCW], op=ALU.mult))
                g[("lntmp", k % 2)] = t2
                g[("lnF", k, tc)] = t2
            g[("fx2c", tc + 1)] = t2
        ln_bar = [p.ve.tok(), p.se.tok()]
        self.dump_bf(self.lnF[0][:], [g[("lnF", 0, tc)] for tc in range(NTC)], 0, T)

        g[("bank", 5)] = self.rd56[0]
        rot = [0]

        def stage_mm_tc(o, nk, wslice, rhs_fn, rhs_toks, wtoks, tc):
            bank = rot[0] % 6
            rot[0] += 1
            p.te.waits(wtoks + self.bank_wait(bank))
            last = None
            for k in range(nk):
                p.te.wait(rhs_toks(k, tc))
                last = p.te.emit(lambda e, k=k, tc=tc, bank=bank, wslice=wslice, rhs_fn=rhs_fn: e.matmul(
                    self.bankap(bank), wslice(k), rhs_fn(k, tc),
                    start=(k == 0), stop=(k == nk - 1)), inc=(k == nk - 1))
            return [(tc, bank, last)]

        def stage_mm(o, nk, wslice, rhs_fn, rhs_toks, wtoks, extra_mm_wait=None, korder=None):
            out = []
            ks = korder if korder is not None else list(range(nk))
            for tc in range(NTC):
                bank = rot[0] % 6
                rot[0] += 1
                p.te.waits(wtoks + self.bank_wait(bank))
                if extra_mm_wait:
                    p.te.waits(extra_mm_wait)
                last = None
                for i_k, k in enumerate(ks):
                    p.te.wait(rhs_toks(k, tc))
                    st_f, sp_f = (i_k == 0), (i_k == len(ks) - 1)
                    last = p.te.emit(lambda e, k=k, tc=tc, bank=bank, wslice=wslice, rhs_fn=rhs_fn,
                                     st_f=st_f, sp_f=sp_f: e.matmul(
                        self.bankap(bank), wslice(k), rhs_fn(k, tc),
                        start=st_f, stop=sp_f), inc=sp_f)
                out.append((tc, bank, last))
            return out

        wuser = {}
        s1_wtok = {}

        def s1_issue(o):
            buf = o % 2
            p.qg.waits([wuser.get((buf, 0)), wuser.get((buf, 1))] + bar)
            chn = "wA" if buf == 0 else "wB"
            tw1 = p.qg.emit(lambda e, o=o, buf=buf: e.dma_start(
                out=self.wbuf[:, buf * 4096:buf * 4096 + D], in_=self.d_f_w1[o, :, :]), ch=chn)
            tw2 = p.qg.emit(lambda e, o=o, buf=buf: e.dma_start(
                out=self.wbuf[:, buf * 4096 + D:buf * 4096 + 2 * D], in_=self.d_f_w2[o, :, :]), ch=chn)
            s1_wtok[o] = [tw1, tw2]

        s1_issue(0)
        s1_issue(1)
        hA_tok = [[None] * NTC for _ in range(NKH)]
        for o in range(NKH):
            buf = o % 2
            w1s = lambda k, buf=buf: self.wbuf[:, buf * 4096 + k * 128:buf * 4096 + (k + 1) * 128]
            w2s = lambda k, buf=buf: self.wbuf[:, buf * 4096 + D + k * 128:buf * 4096 + D + (k + 1) * 128]
            rhs = lambda k, tc: self.lnF[k][:, tc * TCW:(tc + 1) * TCW]
            rtok = lambda k, tc: g[("lnF", k, tc)]
            last1 = last2 = None
            for tc in range(NTC):
                ((_, b1, t1k),) = stage_mm_tc(o, NKD, w1s, rhs, rtok, s1_wtok[o], tc)
                ((_, b2, t2k),) = stage_mm_tc(o, NKD, w2s, rhs, rtok, s1_wtok[o], tc)
                last1, last2 = t1k, t2k
                t1s = self.tmp[:, (2 + tc % 2) * TCW:(3 + tc % 2) * TCW]
                p.se.waits([t1k, g.get(("s1tmp", tc % 2))] + bar)
                ts = p.se.emit(lambda e, t1s=t1s, b1=b1, o=o: e.activation(
                    t1s, self.bankap(b1), AF.Silu, scale=self.sm("f_s1", o, 1)))
                g[("bank", b1)] = ts
                dst = self.hA[o][:, tc * TCW:(tc + 1) * TCW]
                p.ve.waits([t2k, ts] + bar + ln_bar)
                th = p.ve.emit(lambda e, dst=dst, b2=b2, o=o, t1s=t1s: e.scalar_tensor_tensor(
                    dst, self.bankap(b2), self.sm("f_s2", o, 1), t1s, op0=ALU.mult, op1=ALU.mult))
                g[("bank", b2)] = th
                g[("s1tmp", tc % 2)] = th
                hA_tok[o][tc] = th
            wuser[(buf, 0)] = last1
            wuser[(buf, 1)] = last2
            if o + 2 < NKH:
                s1_issue(o + 2)

        self.dump_bf(self.hA[0][:], hA_tok[0], 2048, T)
        def big_stage(d_w, s_name, d_spill, in_toks, final_stage=False, d_xin=None, d_xout=None,
                      in_res=None, resident_out=False, korder=None):
            wtok = {}

            def issue(o):
                buf = o % 2
                p.qg.waits([wuser.get((buf, "big")), wuser.get((buf, 0)), wuser.get((buf, 1))] + bar)
                wtok[o] = [p.qg.emit(lambda e, o=o, buf=buf: e.dma_start(
                    out=self.wbuf[:, buf * 4096:(buf + 1) * 4096], in_=d_w[o, :, :]),
                    ch="wA" if buf == 0 else "wB")]

            issue(0)
            issue(1)
            no = NKD if final_stage else NKH
            spill_tok = [[None] * NTC for _ in range(no)]
            out_toks = [[None] * NTC for _ in range(no)]
            res_out = {}
            scale_nm = s_name
            for o in range(no):
                buf = o % 2
                ws = lambda k, buf=buf: self.wbuf[:, buf * 4096 + k * 128:buf * 4096 + (k + 1) * 128]
                if in_res:
                    rhs = lambda k, tc: (in_res[k][0](tc) if k in in_res
                                         else self.hA[k][:, tc * TCW:(tc + 1) * TCW])
                    rtok = lambda k, tc: (in_res[k][1] if k in in_res else in_toks[k][tc])
                else:
                    rhs = lambda k, tc: self.hA[k][:, tc * TCW:(tc + 1) * TCW]
                    rtok = lambda k, tc: in_toks[k][tc]
                mm = stage_mm(o, NKH, ws, rhs, rtok, wtok[o], korder=korder)
                wuser[(buf, "big")] = mm[-1][2]
                if o + 2 < no:
                    issue(o + 2)
                for tc, bank, mtk in mm:
                    if not final_stage:
                        if resident_out and o >= 24:
                            rdst = self.lnF[o - 24][:, tc * TCW:(tc + 1) * TCW]
                            p.se.waits([mtk, g.get(("resout", o, tc))] + bar)
                            tev = p.se.emit(lambda e, rdst=rdst, bank=bank, o=o: e.activation(
                                rdst, self.bankap(bank), AF.Gelu, scale=self.sm(scale_nm, o, 1)))
                            g[("bank", bank)] = tev
                            res_out[(o, tc)] = tev
                            spill_tok[o][tc] = tev
                            out_toks[o][tc] = tev
                            continue
                        slot = (o * NTC + tc) % 4
                        sdst = self.stg[:, slot * TCW:(slot + 1) * TCW]
                        p.se.waits([mtk, g.get(("stgslot", slot))] + bar)
                        tev = p.se.emit(lambda e, sdst=sdst, bank=bank, o=o: e.activation(
                            sdst, self.bankap(bank), AF.Gelu, scale=self.sm(scale_nm, o, 1)))
                        g[("bank", bank)] = tev
                        dsp = d_spill[o, :, tc * TCW:(tc + 1) * TCW]
                        p.qs.wait(tev)
                        tsp = p.qs.emit(lambda e, dsp=dsp, sdst=sdst: e.dma_start(out=dsp, in_=sdst), ch="sp")
                        g[("stgslot", slot)] = tsp
                        spill_tok[o][tc] = tsp
                    else:
                        slot = (o * NTC + tc) % 2
                        xo = self.tmp[:, slot * TCW:(slot + 1) * TCW]
                        xsrc = d_xin[o, :, tc * TCW:(tc + 1) * TCW]
                        p.qg.waits([g.get(("xold", slot))] + bar)
                        txo = p.qg.emit(lambda e, xo=xo, xsrc=xsrc: e.dma_start(out=xo, in_=xsrc),
                                        ch="xo0" if slot == 0 else "xo1")
                        dst = self.tmp[:, (2 + slot) * TCW:(3 + slot) * TCW]
                        p.ve.waits([mtk, txo, g.get(("xnew", slot))] + bar)
                        tx = p.ve.emit(lambda e, dst=dst, bank=bank, o=o, xo=xo: e.scalar_tensor_tensor(
                            dst, self.bankap(bank), self.sm("f_s3", o, 1), xo, op0=ALU.mult, op1=ALU.add))
                        g[("bank", bank)] = tx
                        g[("xold", slot)] = tx
                        p.qs.wait(tx)
                        if is_final:
                            dd = self.d_out[o * 128:(o + 1) * 128, tc * TCW:(tc + 1) * TCW]
                        else:
                            dd = d_xout[o, :, tc * TCW:(tc + 1) * TCW]
                        tw = p.qs.emit(lambda e, dd=dd, dst=dst: e.dma_start(out=dd, in_=dst), ch="out")
                        g[("xnew", slot)] = tw
                        out_toks[o][tc] = (tx, tw)
            return spill_tok, out_toks, mm[-1][2]

        def chased_reload(d_src, last_mm, sp_toks, n_tiles, phase_tag):
            toks = [None] * n_tiles
            grp = (n_tiles + 3) // 4
            for gi4 in range(4):
                ks = range(gi4 * grp, min((gi4 + 1) * grp, n_tiles))
                q = p.qs if gi4 % 2 == 0 else p.qg
                chn = f"rl{phase_tag}{gi4}"
                for k in ks:
                    q.waits([last_mm, p.qs.tok("sp")])
                    q.emit(lambda e, k=k: e.dma_start(out=self.hA[k][:], in_=d_src[k, :, :]), ch=chn)
                t = q.tok(chn)
                for k in ks:
                    toks[k] = t
            return toks

        KORD = list(range(24, NKH)) + list(range(24))
        sp_a, s2out, s2_last = big_stage(self.d_f_h1, "f_h1s", self.d_hbuf_a, hA_tok,
                                         resident_out=True)
        rel = chased_reload(self.d_hbuf_a, s2_last, None, 24, "a" + str(is_final))
        rel_tok = [[rel[k]] * NTC if k < 24 else None for k in range(24)]
        in_res2 = {k: ((lambda tc, kk=k: self.lnF[kk - 24][:, tc * TCW:(tc + 1) * TCW]), s2out[k][NTC - 1])
                   for k in range(24, NKH)}
        rel_full = [[rel[k]] * NTC for k in range(24)] + [[None] * NTC for _ in range(8)]
        sp_b, _, s3_last = big_stage(self.d_f_h2, "f_h2s", self.d_hbuf_b, rel_full,
                                     in_res=in_res2, korder=KORD)
        rel2 = chased_reload(self.d_hbuf_b, s3_last, None, NKH, "b" + str(is_final))
        rel2_full = [[rel2[k]] * NTC for k in range(NKH)]
        _, out_toks, _ = big_stage(self.d_f_w3, "f_s3", None, rel2_full, final_stage=True,
                                   d_xin=d_xres_in, d_xout=d_xres_out)
        new_hbm_toks = [[out_toks[k][tc][1] for tc in range(NTC)] for k in range(NKD)]
        return None, new_hbm_toks


    def attn_b(self, x2_hbm_tok, is_final=False, ln_hook=None):
        p = self.p
        g = self.guard
        bar = p.barrier() + [x2_hbm_tok]
        SC = ATTN_SCALE

        def scr(i, w=1):
            return self.sm("scr", i, w)

        sc_ev = None
        for tc in range(NTC):
            c0 = tc * TCW
            ld_toks = []
            for k in range(NKD):
                xs = self.d_xres_b[k, :, c0:c0 + TCW]
                dst = self.x2c[:, k * TCW:(k + 1) * TCW]
                p.qg.waits(bar + [g.get(("x2c", tc))])
                ld_toks.append(p.qg.emit(lambda e, xs=xs, dst=dst: e.dma_start(out=dst, in_=xs), ch="x2"))
            ld_all = p.qg.tok("x2")
            tiles = [(self.x2c[:, k * TCW:(k + 1) * TCW], ld_all) for k in range(NKD)]
            pair = (5, 6) if tc % 2 == 0 else (3, 4)
            tok_mu, tok_r = self.ln_stats_chunk(tiles, LN_EPS, *pair)
            if tc == 0:
                p.te.waits([self.tok_small, self.tok_ones_col, g.get(("bank", 4))])
                for k in range(NKD):
                    t_sw = p.te.emit(lambda e, k=k: e.matmul(
                        self.psf[0:1, 4 * TCW:4 * TCW + 1], self.sm("b_ws", k, 1),
                        self.ones_f32_col[:, 0:1], start=(k == 0), stop=(k == NKD - 1)),
                        inc=(k == NKD - 1))
                p.se.wait(t_sw)
                self.tok_sws = p.se.emit(lambda e: e.activation(
                    self.small[0:1, SM["scr"][0] + 16:SM["scr"][0] + 17],
                    self.psf[0:1, 4 * TCW:4 * TCW + 1], AF.Copy))
                g[("bank", 4)] = self.tok_sws
            for k in range(NKD):
                sl = self.tmp[:, (k % 2) * TCW:(k % 2 + 1) * TCW]
                lb = self.lnF[k][:, c0:c0 + TCW]
                x_sl = self.x2c[:, k * TCW:(k + 1) * TCW]
                p.ve.waits([tok_mu, ld_all, g.get(("lntmp", k % 2))] + bar)
                t1 = p.ve.emit(lambda e, sl=sl, x_sl=x_sl: e.tensor_tensor(
                    sl, x_sl, self.musig[:, 0:TCW], op=ALU.subtract))
                p.ve.waits([t1, tok_r])
                t2 = p.ve.emit(lambda e, lb=lb, sl=sl: e.tensor_tensor(lb, sl, self.musig[:, TCW:2 * TCW], op=ALU.mult))
                g[("lnF3", k, tc)] = t2
                g[("lntmp", k % 2)] = t2
                p.te.waits([ld_all, self.tok_small] + (self.bank_wait(4) if k == 0 else []))
                tmm = p.te.emit(lambda e, x_sl=x_sl, k=k: e.matmul(
                    self.psf[0:1, 4 * TCW:5 * TCW], self.sm("b_ws", k, 1), x_sl,
                    start=(k == 0), stop=(k == NKD - 1)))
            srow_sl = self.srow[0:1, c0:c0 + TCW]
            rowt = self.tmp[0:1, 6 * TCW:7 * TCW]
            mu_row = self.musig[0:1, 0:TCW]
            r_row = self.musig[0:1, TCW:2 * TCW]
            p.ve.waits([tmm, tok_mu, tok_r, self.tok_sws, g.get(("srow", tc))])
            tv1 = p.ve.emit(lambda e, srow_sl=srow_sl, r_row=r_row: e.tensor_tensor(
                srow_sl, self.psf[0:1, 4 * TCW:5 * TCW], r_row, op=ALU.mult))
            tv2 = p.ve.emit(lambda e, rowt=rowt, mu_row=mu_row, r_row=r_row: e.tensor_tensor(
                rowt, mu_row, r_row, op=ALU.mult))
            p.ve.wait(tv2)
            tv3 = p.ve.emit(lambda e, rowt=rowt: e.tensor_scalar(
                rowt, rowt, self.small[0:1, SM["scr"][0] + 16:SM["scr"][0] + 17], None, op0=ALU.mult))
            p.ve.waits([tv1, tv3])
            tv4 = p.ve.emit(lambda e, srow_sl=srow_sl, rowt=rowt: e.tensor_tensor(
                srow_sl, srow_sl, rowt, op=ALU.subtract))
            p.ve.wait(tv4)
            sc_ev = p.ve.emit(lambda e, srow_sl=srow_sl: e.tensor_scalar(
                srow_sl, srow_sl, self.small[0:1, SM["b_bs"][0]:SM["b_bs"][0] + 1], None, op0=ALU.add))
            g[("bank", 4)] = sc_ev
            g[("x2c", tc + 1)] = sc_ev
        self.dump_bf(self.lnF[0][:, 0:512], [g[("lnF3", 0, 0)]], 1536, 512)
        self.dump(self.x2c[:, 0:512], [p.te.tok()], 15456, 512)
        if self.dbg_on:
            self.p.qs.waits([sc_ev])
            self.p.qs.emit(lambda e: e.dma_start(out=self.d_dbg[0:1, 15968:16384], in_=self.srow[0:1, 0:416]), ch="dbg")
        ln3_done = [p.ve.tok(), p.se.tok(), p.te.tok()]

        wt_toks = []
        p.ve.waits([self.tok_ones, self.tok_small] + bar + [g.get(("bank", 0))])
        for b3 in range(3):
            for k in range(NKD):
                col = (b3 * NKD + k) * 128
                wcol = self.sm("b_wg", k * 3 + b3, 1)
                p.ve.waits([g.get(("wtg",))])
                wt_toks.append(p.ve.emit(lambda e, col=col, wcol=wcol: e.tensor_scalar(
                    self.wbuf[:, col:col + 128], self.ones_bf[:], wcol, None, op0=ALU.mult)))
        for b3 in range(3):
            for tc in range(NTC):
                bank = (b3 * NTC + tc) % 5
                p.te.waits(wt_toks + self.bank_wait(bank))
                for k in range(NKD):
                    p.te.wait(g[("lnF3", k, tc)])
                    tmm = p.te.emit(lambda e, b3=b3, k=k, tc=tc, bank=bank: e.matmul(
                        self.bankap(bank),
                        self.wbuf[:, (b3 * NKD + k) * 128:(b3 * NKD + k + 1) * 128],
                        self.lnF[k][:, tc * TCW:(tc + 1) * TCW],
                        start=(k == 0), stop=(k == NKD - 1)), inc=(k == NKD - 1))
                g[("wtg",)] = tmm
                dst = self.gn[b3][:, tc * TCW:(tc + 1) * TCW]
                p.se.waits([tmm] + bar)
                tev = p.se.emit(lambda e, dst=dst, bank=bank, b3=b3: e.activation(
                    dst, self.bankap(bank), AF.Sigmoid, bias=self.sm("b_bg", b3, 1)))
                g[("bank", bank)] = tev
                g[("gn", b3, tc)] = tev
        gn_toks = [g[("gn", b3, tc)] for b3 in range(3) for tc in range(NTC)]
        p.ve.waits(gn_toks + bar)
        ts1 = p.ve.emit(lambda e: e.tensor_tensor(self.rgsum[:], self.gn[0][:], self.gn[1][:], op=ALU.add))
        p.ve.wait(ts1)
        ts2 = p.ve.emit(lambda e: e.tensor_tensor(self.rgsum[:], self.rgsum[:], self.gn[2][:], op=ALU.add))
        p.ve.wait(ts2)
        ts3 = p.ve.emit(lambda e: e.tensor_scalar(self.rgsum[:], self.rgsum[:], 1e-6, None, op0=ALU.add))
        p.ve.wait(ts3)
        ts4 = p.ve.emit(lambda e: e.reciprocal(self.rgsum[:], self.rgsum[:]))
        gtok = []
        for b3 in range(3):
            p.ve.wait(ts4)
            gtok.append(p.ve.emit(lambda e, b3=b3: e.tensor_tensor(
                self.gn[b3][:], self.gn[b3][:], self.rgsum[:], op=ALU.mult)))
        self.gn_tok = gtok

        self.dump(self.gn[0][:], gtok, 13408, 2048)
        qkv_wuser = {}
        self.qkv_cached = getattr(self, "qkv_cached", False)

        def load_qkv_w(od, buf, extra=()):
            chn = "wA" if buf == 0 else "wB"
            toks = []
            p.qg.waits([qkv_wuser.get(buf)] + list(extra) + bar)
            if self.qkv_cached:
                p.qs.waits([qkv_wuser.get(buf)] + list(extra) + bar)
                for pj in range(3):
                    toks.append(p.qs.emit(lambda e, pj=pj, od=od, buf=buf: e.dma_start(
                        out=self.wbuf[:, buf * 4096 + pj * D:buf * 4096 + (pj + 1) * D],
                        in_=self.d_qkv_bf[pj, od, :, :]), ch="wC0" if buf == 0 else "wC1"))
                return toks
            for pj, dw in enumerate([self.d_b_Wq, self.d_b_Wk, self.d_b_Wv]):
                p.qg.emit(lambda e, pj=pj, dw=dw, od=od, buf=buf: e.dma_start(
                    out=self.wbuf[:, buf * 4096 + pj * D:buf * 4096 + (pj + 1) * D],
                    in_=dw[od, :, :]), ch=chn)
            t_all = p.qg.tok(chn)
            p.qs.wait(t_all)
            for pj in range(3):
                ts_ = p.qs.emit(lambda e, pj=pj, od=od, buf=buf: e.dma_start(
                    out=self.d_qkv_bf[pj, od, :, :],
                    in_=self.wbuf[:, buf * 4096 + pj * D:buf * 4096 + (pj + 1) * D]), ch="wS0" if buf == 0 else "wS1")
                toks.append(ts_)
            return toks

        def qkv_project(src_tiles, src_toks, width, dst, bias_names=("b_bq", "b_bk", "b_bv"),
                        wtg_guard=None):
            wt = {0: load_qkv_w(0, 0, extra=(wtg_guard,)), 1: load_qkv_w(1, 1, extra=(wtg_guard,))}
            ev_toks = [[None] * NKD for _ in range(3)]
            for od in range(NKD):
                buf = od % 2
                lastmm = None
                for pj in range(3):
                    bank = (od * 3 + pj) % 5
                    p.te.waits(wt[od] + self.bank_wait(bank))
                    for k in range(NKD):
                        st_ = src_toks[k]
                        p.te.waits(st_ if isinstance(st_, list) else [st_])
                        lastmm = p.te.emit(lambda e, pj=pj, k=k, bank=bank, buf=buf: e.matmul(
                            self.bankap(bank, w=width),
                            self.wbuf[:, buf * 4096 + pj * D + k * 128:buf * 4096 + pj * D + (k + 1) * 128],
                            src_tiles[k], start=(k == 0), stop=(k == NKD - 1)), inc=(k == NKD - 1))
                    dstap = dst[pj][od][:]
                    p.se.waits([lastmm] + bar)
                    tev = p.se.emit(lambda e, dstap=dstap, bank=bank, pj=pj, od=od, bias_names=bias_names:
                                    e.activation(dstap, self.bankap(bank, w=width), AF.Identity,
                                                 bias=self.sm(bias_names[pj], od, 1)))
                    g[("bank", bank)] = tev
                    ev_toks[pj][od] = tev
                qkv_wuser[buf] = lastmm
                if od + 2 < NKD:
                    wt[od + 2] = load_qkv_w(od + 2, (od + 2) % 2)
            return ev_toks

        def attend(qt_n, kw, q_tiles, k_tiles, v_tiles, qkv_toks, out_fn, gslice, gtok):
            p_toks = []
            for qt in range(qt_n):
                bank = qt % 5
                p.te.waits(self.bank_wait(bank))
                last = None
                for dk in range(NKD):
                    p.te.waits([qkv_toks[0][dk], qkv_toks[1][dk]])
                    last = p.te.emit(lambda e, dk=dk, qt=qt, bank=bank: e.matmul(
                        self.bankap(bank, w=kw),
                        q_tiles[dk][:, qt * 128:(qt + 1) * 128],
                        k_tiles[dk][:, 0:kw], start=(dk == 0), stop=(dk == NKD - 1)),
                        inc=(dk == NKD - 1))
                rmax = scr(4 + qt, 1)
                p.ve.waits([last, g.get(("smx", qt))])
                t_m = p.ve.emit(lambda e, rmax=rmax, bank=bank, kw=kw: e.reduce_max(
                    rmax, self.bankap(bank, w=kw), axis=AX))
                p.ve.wait(t_m)
                t_b = p.ve.emit(lambda e, rmax=rmax, qt=qt: e.tensor_scalar(
                    scr(8 + qt, 1), rmax, -SC, None, op0=ALU.mult))
                pdst = self.pP[qt][:, 0:kw]
                p.se.waits([t_b, g.get(("pP", qt))] + bar)
                t_e = p.se.emit(lambda e, pdst=pdst, bank=bank, qt=qt, kw=kw: e.activation(
                    pdst, self.bankap(bank, w=kw), AF.Exp, bias=scr(8 + qt, 1), scale=SC,
                    accum_out=scr(12 + qt, 1)))
                g[("bank", bank)] = t_e
                p.ve.wait(t_e)
                t_r = p.ve.emit(lambda e, qt=qt: e.reciprocal(scr(8 + qt, 1), scr(12 + qt, 1)))
                p.ve.wait(t_r)
                t_n = p.ve.emit(lambda e, pdst=pdst, qt=qt: e.tensor_scalar(
                    pdst, pdst, scr(8 + qt, 1), None, op0=ALU.mult))
                g[("smx", qt)] = t_n
                p_toks.append(t_n)
            jn = kw // 128
            tr_toks = []
            ev_ps = g.get(("psb",))
            ntr = 0
            pend = []
            pT_tok = [None] * jn
            for qt in range(qt_n):
                for j in range(jn):
                    p.te.waits([p_toks[qt], self.tok_eye, ev_ps])
                    ttr = p.te.emit(lambda e, qt=qt, j=j, ntr=ntr: e.transpose(
                        self.psb[:, (ntr % 8) * 128:(ntr % 8 + 1) * 128],
                        self.pP[qt][:, j * 128:(j + 1) * 128], self.eye_bf[:]))
                    pend.append((ttr, qt, j, ntr % 8))
                    ntr += 1
                    if ntr % 8 == 0 or (qt == qt_n - 1 and j == jn - 1):
                        batch_last = pend[-1][0]
                        for (tt, q2, j2, s2) in pend:
                            dstp = self.pTt[j2][:, q2 * 128:(q2 + 1) * 128]
                            p.se.waits([batch_last, g.get(("pT", j2))] + bar)
                            tev = p.se.emit(lambda e, dstp=dstp, s2=s2: e.activation(
                                dstp, self.psb[:, s2 * 128:(s2 + 1) * 128], AF.Copy))
                            pT_tok[j2] = tev
                        ev_ps = tev
                        pend = []
            for qt in range(qt_n):
                g[("pP", qt)] = ev_ps
            g[("psb",)] = ev_ps
            vT_tok = [None] * jn
            pend = []
            ntr = 0
            for dv in range(NKD):
                for j in range(jn):
                    p.te.waits([qkv_toks[2][dv], self.tok_eye, ev_ps])
                    ttr = p.te.emit(lambda e, dv=dv, j=j, ntr=ntr: e.transpose(
                        self.psb[:, (ntr % 8) * 128:(ntr % 8 + 1) * 128],
                        v_tiles[dv][:, j * 128:(j + 1) * 128], self.eye_bf[:]))
                    pend.append((ttr, dv, j, ntr % 8))
                    ntr += 1
                    if ntr % 8 == 0 or (dv == NKD - 1 and j == jn - 1):
                        batch_last = pend[-1][0]
                        for (tt, d2, j2, s2) in pend:
                            dstv = self.vT[j2][:, d2 * 128:(d2 + 1) * 128]
                            p.se.waits([batch_last, g.get(("vT", j2))] + bar)
                            tev = p.se.emit(lambda e, dstv=dstv, s2=s2: e.activation(
                                dstv, self.psb[:, s2 * 128:(s2 + 1) * 128], AF.Copy))
                            vT_tok[j2] = tev
                        ev_ps = tev
                        pend = []
            g[("psb",)] = ev_ps
            out_toks = []
            qw = qt_n * 128
            for dv in range(NKD):
                bank = dv % 5
                p.te.waits(self.bank_wait(bank))
                last = None
                for j in range(jn):
                    p.te.waits([vT_tok[j], pT_tok[j]])
                    last = p.te.emit(lambda e, dv=dv, j=j, bank=bank, qw=qw: e.matmul(
                        self.bankap(bank, w=qw),
                        self.vT[j][:, dv * 128:(dv + 1) * 128],
                        self.pTt[j][:, 0:qw], start=(j == 0), stop=(j == jn - 1)),
                        inc=(j == jn - 1))
                dst, dguard = out_fn(dv)
                p.ve.waits([last, gtok, dguard] + bar)
                tev = p.ve.emit(lambda e, dst=dst, bank=bank, gslice=gslice, qw=qw: e.tensor_tensor(
                    dst, self.bankap(bank, w=qw), gslice, op=ALU.mult))
                g[("bank", bank)] = tev
                g[("pT", None)] = tev
                out_toks.append(tev)
            for j in range(jn):
                g[("pT", j)] = out_toks[-1]
                g[("vT", j)] = out_toks[-1]
            return out_toks

        for tc in range(NTC):
            bank = tc % 5
            p.te.waits([sc_ev, self.tok_ones1] + self.bank_wait(bank))
            tb = p.te.emit(lambda e, tc=tc, bank=bank: e.matmul(
                self.bankap(bank), self.ones_f1[0:1, :],
                self.srow[0:1, tc * TCW:(tc + 1) * TCW], start=True, stop=True))
            dst = self.scb[:, tc * TCW:(tc + 1) * TCW]
            p.ve.waits([tb] + bar)
            tev = p.ve.emit(lambda e, dst=dst, bank=bank: e.tensor_copy(dst, self.bankap(bank)))
            g[("bank", bank)] = tev
            g[("scb", tc)] = tev
        p.te.waits([sc_ev, self.tok_eye_f] + self.bank_wait(5) + self.bank_wait(6))
        ttrs = []
        for k in range(16):
            ttrs.append(p.te.emit(lambda e, k=k: e.transpose(
                self.psf[:, 5 * TCW + k:5 * TCW + k + 1],
                self.srow[0:1, k * 128:(k + 1) * 128], self.ones_f1[0:1, 0:1])))
        p.se.waits(ttrs + bar)
        t_sT = p.se.emit(lambda e: e.activation(self.musig[:, 0:16], self.psf[:, 5 * TCW:5 * TCW + 16], AF.Copy))
        self.rdbk[5] = [t_sT]
        rkA = self.rkA[:]
        rk_dep = [t_sT] + [g[("scb", tc)] for tc in range(NTC)] + bar
        for k in range(16):
            p.ve.waits(rk_dep + [g.get(("rk",))])
            t1 = p.ve.emit(lambda e, k=k: e.tensor_scalar(
                rkA, self.scb[:], self.musig[:, k:k + 1], 0.0, op0=ALU.is_gt, op1=ALU.add,
                accum_out=self.musig[:, 64 + k:65 + k]))
            g[("rk",)] = t1
        p.ve.wait(g[("rk",)])
        tm = p.ve.emit(lambda e: e.tensor_scalar(self.musig[:, 16:32], self.musig[:, 64:80],
                                                 512.0, None, op0=ALU.is_lt))
        wc_user = None
        wc_tok = {}

        def wc_issue(od):
            buf = od % 2
            chn = "wA" if buf == 0 else "wB"
            p.qg.waits([wc_user, g.get(("wtg",))] + bar)
            wc_tok[od] = [p.qg.emit(lambda e, od=od, buf=buf: e.dma_start(
                out=self.wbuf[:, buf * 4096:(buf + 1) * 4096], in_=self.d_b_Wc[od, :, :]), ch=chn)]

        wc_issue(0)
        wc_issue(1)
        cmp_toks = [None] * NKD
        for od in range(NKD):
            buf = od % 2
            bank = od % 5
            p.te.waits(wc_tok[od] + self.bank_wait(bank) + [g.get(("wtg",))])
            last = None
            for kk in range(NKH):
                u, c = kk // NKD, kk % NKD
                p.te.wait(g[("lnF3", c, 0)])
                p.te.wait(g[("lnF3", c, 1)])
                p.te.wait(g[("lnF3", c, 2)])
                p.te.wait(g[("lnF3", c, 3)])
                rhs = self.lnF[c][:, u::4]
                last = p.te.emit(lambda e, kk=kk, bank=bank, buf=buf, rhs=rhs: e.matmul(
                    self.bankap(bank), self.wbuf[:, buf * 4096 + kk * 128:buf * 4096 + (kk + 1) * 128],
                    rhs, start=(kk == 0), stop=(kk == NKH - 1)), inc=(kk == NKH - 1))
            wc_user = last
            if od + 2 < NKD:
                wc_issue(od + 2)
            dst = self.cmp[od][:]
            p.se.waits([last] + bar)
            tev = p.se.emit(lambda e, dst=dst, bank=bank, od=od: e.activation(
                dst, self.bankap(bank), AF.Identity, bias=self.sm("b_bc", od, 1)))
            g[("bank", bank)] = tev
            cmp_toks[od] = tev
        self.dump_bf(self.cmp[0][:], cmp_toks, 2048, TCW)
        qkv_toks = qkv_project([self.cmp[k][:] for k in range(NKD)], cmp_toks, TCW,
                               self.qkv, wtg_guard=wc_user)
        self.qkv_cached = True
        self.dump_bf(self.qkv[0][0][:], [qkv_toks[0][0]], 2560, TCW)
        q_t = [self.qkv[0][k] for k in range(NKD)]
        k_t = [self.qkv[1][k] for k in range(NKD)]
        v_t = [self.qkv[2][k] for k in range(NKD)]

        def comp_out(dv):
            return self.comp_g[:, dv * TCW:(dv + 1) * TCW], None
        comp_done = attend(4, TCW, q_t, k_t, v_t, qkv_toks, comp_out,
                           self.gn[0][:, 0:TCW], gtok[0])

        self.dump_bf(self.comp_g[:], comp_done, 3072, 8 * TCW)
        p.ve.wait(tm)
        p.te.waits([tm, self.tok_tri] + self.bank_wait(5))
        t_pr = p.te.emit(lambda e: e.matmul(self.bankap(5, w=16), self.tri_f[:],
                                            self.musig[:, 16:32], start=True, stop=True))
        p.te.waits([tm, self.tok_ones1])
        t_cs = p.te.emit(lambda e: e.matmul(self.psf[0:1, 6 * TCW:6 * TCW + 16], self.ones_f1[0:1, :].transpose([0, 1]) if False else self.ones_f32_col[:, 0:1],
                                            self.musig[:, 16:32], start=True, stop=True))
        p.se.waits([t_pr] + bar)
        t_pre = p.se.emit(lambda e: e.activation(self.musig[:, 48:64], self.bankap(5, w=16), AF.Copy))
        self.rdbk[5] = [t_pre]
        p.se.wait(t_cs)
        t_cse = p.se.emit(lambda e: e.activation(self.musig[0:1, 96:112], self.psf[0:1, 6 * TCW:6 * TCW + 16], AF.Copy))
        p.ve.waits([t_cse, self.tok_ones_row])
        t_sc1 = p.ve.emit(lambda e: e.tensor_tensor_scan(
            self.musig[0:1, 112:128], self.ones_row[0:1, 0:16], self.musig[0:1, 96:112],
            0.0, op0=ALU.mult, op1=ALU.add))
        p.ve.wait(t_sc1)
        t_sc2 = p.ve.emit(lambda e: e.tensor_tensor(self.musig[0:1, 112:128], self.musig[0:1, 112:128],
                                                    self.musig[0:1, 96:112], op=ALU.subtract))
        p.te.waits([t_sc2, self.tok_ones1] + self.bank_wait(6))
        t_bc = p.te.emit(lambda e: e.matmul(self.bankap(6, w=16), self.ones_f1[0:1, :],
                                            self.musig[0:1, 112:128], start=True, stop=True))
        p.ve.waits([t_bc, t_pre])
        t_slot = p.ve.emit(lambda e: e.tensor_tensor(self.musig[:, 32:48], self.musig[:, 48:64],
                                                     self.bankap(6, w=16), op=ALU.add))
        self.rdbk[6] = [t_slot]
        self.dump(self.musig[:, 0:96], [t_slot, tm], 7168, 96)
        oh_toks = []
        for k in range(16):
            p.ve.waits([t_slot, self.tok_iota, comp_done[-1], g.get(("oh", k))])
            oh_toks.append(p.ve.emit(lambda e, k=k: e.tensor_scalar(
                self.oh[k][:], self.iota_f[:], self.musig[:, 32 + k:33 + k],
                self.musig[:, 16 + k:17 + k], op0=ALU.is_equal, op1=ALU.mult)))
        ln3T_tok = [None] * 16
        ev_ps = g.get(("psb",))
        ln3T_extra = list(gtok) + [g.get(("rk",)), tm]
        for tt in range(16):
            trs = []
            for c in range(NKD):
                p.te.waits([g[("lnF3", c, tt // 4)], self.tok_eye, ev_ps])
                trs.append(p.te.emit(lambda e, tt=tt, c=c: e.transpose(
                    self.psb[:, c * 128:(c + 1) * 128],
                    self.lnF[c][:, tt * 128:(tt + 1) * 128], self.eye_bf[:])))
            p.se.waits(trs + [g.get(("ln3T", tt))] + ln3T_extra + bar)
            ln3T_tok[tt] = p.se.emit(lambda e, tt=tt: e.activation(
                self.ln3T[tt][:], self.psb[:], AF.Copy))
            ev_ps = ln3T_tok[tt]
        g[("psb",)] = ev_ps
        selT_tok = [None] * 4
        for rt in range(4):
            for half in range(2):
                bank = (rt * 2 + half) % 5
                p.te.waits(self.bank_wait(bank))
                last = None
                for k in range(16):
                    p.te.waits([oh_toks[k], ln3T_tok[k]])
                    last = p.te.emit(lambda e, rt=rt, half=half, k=k, bank=bank: e.matmul(
                        self.bankap(bank),
                        self.oh[k][:, rt * 128:(rt + 1) * 128],
                        self.ln3T[k][:, half * TCW:(half + 1) * TCW],
                        start=(k == 0), stop=(k == 15)), inc=(k == 15))
                dst = self.selT[rt][:, half * TCW:(half + 1) * TCW]
                p.se.waits([last] + bar)
                tev = p.se.emit(lambda e, dst=dst, bank=bank: e.activation(
                    dst, self.bankap(bank), AF.Copy))
                g[("bank", bank)] = tev
                selT_tok[rt] = tev
        sel_tok = [None] * NKD
        ev_ps = g.get(("psb",))
        for ct in range(NKD):
            trs = []
            for rt in range(4):
                p.te.waits([selT_tok[rt], self.tok_eye, ev_ps, g.get(("rk2",))])
                trs.append(p.te.emit(lambda e, ct=ct, rt=rt: e.transpose(
                    self.psb[:, rt * 128:(rt + 1) * 128],
                    self.selT[rt][:, ct * 128:(ct + 1) * 128], self.eye_bf[:])))
            p.se.waits(trs + [cmp_toks[ct], g.get(("selc", ct))] + bar)
            sel_tok[ct] = p.se.emit(lambda e, ct=ct: e.activation(
                self.cmp[ct][:], self.psb[:, 0:TCW], AF.Copy))
            ev_ps = sel_tok[ct]
        g[("psb",)] = ev_ps
        qkv_toks2 = qkv_project([self.cmp[k][:] for k in range(NKD)], sel_tok, TCW,
                                self.qkv, wtg_guard=selT_tok[3])

        def sel_out(dv):
            return self.sel_g[:, dv * TCW:(dv + 1) * TCW], None
        sel_done = attend(4, TCW, [self.qkv[0][k] for k in range(NKD)],
                          [self.qkv[1][k] for k in range(NKD)],
                          [self.qkv[2][k] for k in range(NKD)], qkv_toks2, sel_out,
                          self.gn[1][:, 0:TCW], gtok[1])

        self.dump_bf(self.sel_g[:], sel_done, 7264, 8 * TCW)
        self.dump(self.scb[:, 0:1536], [g[("scb", tc)] for tc in range(NTC)] + [t_sT], 0, 1536)
        woc = self.scb[:, 0:1536]
        woc_bf = self.nc.alloc_sbuf_tensor_at("woc_bf", [128, 3 * D], BF16,
                                              offset=self.nc.lookup_mloc(self.scb).addr - 0 + 0)             if False else None
        prev_st = None
        rank_done = g.get(("rk",))
        for ot in range(NKD):
            p.qg.waits([prev_st, rank_done, t_slot] + bar)
            tl = p.qg.emit(lambda e, ot=ot: e.dma_start(
                out=self.wocast[:], in_=self.d_b_Wo[ot, :, :]), ch="woL")
            p.qs.wait(tl)
            prev_st = p.qs.emit(lambda e, ot=ot: e.dma_start(
                out=self.d_wo_bf[ot, :, :], in_=self.wocast[:]), ch="woS")
        self.tok_wo_cache = prev_st
        for Q in range(4):
            wq_tiles = self.wq if Q % 2 == 0 else self.wqB
            start = 256 * Q
            src_tiles = [self.lnF[k][:, start:start + 384] for k in range(NKD)]
            tc_a, tc_b = start // TCW, (start + 383) // TCW
            src_toks = [[g[("lnF3", k, tc_a)], g[("lnF3", k, tc_b)]] for k in range(NKD)]
            wg_guard = g.get(("wqg", Q % 2))
            if Q == 0:
                wg_guard = sel_done[-1]
            elif Q == 1:
                wg_guard = self.tok_wo_cache
            qkv_toksw = qkv_project(src_tiles, src_toks, 384, wq_tiles, wtg_guard=wg_guard)
            for wloc in range(2):
                wg = 2 * Q + wloc
                off = 128 * wloc
                qw = [wq_tiles[0][k][:, off:off + 256] for k in range(NKD)]
                kw_ = [wq_tiles[1][k][:, off:off + 256] for k in range(NKD)]
                vw = [wq_tiles[2][k][:, off:off + 256] for k in range(NKD)]

                def win_out(dv, wg=wg):
                    return self.win_g[dv][:, wg * 256:(wg + 1) * 256], selT_tok[3]
                wdone = attend(2, 256, qw, kw_, vw, qkv_toksw, win_out,
                               self.gn[2][:, wg * 256:(wg + 1) * 256], gtok[2])
                g[("wqg", Q % 2)] = wdone[-1]
                for dv in range(NKD):
                    g[("win", dv, wg)] = wdone[dv]

        self.dump_bf(self.win_g[0][:], [g[("win", 0, wg)] for wg in range(NWIN)], 11360, T)
        wo_user = None
        wo_tok = {}

        def wo_issue(idx):
            buf = idx % 2
            p.qs.waits([wo_user, g.get(("wqg", 0)), g.get(("wqg", 1)), self.tok_wo_cache] + bar)
            wo_tok[idx] = [p.qs.emit(lambda e, idx=idx, buf=buf: e.dma_start(
                out=self.wbuf[:, buf * 4096:buf * 4096 + 3 * D], in_=self.d_wo_bf[idx % NKD, :, :]),
                ch="woB0" if buf == 0 else "woB1")]

        self.tok_x3 = [None] * NKD
        self.tok_x3_hbm = [[None] * NTC for _ in range(NKD)]
        comp_all = comp_done[-1]
        sel_all = sel_done[-1]
        win_all = [g[("win", dv, wg)] for dv in range(NKD) for wg in range(NWIN)]
        wo_issue(0)
        wo_issue(1)
        widx = 0
        for tc in range(NTC):
            c0 = tc * TCW
            xbuf = self.x2c if tc % 2 == 0 else self.x2cB
            xch = "x2" if tc % 2 == 0 else "x2b"
            ld_toks = []
            for k in range(NKD):
                xs = self.d_xres_b[k, :, c0:c0 + TCW]
                dst = xbuf[:, k * TCW:(k + 1) * TCW]
                extra0 = [g.get(("rk",)), oh_toks[-1], sel_done[-1], win_all[-1]] + win_all if tc <= 1 else []
                p.qg.waits(bar + [g.get(("x2cc", tc))] + extra0)
                ld_toks.append(p.qg.emit(lambda e, xs=xs, dst=dst: e.dma_start(out=dst, in_=xs), ch=xch))
            ld_all = p.qg.tok(xch)
            z4_toks = []
            for ot in range(NKD):
                buf = widx % 2
                bank = widx % 3
                p.te.waits(wo_tok[widx] + self.bank_wait(bank))
                last = None
                klist = list(range(24)) if tc == 0 else list(range(16, 24))
                for i2, kk in enumerate(klist):
                    if kk < 8:
                        rhs = self.comp_g[:, kk * TCW:(kk + 1) * TCW]
                        p.te.wait(comp_all)
                    elif kk < 16:
                        rhs = self.sel_g[:, (kk - 8) * TCW:(kk - 7) * TCW]
                        p.te.wait(sel_all)
                    else:
                        rhs = self.win_g[kk - 16][:, c0:c0 + TCW]
                        p.te.waits(win_all)
                    st_f = (i2 == 0)
                    sp_f = (i2 == len(klist) - 1)
                    last = p.te.emit(lambda e, kk=kk, bank=bank, buf=buf, rhs=rhs, st_f=st_f, sp_f=sp_f: e.matmul(
                        self.bankap(bank), self.wbuf[:, buf * 4096 + kk * 128:buf * 4096 + (kk + 1) * 128],
                        rhs, start=st_f, stop=sp_f), inc=sp_f)
                wo_user = last
                widx += 1
                if widx + 2 <= NTC * NKD:
                    wo_issue(widx + 1)
                m1 = self.tmp[:, (2 + ot % 2) * TCW:(3 + ot % 2) * TCW]
                p.se.waits([last, self.tok_small, g.get(("m1", ot % 2))] + bar)
                t_m1 = p.se.emit(lambda e, m1=m1, bank=bank, ot=ot: e.activation(
                    m1, self.bankap(bank), AF.Identity, bias=self.sm("b_bo", ot, 1)))
                g[("bank", bank)] = t_m1
                z4d = self.z4[:, ot * TCW:(ot + 1) * TCW]
                lnsl = self.lnF[ot][:, c0:c0 + TCW]
                p.ve.waits([t_m1, g.get(("z4r", ot))] + bar)
                t_z4 = p.ve.emit(lambda e, z4d=z4d, m1=m1, lnsl=lnsl: e.tensor_tensor(
                    z4d, m1, lnsl, op=ALU.add))
                g[("m1", ot % 2)] = t_z4
                z4_toks.append((z4d, t_z4))
            tA = tB = None
            for e_i, (z4d, t_z4) in enumerate(z4_toks):
                zb = self.stg[:, e_i * TCW:(e_i + 1) * TCW]
                zq = self.stgC[:, (e_i % 2) * TCW:(e_i % 2 + 1) * TCW]
                p.qg.waits([t_z4, g.get(("zbslot", e_i))], )
                t_zb = p.qg.emit(lambda e, zb=zb, z4d=z4d: e.tensor_copy(zb, z4d), ch="gpc")
                p.qg.waits([g.get(("stgCsl", e_i % 2))])
                t_sq = p.qg.emit(lambda e, zq=zq, z4d=z4d: e.tensor_tensor(zq, z4d, z4d, op=ALU.mult), ch="gpc")
                p.te.waits([t_zb, self.tok_ones])
                if e_i == 0:
                    p.te.waits(self.rdbk.get(3, []) + self.rdbk.get(4, []) + [g.get(("bank", 3)), g.get(("bank", 4))])
                tA = p.te.emit(lambda e, zb=zb, e_i=e_i: e.matmul(
                    self.bankap(3), self.ones_bf[:], zb, start=(e_i == 0), stop=(e_i == NKD - 1)))
                p.te.wait(t_sq)
                tB = p.te.emit(lambda e, zq=zq, e_i=e_i: e.matmul(
                    self.bankap(4), self.ones_bf[:], zq, start=(e_i == 0), stop=(e_i == NKD - 1)))
                g[("zbslot", e_i)] = tA
                g[("stgCsl", e_i % 2)] = tB
            self.stg_guard = tB
            self.stgB_guard = tB
            tok_mu4, tok_r4 = self.ln_finalize(tA, tB, LN4_EPS, 3, 4)
            o_ap = self.tmp[:, 6 * TCW:7 * TCW]
            for ot in range(NKD):
                z4d = self.z4[:, ot * TCW:(ot + 1) * TCW]
                x_sl = xbuf[:, ot * TCW:(ot + 1) * TCW]
                p.ve.waits([tok_mu4, tok_r4, ld_all])
                tz1 = p.ve.emit(lambda e, z4d=z4d, o_ap=o_ap: e.tensor_tensor(
                    o_ap, z4d, self.musig[:, 0:TCW], op=ALU.subtract))
                p.ve.wait(tz1)
                tz2 = p.ve.emit(lambda e, o_ap=o_ap: e.tensor_tensor(
                    o_ap, o_ap, self.musig[:, TCW:2 * TCW], op=ALU.mult))
                p.ve.wait(tz2)
                tx = p.ve.emit(lambda e, x_sl=x_sl, o_ap=o_ap: e.tensor_tensor(
                    x_sl, x_sl, o_ap, op=ALU.add))
                g[("z4r", ot)] = tx
                self.tok_x3[ot] = tx
                if is_final:
                    xr = self.d_out[ot * 128:(ot + 1) * 128, c0:c0 + TCW]
                    chn = "out"
                else:
                    xr = self.d_xres_a[ot, :, c0:c0 + TCW]
                    chn = "x"
                p.qs.wait(tx)
                self.tok_x3_hbm[ot][tc] = p.qs.emit(lambda e, xr=xr, x_sl=x_sl: e.dma_start(out=xr, in_=x_sl), ch=chn)
            g[("sq",)] = tx
            self.tmp6_guard = tx
            if ln_hook is not None:
                xt = [xbuf[:, k * TCW:(k + 1) * TCW] for k in range(NKD)]
                xk = [self.tok_x3[k] for k in range(NKD)]
                ln_hook(tc, xt, xk, (5, 6))
            g[("x2cc", tc + 2)] = self.tok_x3_hbm[NKD - 1][tc]
        return self.tok_x3


    def finish_hbm_out(self, hbm_toks):
        p = self.p
        for k in range(NKD):
            p.qs.waits(hbm_toks[k])
        for key, v in list(p.qs.n.items()) + list(p.qg.n.items()):
            if key.endswith(".out") or key.endswith(".dbg"):
                p.qs.ops.append(lambda e, sems, key=key, v=v: e.wait_ge(sems[key], v))

    def finish_output(self, src_tiles, src_toks):
        p = self.p
        for k in range(NKD):
            p.qs.wait(src_toks[k])
            p.qs.emit(lambda e, k=k: e.dma_start(
                out=self.d_out[k * 128:(k + 1) * 128, :], in_=src_tiles[k][:]), ch="out")
        for key, v in list(p.qs.n.items()) + list(p.qg.n.items()):
            if key.endswith(".out") or key.endswith(".dbg"):
                p.qs.ops.append(lambda e, sems, key=key, v=v: e.wait_ge(sems[key], v))

    def run_block(self):
        nc = self.nc
        p = self.p
        from contextlib import ExitStack
        chans = set()
        for s in p.streams.values():
            chans.update(s.n.keys())
            chans.add(s._ch(None))
        with ExitStack() as es:
            sems = {nm: es.enter_context(nc.semaphore(nm.replace(".", "_"))) for nm in sorted(chans)}
            block = es.enter_context(nc.Block())

            def runner(stream):
                def f(e):
                    for op in stream.ops:
                        op(e, sems)
                return f
            block.tensor(runner(p.te))
            block.vector(runner(p.ve))
            block.scalar(runner(p.se))
            block.gpsimd(runner(p.qg))
            block.sync(runner(p.qs))


def build():
    nc = bass.Bass("TRN2", target_bir_lowering=False)
    pr = Prog(nc)
    pr.setup()
    hook = pr.emit_lnF_chunk if PHASES >= 2 else None
    pr.attn_a(ln_hook=hook)
    cur_toks = [pr.tok_x1[k][NTC - 1] for k in range(NKD)]
    if PHASES >= 2:
        is_final = (PHASES == 2)
        cur_toks, hbm_toks = pr.ffn(cur_toks, pr.d_xres_a, pr.d_xres_b, is_final=is_final,
                                    ln_pre=(hook is not None))
        if is_final:
            pr.finish_hbm_out(hbm_toks)
            pr.run_block()
            return nc
    if PHASES >= 3:
        x2_tok = pr.p.qs.tok("out")
        hook2 = pr.emit_lnF_chunk if PHASES >= 4 else None
        cur_toks = pr.attn_b(x2_tok, is_final=(PHASES == 3), ln_hook=hook2)
        if PHASES == 3:
            pr.finish_hbm_out(pr.tok_x3_hbm)
            pr.run_block()
            return nc
    if PHASES >= 4:
        cur_toks, hbm_toks = pr.ffn(None, pr.d_xres_a, pr.d_xres_b, is_final=True, x_from_hbm=True,
                                    ln_pre=True)
        pr.finish_hbm_out(hbm_toks)
    pr.run_block()
    return nc


_NC_CACHE = None

def kernel(**inputs):
    global _NC_CACHE
    P = host_prep(inputs)
    x_full = np.asarray(inputs["L0_attn_a_x"], dtype=np.float32)
    if _NC_CACHE is None:
        _NC_CACHE = build()
    nc = _NC_CACHE
    in_maps = []
    for b in range(8):
        m = {"x_in": np.ascontiguousarray(x_full[b].T), "small": P["small"]}
        for nm in ["a_Wi", "a_Wf", "a_Wo", "a_Ws", "f_w1", "f_w2", "f_h1", "f_h2", "f_w3",
                   "b_Wq", "b_Wk", "b_Wv", "b_Wo", "b_Wc"]:
            m[nm] = P[nm]
        in_maps.append(m)
    res = run_bass_kernel_spmd(nc, in_maps, core_ids=list(range(8)))
    return np.stack([res.results[b]["x_out"].T for b in range(8)])
